# version 12
# baseline (speedup 1.0000x reference)
import contextlib
import numpy as np
import ml_dtypes
import concourse.bass as bass
import concourse.mybir as mybir
from concourse.bass_utils import run_bass_kernel_spmd
from concourse.aot_env import allow_ici_all_to_all

F32 = mybir.dt.float32
BF16 = mybir.dt.bfloat16
AF = mybir.ActivationFunctionType
ALU = mybir.AluOpType

NCORES = 8
B = 2
S = 8192
D = 1024
NPT = D // 128
SC = S // NCORES
T = B * SC
FF = 4096
EPS_LN = 1e-5
EPS_L1 = 1e-6
C01 = float(np.sqrt(0.5))


class Sem:
    def __init__(self, h):
        self.h = h
        self.n = 0


class Prog:
    ENGINES = ("tensor", "vector", "scalar", "gpsimd", "sync")

    def __init__(self, nc, stack):
        self.nc = nc
        self.stack = stack
        self.q = {e: [] for e in self.ENGINES}
        self.pending = {e: [] for e in self.ENGINES}

    def sem(self, name):
        return Sem(self.stack.enter_context(self.nc.semaphore(name)))

    def gate(self, eng, wait):
        for w in wait:
            self.pending[eng].append(w if not isinstance(w, Sem) else (w, w.n))

    def gate_all(self, wait, engines=None):
        for e in (engines or self.ENGINES):
            self.gate(e, wait)

    def wait_only(self, eng, wait):
        waits = [(s.h, v) for s, v in ((w, w.n) if isinstance(w, Sem) else w for w in wait) if v > 0]
        self.q[eng].append(lambda e, waits=tuple(waits): [e.wait_ge(h, v) for h, v in waits])

    def emit(self, eng, fn, wait=(), inc=None, inck=1):
        waits = []
        wait = tuple(self.pending[eng]) + tuple(wait)
        self.pending[eng] = []
        for w in wait:
            if isinstance(w, Sem):
                if w.n > 0:
                    waits.append((w.h, w.n))
            else:
                s, v = w
                if v > 0:
                    waits.append((s.h, v))
        if inc is not None:
            inc.n += inck

        def run(e, waits=tuple(waits), fn=fn, inc=inc, inck=inck):
            for h, v in waits:
                e.wait_ge(h, v)
            r = fn(e)
            if inc is not None:
                r.then_inc(inc.h, inck)

        self.q[eng].append(run)

    def t(self, fn, wait=(), inc=None):
        self.emit("tensor", fn, wait, inc)

    def v(self, fn, wait=(), inc=None):
        self.emit("vector", fn, wait, inc)

    def s(self, fn, wait=(), inc=None):
        self.emit("scalar", fn, wait, inc)

    def g(self, fn, wait=(), inc=None, inck=16):
        self.emit("gpsimd", fn, wait, inc, inck)

    def y(self, fn, wait=(), inc=None, inck=16):
        self.emit("sync", fn, wait, inc, inck)

    def run_all(self):
        block = self.stack.enter_context(self.nc.Block())
        nc = self.nc

        def runner(queue):
            def go(e):
                with nc.allow_low_precision(reason="bf16 matmuls/elementwise blessed by problem tolerance"):
                    for f in queue:
                        f(e)
            return go

        block.tensor(runner(self.q["tensor"]))
        block.vector(runner(self.q["vector"]))
        block.scalar(runner(self.q["scalar"]))
        block.gpsimd(runner(self.q["gpsimd"]))
        block.sync(runner(self.q["sync"]))


def _rope_perm():
    perm = np.zeros(D, dtype=np.int64)
    for p in range(8):
        h0, h1 = 2 * p, 2 * p + 1
        base = p * 128
        perm[base + 0:base + 32] = h0 * 64 + np.arange(0, 64, 2)
        perm[base + 32:base + 64] = h0 * 64 + np.arange(1, 64, 2)
        perm[base + 64:base + 96] = h1 * 64 + np.arange(0, 64, 2)
        perm[base + 96:base + 128] = h1 * 64 + np.arange(1, 64, 2)
    return perm


def _host_prep(inputs):
    h = {}
    x = np.asarray(inputs["L0_attn_x"])
    perm = _rope_perm()

    qkv_w = np.asarray(inputs["L0_attn_qkv_w"])
    qkv_b = np.asarray(inputs["L0_attn_qkv_b"])
    qkv_perm = np.concatenate([perm, D + perm, 2 * D + np.arange(D)])
    h["qkv_wT"] = np.ascontiguousarray(qkv_w[qkv_perm].T)
    h["qkv_b"] = np.ascontiguousarray(qkv_b[qkv_perm].reshape(24, 128).T)
    h["out_wT"] = np.ascontiguousarray(np.asarray(inputs["L0_attn_out_w"]).T)
    h["out_b"] = np.ascontiguousarray(np.asarray(inputs["L0_attn_out_b"]).reshape(8, 128).T)

    h["w1"] = np.ascontiguousarray(np.asarray(inputs["L0_ffn_w1"]))
    h["b1"] = np.ascontiguousarray(np.asarray(inputs["L0_ffn_b1"]).reshape(32, 128).T)
    h["w2"] = np.ascontiguousarray(np.asarray(inputs["L0_ffn_w2"]))
    h["b2"] = np.ascontiguousarray(np.asarray(inputs["L0_ffn_b2"]).reshape(8, 128).T)

    for nm in ("Wq", "Wk", "Wv", "Wo"):
        h[nm + "T"] = np.ascontiguousarray(np.asarray(inputs["L1_attn_" + nm]).T)
    for nm in ("W_r", "W_k", "W_v"):
        h[nm + "T"] = np.ascontiguousarray(np.asarray(inputs["L1_ffn_" + nm]).T)
    h["mu_r"] = np.ascontiguousarray(np.asarray(inputs["L1_ffn_mu_r"]).reshape(8, 128).T)
    h["mu_k"] = np.ascontiguousarray(np.asarray(inputs["L1_ffn_mu_k"]).reshape(8, 128).T)

    h["ones"] = np.ones((128, 128), np.float32)
    hsel = np.zeros((128, 8 * 16), np.float32)
    hbc = np.zeros((16, 8 * 128), np.float32)
    for pt in range(8):
        for r in range(128):
            head = 2 * pt + (1 if r >= 64 else 0)
            hsel[r, pt * 16 + head] = 1.0
            hbc[head, pt * 128 + r] = 1.0
    h["hsel"] = hsel
    h["hbc"] = hbc

    h["qkv_wt4"] = np.ascontiguousarray(
        h["qkv_wT"][:, 0:2048].reshape(8, 128, 16, 128).transpose(2, 0, 1, 3))
    h["wvT"] = np.ascontiguousarray(h["qkv_wT"][:, 2048:3072])
    h["qk_b"] = np.ascontiguousarray(h["qkv_b"][:, 0:16])
    h["bvt"] = []
    vb = qkv_b[2 * D:3 * D]
    for c in range(NCORES):
        h["bvt"].append(np.ascontiguousarray(vb[c * 128:(c + 1) * 128].reshape(128, 1)))

    h["xT"] = []
    h["CCS"] = []
    h["SSG"] = []
    inv = (10000.0 ** (-(np.arange(0, 64, 2, dtype=np.float64)) / 64.0))
    for c in range(NCORES):
        xs = x[:, c * SC:(c + 1) * SC, :].reshape(T, D)
        h["xT"].append(np.ascontiguousarray(xs.T))
        pos = np.arange(c * SC, (c + 1) * SC, dtype=np.float64)
        th = np.outer(inv, pos)
        cos = np.cos(th).astype(np.float32)
        sin = np.sin(th).astype(np.float32)
        cosw = np.tile(cos, (1, 2))
        sinw = np.tile(sin, (1, 2))
        h["CCS"].append(np.ascontiguousarray(np.vstack([cosw] * 4)))
        h["SSG"].append(np.ascontiguousarray(np.vstack([-sinw, sinw, -sinw, sinw])))
    return h


class Ctx:

    def __init__(self, P, nt):
        self.nt = nt
        nc, ec = P.nc, P.stack.enter_context
        self.ones_bf = ec(nc.sbuf_tensor("ones_bf", [128, 128], BF16))
        self.ones_f = ec(nc.sbuf_tensor("ones_f", [128, 128], F32))
        self.eps_t = ec(nc.sbuf_tensor("eps_t", [128, 1], F32))
        self.invD = ec(nc.sbuf_tensor("invD", [1, 128], F32))
        self.row = ec(nc.sbuf_tensor("lnrow", [1, 2 * nt], F32))
        self.abc = ec(nc.sbuf_tensor("lnabc", [128, nt], F32))
        self.bbc = ec(nc.sbuf_tensor("lnbbc", [128, nt], F32))
        self.tmp = [ec(nc.sbuf_tensor(f"lntmp{i}", [128, nt], F32)) for i in range(2)]
        self.tmpg = ec(nc.sbuf_tensor("lntmpg", [128, nt], F32))
        self.sq = [ec(nc.sbuf_tensor(f"lnsq{i}", [128, nt], BF16)) for i in range(2)]
        self.hs = [ec(nc.sbuf_tensor(f"hs{pt}", [128, T], BF16)) for pt in range(NPT)]
        self.psBig = ec(nc.psum_tensor("psBig", [128, 2048], F32))
        self.ps = [ec(nc.psum_tensor(f"ps{i}", [128, 512], F32)) for i in range(4)]
        self.sT = P.sem("sT")
        self.sV = P.sem("sV")
        self.sS = P.sem("sS")
        self.sG = P.sem("sG")
        self.dG = P.sem("dG")
        self.dY = P.sem("dY")
        self.sC = P.sem("sC")

    def reg(self, i):
        if i < 4:
            return self.psBig[:, i * 512:(i + 1) * 512]
        return self.ps[i - 4][:, :]

    def load_const(self, P, ones_dram, eps_val=EPS_LN):
        P.y(lambda e: e.dma_start(out=self.ones_f[:, :], in_=ones_dram[:, :]), inc=self.dY)
        P.g(lambda e: e.dma_start(out=self.ones_bf[:, :], in_=ones_dram[:, :]), inc=self.dG)
        P.v(lambda e: e.memset(self.eps_t[:, :], eps_val), inc=self.sV)
        P.v(lambda e: e.memset(self.invD[:, :], 1.0 / D), inc=self.sV)


def emit_ln(P, c, x_aps, nt, out_aps=None, resid_aps=None, xb_stage=None):
    src_bf16 = resid_aps is not None
    v0 = c.sV.n
    if src_bf16:
        xb = x_aps
        xb_ready = [() for _ in range(NPT)]
    else:
        xb = xb_stage if xb_stage is not None else out_aps
        xb_ready = []
        for pt in range(NPT):
            P.s(lambda e, pt=pt: e.copy(xb[pt], x_aps[pt]),
                wait=((c.sS, c.sS.n), (c.sV, v0)), inc=c.sS)
            xb_ready.append(((c.sS, c.sS.n),))
    assert nt <= 512
    P.gate("scalar", ((c.sT, c.sT.n),))
    sq_free_at = [0, 0]
    for pt in range(NPT):
        par = pt % 2
        P.s(lambda e, pt=pt, par=par: e.activation(out=c.sq[par][:, 0:nt], in_=xb[pt], func=AF.Square),
            wait=xb_ready[pt] + ((c.sT, sq_free_at[par]),), inc=c.sS)
        sq_done = c.sS.n
        st, sp = (pt == 0), (pt == NPT - 1)
        P.t(lambda e, pt=pt, st=st, sp=sp: (
            e.ldweights(c.ones_bf[:, 0:1], tile_position=(0, 0)),
            e.matmul(c.ps[2][0:1, 0:nt], c.ones_bf[:, 0:1], xb[pt],
                     start=st, stop=sp, tile_position=(0, 0), skip_group_check=True))[-1],
            wait=xb_ready[pt] + (((c.sV, v0),) if pt == 0 else ()), inc=c.sT)
        P.t(lambda e, par=par, st=st, sp=sp: (
            e.ldweights(c.ones_bf[:, 0:1], tile_position=(0, 0)),
            e.matmul(c.ps[3][0:1, 0:nt], c.ones_bf[:, 0:1], c.sq[par][:, 0:nt],
                     start=st, stop=sp, tile_position=(0, 0), skip_group_check=True))[-1],
            wait=((c.sS, sq_done),), inc=c.sT)
        sq_free_at[par] = c.sT.n
    stats_done = c.sT.n
    r = c.row
    P.s(lambda e: e.copy(r[0:1, 0:nt], c.ps[2][0:1, 0:nt]), wait=((c.sT, stats_done),), inc=c.sS)
    P.s(lambda e: e.copy(r[0:1, nt:2 * nt], c.ps[3][0:1, 0:nt]), inc=c.sS)
    rows_ev = c.sS.n
    P.t(lambda e: e.matmul(c.ps[2][:, 0:nt], c.invD[0:1, :], r[0:1, 0:nt],
                           start=True, stop=True, skip_group_check=True),
        wait=((c.sS, rows_ev),), inc=c.sT)
    P.t(lambda e: e.matmul(c.ps[3][:, 0:nt], c.invD[0:1, :], r[0:1, nt:2 * nt],
                           start=True, stop=True, skip_group_check=True),
        inc=c.sT)
    bc_done = c.sT.n
    P.v(lambda e: e.tensor_copy(c.abc[:, 0:nt], c.ps[2][:, 0:nt]),
        wait=((c.sT, bc_done), (c.sV, c.sV.n)), inc=c.sV)
    P.v(lambda e: e.tensor_copy(c.bbc[:, 0:nt], c.ps[3][:, 0:nt]), inc=c.sV)
    P.v(lambda e: e.tensor_tensor(c.tmp[0][:, 0:nt], c.abc[:, 0:nt], c.abc[:, 0:nt], ALU.mult),
        wait=((c.sV, c.sV.n),), inc=c.sV)
    P.v(lambda e: e.tensor_tensor(c.bbc[:, 0:nt], c.bbc[:, 0:nt], c.tmp[0][:, 0:nt], ALU.subtract),
        wait=((c.sV, c.sV.n),), inc=c.sV)
    var_done = c.sV.n
    P.s(lambda e: e.activation(out=c.bbc[:, 0:nt], in_=c.bbc[:, 0:nt], func=AF.Ln,
                               bias=c.eps_t[:, :]), wait=((c.sV, var_done),), inc=c.sS)
    P.s(lambda e: e.activation(out=c.bbc[:, 0:nt], in_=c.bbc[:, 0:nt], func=AF.Exp,
                               scale=-0.5), wait=((c.sS, c.sS.n),), inc=c.sS)
    rstd_done = c.sS.n
    P.gate("vector", ((c.sS, rstd_done),))
    for pt in range(NPT):
        par = pt % 2
        if resid_aps is None:
            P.v(lambda e, pt=pt, par=par: e.tensor_tensor(c.tmp[par][:, 0:nt], x_aps[pt], c.abc[:, 0:nt],
                                                          ALU.subtract),
                wait=((c.sV, c.sV.n),), inc=c.sV)
            P.v(lambda e, pt=pt, par=par: e.tensor_tensor(out_aps[pt], c.tmp[par][:, 0:nt], c.bbc[:, 0:nt],
                                                          ALU.mult),
                wait=((c.sV, c.sV.n),), inc=c.sV)
        else:
            P.v(lambda e, pt=pt, par=par: e.tensor_tensor(c.tmp[par][:, 0:nt], x_aps[pt], c.abc[:, 0:nt],
                                                          ALU.subtract),
                wait=((c.sV, c.sV.n),), inc=c.sV)
            P.v(lambda e, pt=pt, par=par: e.tensor_tensor(c.tmp[par][:, 0:nt], c.tmp[par][:, 0:nt],
                                                          c.bbc[:, 0:nt], ALU.mult),
                wait=((c.sV, c.sV.n),), inc=c.sV)
            P.v(lambda e, pt=pt, par=par: e.tensor_tensor(resid_aps[pt], resid_aps[pt], c.tmp[par][:, 0:nt],
                                                          ALU.add),
                wait=((c.sV, c.sV.n),), inc=c.sV)
    return c.sV.n


def blk2d(t, d, off, p, n):
    return t[d:d + 1, off:off + p * n].rearrange("a (p t) -> (a p) t", p=p, t=n)


def barrier(P, c):
    P.gate_all(tuple((s, s.n) for s in (c.sT, c.sV, c.sS, c.dY, c.dG, c.sC)))


def emit_l0attn(P, c, nc, io, debug=False):
    xT = io["xT"]
    qkvw4 = io["qkvw4"]
    wvT = io["wvT"]
    qk_b = io["qk_b"]
    bvt = io["bvt"]
    owT = io["owT"]
    out_b = io["out_b"]
    CCSd = io["CCS"]
    SSGd = io["SSG"]
    xO = io["xb"]
    BLK = 3 * 128 * T
    QOFF, KOFF, VOFF = 0, 128 * T, 2 * 128 * T
    aain = nc.dram_tensor("aain", [NCORES, BLK], BF16).ap()
    aaout = nc.dram_tensor("aaout", [NCORES, BLK], BF16).ap()
    abin = [nc.dram_tensor(f"abin{b}", [NCORES, 128 * SC], BF16).ap() for b in range(B)]
    about = [nc.dram_tensor(f"about{b}", [NCORES, 128 * SC], BF16).ap() for b in range(B)]

    TC = 512
    NQC = S // TC
    NKT = S // 128
    sc0 = c.sC.n

    with contextlib.ExitStack() as stack:
        ec = stack.enter_context
        xc = [ec(nc.sbuf_tensor(f"A_xc{pt}", [128, TC], F32)) for pt in range(NPT)]
        yc = [ec(nc.sbuf_tensor(f"A_yc{pt}", [128, TC], BF16)) for pt in range(NPT)]
        wv_sb = [ec(nc.sbuf_tensor(f"A_wv{kt}", [128, 1024], BF16)) for kt in range(NPT)]
        outb_sb = ec(nc.sbuf_tensor("A_outb_sb", [128, 8], F32))
        hn = c.hs
        Kb = ec(nc.sbuf_tensor("A_Kb", [128, B * S], BF16))
        Vb = ec(nc.sbuf_tensor("A_Vb", [128, B * S], BF16))
        qsb = [ec(nc.sbuf_tensor(f"A_qsb{i}", [128, TC], BF16)) for i in range(2)]
        expb = [ec(nc.sbuf_tensor(f"A_expb{i}", [128, 1024], BF16)) for i in range(2)]
        exps = ec(nc.sbuf_tensor("A_exps", [128, 1024], BF16))
        astg = [ec(nc.sbuf_tensor(f"A_astg{i}", [128, TC], BF16)) for i in range(2)]
        denr = ec(nc.sbuf_tensor("A_denr", [1, 1024], BF16))
        denb_sb = ec(nc.sbuf_tensor("A_denb_sb", [128, TC], BF16))
        mt = [ec(nc.sbuf_tensor(f"A_mt{i}", [128, 1], F32)) for i in range(2)]
        nacc = ec(nc.sbuf_tensor("A_nacc", [1, 8], F32))
        qkbt = ec(nc.sbuf_tensor("A_qkbt", [128, 16], F32))
        bvt_sb = ec(nc.sbuf_tensor("A_bvt_sb", [128, 1], F32))
        attn_scope = contextlib.ExitStack()
        ea = attn_scope.enter_context
        wqb = [ea(nc.sbuf_tensor(f"A_wqb{i}", [128, 1024], BF16)) for i in range(2)]
        stg = [ea(nc.sbuf_tensor(f"A_stg{i}", [128, T], BF16)) for i in range(2)]
        swp = [ea(nc.sbuf_tensor(f"A_swp{i}", [128, T], BF16)) for i in range(2)]
        vstg = [ea(nc.sbuf_tensor(f"A_vstg{i}", [128, 1024], BF16)) for i in range(2)]
        ccs = ea(nc.sbuf_tensor("A_ccs", [128, T], BF16))
        ssg = ea(nc.sbuf_tensor("A_ssg", [128, T], BF16))

        P.g(lambda e: e.dma_start(out=ccs[:, :], in_=CCSd[:, :]), inc=c.dG)
        P.g(lambda e: e.dma_start(out=ssg[:, :], in_=SSGd[:, :]), inc=c.dG)
        for kt in range(NPT):
            P.g(lambda e, kt=kt: e.dma_start(out=wv_sb[kt][:, :], in_=wvT[kt * 128:(kt + 1) * 128, :]), inc=c.dG)
        P.y(lambda e: e.dma_start(out=qkbt[:, :], in_=qk_b[:, :]), inc=c.dY)
        P.y(lambda e: e.dma_start(out=bvt_sb[:, :], in_=bvt[:, :]), inc=c.dY)
        P.y(lambda e: e.dma_start(out=outb_sb[:, :], in_=out_b[:, :]), inc=c.dY)
        consts_g, consts_y = c.dG.n, c.dY.n

        prev_ln = 0
        for tcc in range(T // TC):
            sl = slice(tcc * TC, (tcc + 1) * TC)
            for pt in range(NPT):
                P.y(lambda e, pt=pt, sl=sl: e.dma_start(out=xc[pt][:, :], in_=xT[pt * 128:(pt + 1) * 128, sl]),
                    wait=((c.sV, prev_ln),), inc=c.dY)
            P.gate_all(((c.dY, c.dY.n), (c.dG, consts_g)), engines=("vector", "scalar", "tensor"))
            prev_ln = emit_ln(P, c, [xc[pt][:, :] for pt in range(NPT)], TC,
                              out_aps=[hn[pt][:, sl] for pt in range(NPT)])
        hn_done = prev_ln

        wq_cons = [0, 0]
        stg_free = [0, 0]
        ev_done = {}
        for ot in range(16):
            par = ot % 2
            P.g(lambda e, ot=ot: e.dma_start(
                out=wqb[ot % 2][:, :].rearrange("p (k q) -> p k q", k=8, q=128),
                in_=qkvw4[ot:ot + 1, :, :, :].rearrange("a k p q -> p (a k) q")),
                wait=((c.sT, wq_cons[par]),), inc=c.dG)
            wq_ld = c.dG.n
            for ch in range(4):
                rg = (ot * 4 + ch) % 8
                for kt in range(NPT):
                    P.t(lambda e, ot=ot, ch=ch, kt=kt, rg=rg, par=par: (
                        e.ldweights(wqb[par][:, kt * 128:(kt + 1) * 128]),
                        e.matmul(c.reg(rg), wqb[par][:, kt * 128:(kt + 1) * 128],
                                 hn[kt][:, ch * 512:(ch + 1) * 512],
                                 start=(kt == 0), stop=(kt == NPT - 1)))[-1],
                        wait=(((c.dG, wq_ld), (c.sV, hn_done), (c.sV, ev_done.get(rg, 0)),
                               (c.dY, stg_free[par])) if kt == 0 else ()),
                        inc=c.sT)
                P.v(lambda e, ch=ch, rg=rg, par=par, ot=ot: e.tensor_scalar_add(
                    stg[par][:, ch * 512:(ch + 1) * 512], c.reg(rg), qkbt[:, ot:ot + 1]),
                    wait=((c.sT, c.sT.n),), inc=c.sV)
                ev_done[rg] = c.sV.n
            wq_cons[par] = c.sT.n
            evd = c.sV.n
            P.y(lambda e, par=par: e.dma_start(out=swp[par][0:32, :], in_=stg[par][32:64, :]),
                wait=((c.sV, evd),), inc=c.dY)
            P.y(lambda e, par=par: e.dma_start(out=swp[par][32:64, :], in_=stg[par][0:32, :]),
                inc=c.dY)
            P.y(lambda e, par=par: e.dma_start(out=swp[par][64:96, :], in_=stg[par][96:128, :]),
                inc=c.dY)
            P.y(lambda e, par=par: e.dma_start(out=swp[par][96:128, :], in_=stg[par][64:96, :]),
                inc=c.dY)
            swp_ld = c.dY.n
            P.v(lambda e, par=par: e.tensor_tensor(stg[par][:, :], stg[par][:, :], ccs[:, :], ALU.mult),
                wait=((c.dY, swp_ld),), inc=c.sV)
            P.v(lambda e, par=par: e.tensor_tensor(swp[par][:, :], swp[par][:, :], ssg[:, :], ALU.mult),
                inc=c.sV)
            P.v(lambda e, par=par: e.tensor_tensor(stg[par][:, :], stg[par][:, :], swp[par][:, :], ALU.add),
                wait=((c.sV, c.sV.n),), inc=c.sV)
            rope_done = c.sV.n
            dst, off = (ot, QOFF) if ot < 8 else (ot - 8, KOFF)
            P.y(lambda e, dst=dst, off=off, par=par: e.dma_start(
                out=blk2d(aain, dst, off, 128, T), in_=stg[par][:, :]),
                wait=((c.sV, rope_done),), inc=c.dY)
            stg_free[par] = c.dY.n

        vst_free = [0, 0]
        for tt in range(16):
            par = tt % 2
            for dvc in range(2):
                rg = (tt * 2 + dvc) % 8
                for kt in range(NPT):
                    P.t(lambda e, tt=tt, dvc=dvc, kt=kt, rg=rg: (
                        e.ldweights(hn[kt][:, tt * 128:(tt + 1) * 128]),
                        e.matmul(c.reg(rg), hn[kt][:, tt * 128:(tt + 1) * 128],
                                 wv_sb[kt][:, dvc * 512:(dvc + 1) * 512],
                                 start=(kt == 0), stop=(kt == NPT - 1)))[-1],
                        wait=(((c.sV, ev_done.get(rg, 0)), (c.dY, vst_free[par])) if kt == 0 else ()),
                        inc=c.sT)
                P.v(lambda e, dvc=dvc, rg=rg, par=par: e.tensor_copy(
                    vstg[par][:, dvc * 512:(dvc + 1) * 512], c.reg(rg)),
                    wait=((c.sT, c.sT.n),), inc=c.sV)
                ev_done[rg] = c.sV.n
            vev = c.sV.n
            for d in range(NCORES):
                P.y(lambda e, tt=tt, d=d, par=par: e.dma_start(
                    out=blk2d(aain, d, VOFF + tt * 128 * 128, 128, 128),
                    in_=vstg[par][:, d * 128:(d + 1) * 128]),
                    wait=((c.sV, vev),) if d == 0 else (), inc=c.dY)
            vst_free[par] = c.dY.n
        vproj_T = c.sT.n
        stage_done = c.dY.n

        for kt in range(NPT):
            P.g(lambda e, kt=kt: e.dma_start(out=wv_sb[kt][:, :], in_=owT[kt * 128:(kt + 1) * 128, :]),
                wait=((c.sT, vproj_T),) if kt == 0 else (), inc=c.dG)
        wo_ld = c.dG.n
        attn_scope.close()
        af_a = [ec(nc.sbuf_tensor(f"A_afa{pt}", [128, SC], BF16)) for pt in range(NPT)]
        af_b = [ec(nc.sbuf_tensor(f"A_afb{pt}", [128, SC], BF16)) for pt in range(NPT)]

        def cc1(e):
            with allow_ici_all_to_all():
                return e.collective_compute(
                    "AllToAll", ALU.bypass, replica_groups=[list(range(NCORES))],
                    ins=[aain[:, :]], outs=[aaout[:, :]])
        P.g(cc1, wait=((c.dY, stage_done),), inc=c.sC, inck=1)

        for src in range(NCORES):
            for b in range(B):
                P.y(lambda e, src=src, b=b: e.dma_start(
                    out=Kb[:, b * S + src * SC: b * S + (src + 1) * SC],
                    in_=blk2d(aaout, src, KOFF, 128, T)[:, b * SC:(b + 1) * SC]),
                    wait=((c.sC, sc0 + 1),), inc=c.dY)
                P.y(lambda e, src=src, b=b: e.dma_start(
                    out=Vb[:, (b * S + src * SC):(b * S + (src + 1) * SC)]
                        .rearrange("p (t c) -> p t c", t=SC // 128, c=128),
                    in_=aaout[src:src + 1, VOFF + b * SC * 128: VOFF + (b + 1) * SC * 128]
                        .rearrange("a (t p c) -> p (a t) c", t=SC // 128, p=128, c=128)),
                    wait=((c.sC, sc0 + 1),), inc=c.dY)
        kv_ld = c.dY.n

        P.v(lambda e: e.memset(nacc[:, :], 0.0), inc=c.sV)
        nacc0 = c.sV.n
        for b in range(B):
            for (acol, is_q) in ((1, False), (0, True)):
                for ch in range(16):
                    par = ch % 2
                    if is_q:
                        srcc = (ch * 512) // SC
                        o_in = (ch * 512) % SC
                        P.y(lambda e, b=b, srcc=srcc, o_in=o_in, par=par: e.dma_start(
                            out=qsb[par][:, :],
                            in_=blk2d(aaout, srcc, QOFF, 128, T)[:, b * SC + o_in: b * SC + o_in + 512]),
                            wait=((c.sC, sc0 + 1), (c.sS, c.sS.n)), inc=c.dY)
                        qld = c.dY.n
                        P.s(lambda e, par=par: e.activation(out=c.sq[par][:, :], in_=qsb[par][:, :],
                                                            func=AF.Square),
                            wait=((c.dY, qld), (c.sT, c.sT.n)), inc=c.sS)
                    else:
                        P.s(lambda e, b=b, ch=ch, par=par: e.activation(
                            out=c.sq[par][:, :], in_=Kb[:, b * S + ch * 512: b * S + (ch + 1) * 512],
                            func=AF.Square),
                            wait=((c.dY, kv_ld), (c.sT, c.sT.n)), inc=c.sS)
                    sqd = c.sS.n
                    P.t(lambda e, par=par: (
                        e.ldweights(c.ones_bf[:, 0:1]),
                        e.matmul(c.ps[2][0:1, :], c.ones_bf[:, 0:1],
                                 c.sq[par][:, :], start=True, stop=True,
                                 skip_group_check=True))[-1],
                        wait=((c.sS, sqd), (c.sV, c.sV.n)), inc=c.sT)
                    P.v(lambda e: e.tensor_reduce(
                        nacc[0:1, 6:7], c.ps[2][0:1, :], axis=mybir.AxisListType.X,
                        op=ALU.max), wait=((c.sT, c.sT.n),), inc=c.sV)
                    P.v(lambda e, acol=acol: e.tensor_tensor(
                        nacc[0:1, acol:acol + 1], nacc[0:1, acol:acol + 1], nacc[0:1, 6:7],
                        ALU.max), wait=((c.sV, c.sV.n), (c.sV, nacc0)), inc=c.sV)
            P.v(lambda e: e.tensor_copy(nacc[0:1, 4:5], nacc[0:1, 0:1]),
                wait=((c.sV, c.sV.n),), inc=c.sV)
            P.v(lambda e: e.tensor_copy(nacc[0:1, 5:6], nacc[0:1, 1:2]),
                wait=((c.sV, c.sV.n),), inc=c.sV)
            mx_done = c.sV.n
            P.s(lambda e: e.activation(out=nacc[0:1, 4:5], in_=nacc[0:1, 4:5], func=AF.Ln),
                wait=((c.sV, mx_done),), inc=c.sS)
            P.s(lambda e: e.activation(out=nacc[0:1, 5:6], in_=nacc[0:1, 5:6], func=AF.Ln),
                wait=((c.sS, c.sS.n),), inc=c.sS)
            P.v(lambda e: e.tensor_tensor(nacc[0:1, 4:5], nacc[0:1, 4:5], nacc[0:1, 5:6], ALU.add),
                wait=((c.sS, c.sS.n),), inc=c.sV)
            P.s(lambda e: e.activation(out=nacc[0:1, 4:5], in_=nacc[0:1, 4:5], func=AF.Exp, scale=0.5),
                wait=((c.sV, c.sV.n),), inc=c.sS)
            P.v(lambda e: e.tensor_scalar_mul(nacc[0:1, 4:5], nacc[0:1, 4:5], -0.125),
                wait=((c.sS, c.sS.n),), inc=c.sV)
            mrow = c.sV.n
            P.t(lambda e: e.matmul(c.ps[2][:, 0:1], c.ones_f[0:1, :], nacc[0:1, 4:5],
                                   start=True, stop=True),
                wait=((c.sV, mrow),), inc=c.sT)
            P.v(lambda e, b=b: e.tensor_copy(mt[b][:, :], c.ps[2][:, 0:1]),
                wait=((c.sT, c.sT.n),), inc=c.sV)
            P.v(lambda e: e.memset(nacc[0:1, 0:2], 0.0), wait=((c.sV, c.sV.n),), inc=c.sV)
        mt_done = c.sV.n

        phG_sS = c.sS.n
        abst = {}
        scores_end = {}
        av_evac_at = {}
        den_evac_at = {}
        recip_at = {}
        prev_exp_tail = [0, 0]
        prev_av_tail = [0, 0]
        prev_add_tail = [0, 0]
        prev_denq = 0
        DQ = 8
        pending_tail = [None]
        af_ld_part = [0, 0]

        def make_tail(gi, qpar, gpos, b, denq_last):
            def tail():
                P.v(lambda e: e.tensor_copy(denr[0:1, 0:512], c.ps[2][0:1, :]),
                    wait=((c.sT, denq_last), (c.sV, c.sV.n)), inc=c.sV)
                P.v(lambda e: e.tensor_copy(denr[0:1, 512:1024], c.ps[3][0:1, :]), inc=c.sV)
                recip_at[gi] = c.sV.n
                P.t(lambda e: (
                    e.ldweights(c.ones_bf[0:1, 0:64], tile_position=(0, 64)),
                    e.matmul(c.ps[2][64:128, :], c.ones_bf[0:1, 0:64], denr[0:1, 0:512],
                             start=True, stop=True, tile_position=(0, 64), skip_group_check=True))[-1],
                    wait=((c.sV, recip_at[gi]), (c.sV, den_evac_at.get(gi - 1, 0))), inc=c.sT)
                P.t(lambda e: (
                    e.ldweights(c.ones_bf[0:1, 0:64], tile_position=(0, 64)),
                    e.matmul(c.ps[3][64:128, :], c.ones_bf[0:1, 0:64], denr[0:1, 512:1024],
                             start=True, stop=True, tile_position=(0, 64), skip_group_check=True))[-1],
                    inc=c.sT)
                denb_T = c.sT.n
                P.v(lambda e: e.reciprocal(denb_sb[0:64, :], c.ps[2][64:128, :]),
                    wait=((c.sT, denb_T), (c.sV, c.sV.n)), inc=c.sV)
                P.v(lambda e: e.reciprocal(denb_sb[64:128, :], c.ps[3][64:128, :]), inc=c.sV)
                den_evac_at[gi] = c.sV.n
                P.v(lambda e: e.tensor_tensor(astg[qpar][:, :], c.ps[qpar][:, :],
                                              denb_sb[:, :], ALU.mult),
                    wait=((c.sV, c.sV.n), (c.dY, abst.get(gi - 2, 0))), inc=c.sV)
                av_evac_at[gi] = c.sV.n
                P.v(lambda e: e.tensor_scalar_add(astg[qpar][:, :], astg[qpar][:, :],
                                                  bvt_sb[:, :]),
                    wait=((c.sV, c.sV.n), (c.dY, consts_y)), inc=c.sV)
                a_done = c.sV.n
                dstc = gpos // SC
                o_in = gpos % SC
                P.y(lambda e, dstc=dstc, o_in=o_in, b=b: e.dma_start(
                    out=blk2d(abin[b], dstc, 0, 128, SC)[:, o_in:o_in + 512],
                    in_=astg[qpar][:, :]),
                    wait=((c.sV, a_done),), inc=c.dY)
                abst[gi] = c.dY.n
            return tail

        for b in range(B):
            for qc in range(NQC):
                gi = b * NQC + qc
                qpar = gi % 2
                gpos = qc * TC
                src = gpos // SC
                o_in = gpos % SC
                P.y(lambda e, src=src, o_in=o_in, b=b, qpar=qpar: e.dma_start(
                    out=qsb[qpar][:, :],
                    in_=blk2d(aaout, src, QOFF, 128, T)[:, b * SC + o_in: b * SC + o_in + 512]),
                    wait=((c.sT, scores_end.get(gi - 2, 0)), (c.sS, phG_sS if gi < 2 else 0)),
                    inc=c.dY)
                q_ld = c.dY.n
                exp_at = {-2: prev_exp_tail[0], -1: prev_exp_tail[1]}
                av_at = {-2: prev_av_tail[0], -1: prev_av_tail[1]}
                add_at = {-2: prev_add_tail[0], -1: prev_add_tail[1]}
                scores_at = {}
                exp0_at = {}
                denq_at = {-1: prev_denq}
                for step in range(NKT + 2):
                    if step < NKT:
                        kt = step
                        p = kt % 2
                        kcol = b * S + kt * 128
                        P.t(lambda e, kcol=kcol, p=p, qpar=qpar: (
                            e.ldweights(Kb[0:64, kcol:kcol + 128], tile_position=(0, 0)),
                            e.matmul(c.psBig[:, p * 1024:p * 1024 + 512], Kb[0:64, kcol:kcol + 128],
                                     qsb[qpar][0:64, :], start=True, stop=True, tile_position=(0, 0)))[-1],
                            wait=(((c.dY, q_ld), (c.sV, mt_done)) if kt == 0 else
                                  ((c.sS, exp_at.get(kt - 2, 0)),)),
                            inc=c.sT)
                        P.t(lambda e, kcol=kcol, p=p, qpar=qpar: (
                            e.ldweights(Kb[64:128, kcol:kcol + 128], tile_position=(64, 0)),
                            e.matmul(c.psBig[:, p * 1024 + 512:p * 1024 + 1024], Kb[64:128, kcol:kcol + 128],
                                     qsb[qpar][64:128, :], start=True, stop=True, tile_position=(64, 0),
                                     skip_group_check=True))[-1],
                            inc=c.sT)
                        scores_at[kt] = c.sT.n
                    if step == 2 and pending_tail[0] is not None:
                        pending_tail[0]()
                        pending_tail[0] = None
                        if gi == NQC:
                            ab0_done = c.dY.n

                            def cc2a(e):
                                with allow_ici_all_to_all():
                                    return e.collective_compute(
                                        "AllToAll", ALU.bypass, replica_groups=[list(range(NCORES))],
                                        ins=[abin[0][:, :]], outs=[about[0][:, :]])
                            P.g(cc2a, wait=((c.dY, ab0_done),), inc=c.sC, inck=1)
                    if step >= 2:
                        kt = step - 2
                        p = kt % 2
                        first, last = (kt == 0), (kt == NKT - 1)
                        vcol = (b * 64 + kt) * 128
                        P.t(lambda e, vcol=vcol, p=p, qpar=qpar, first=first, last=last: (
                            e.ldweights(Vb[:, vcol:vcol + 64], tile_position=(0, 0)),
                            e.matmul(c.ps[qpar][0:64, :], Vb[:, vcol:vcol + 64], expb[p][:, 0:512],
                                     start=first, stop=last, tile_position=(0, 0), skip_group_check=True))[-1],
                            wait=(((c.sS, exp0_at[kt]),) + (((c.sV, av_evac_at.get(gi - 2, 0)),)
                                                            if first else ())),
                            inc=c.sT)
                        P.t(lambda e, vcol=vcol, p=p, qpar=qpar, first=first, last=last: (
                            e.ldweights(Vb[:, vcol + 64:vcol + 128], tile_position=(0, 64)),
                            e.matmul(c.ps[qpar][64:128, :], Vb[:, vcol + 64:vcol + 128], expb[p][:, 512:1024],
                                     start=first, stop=last, tile_position=(0, 64), skip_group_check=True))[-1],
                            wait=((c.sS, exp_at[kt]),),
                            inc=c.sT)
                        av_at[kt] = c.sT.n
                        if kt % DQ == 0:
                            P.v(lambda e, p=p: e.tensor_copy(exps[:, :], expb[p][:, :]),
                                wait=((c.sS, exp_at[kt]), (c.sT, denq_at[kt // DQ - 1]),
                                      (c.sV, c.sV.n)), inc=c.sV)
                        else:
                            P.v(lambda e, p=p: e.tensor_tensor(exps[:, :], exps[:, :], expb[p][:, :],
                                                               ALU.add),
                                wait=((c.sS, exp_at[kt]), (c.sV, c.sV.n)), inc=c.sV)
                        add_at[kt] = c.sV.n
                        if kt % DQ == DQ - 1:
                            st, sp = (kt == DQ - 1), (kt == NKT - 1)
                            P.t(lambda e, st=st, sp=sp: (
                                e.ldweights(c.ones_bf[:, 0:1], tile_position=(0, 0)),
                                e.matmul(c.ps[2][0:1, :], c.ones_bf[:, 0:1], exps[:, 0:512],
                                         start=st, stop=sp, tile_position=(0, 0), skip_group_check=True),
                                e.matmul(c.ps[3][0:1, :], c.ones_bf[:, 0:1], exps[:, 512:1024],
                                         start=st, stop=sp, tile_position=(0, 0), skip_group_check=True))[-1],
                                wait=(((c.sV, add_at[kt]),) + (((c.sV, recip_at.get(gi - 1, 0)),)
                                                               if st else ())),
                                inc=c.sT)
                            denq_at[kt // DQ] = c.sT.n
                    if step < NKT:
                        kt = step
                        p = kt % 2
                        P.s(lambda e, p=p, b=b: e.activation(
                            out=expb[p][:, 0:512], in_=c.psBig[:, p * 1024:p * 1024 + 512], func=AF.Exp,
                            scale=0.125, bias=mt[b][:, :]),
                            wait=((c.sT, scores_at[kt]), (c.sT, av_at.get(kt - 2, 0)),
                                  (c.sV, add_at.get(kt - 2, 0))), inc=c.sS)
                        exp0_at[kt] = c.sS.n
                        P.s(lambda e, p=p, b=b: e.activation(
                            out=expb[p][:, 512:1024], in_=c.psBig[:, p * 1024 + 512:(p + 1) * 1024],
                            func=AF.Exp, scale=0.125, bias=mt[b][:, :]), inc=c.sS)
                        exp_at[kt] = c.sS.n
                prev_exp_tail = [exp_at[NKT - 2], exp_at[NKT - 1]]
                prev_av_tail = [av_at[NKT - 2], av_at[NKT - 1]]
                prev_add_tail = [add_at[NKT - 2], add_at[NKT - 1]]
                prev_denq = denq_at[NKT // DQ - 1]
                scores_end[gi] = scores_at[NKT - 1]
                pending_tail[0] = make_tail(gi, qpar, gpos, b, prev_denq)
        pending_tail[0]()
        pending_tail[0] = None
        ab_stage_done = c.dY.n

        def cc2b(e):
            with allow_ici_all_to_all():
                return e.collective_compute(
                    "AllToAll", ALU.bypass, replica_groups=[list(range(NCORES))],
                    ins=[abin[1][:, :]], outs=[about[1][:, :]])
        P.g(cc2b, wait=((c.dY, ab_stage_done),), inc=c.sC, inck=1)
        for srcc in range(NCORES):
            P.g(lambda e, srcc=srcc: e.dma_start(
                out=af_a[srcc][:, :], in_=blk2d(about[0], srcc, 0, 128, SC)),
                wait=((c.sC, sc0 + 2),) if srcc == 0 else (), inc=c.dG)
        af_ld_part[0] = c.dG.n
        for srcc in range(NCORES):
            P.g(lambda e, srcc=srcc: e.dma_start(
                out=af_b[srcc][:, :], in_=blk2d(about[1], srcc, 0, 128, SC)),
                wait=((c.sC, sc0 + 3),) if srcc == 0 else (), inc=c.dG)
        af_ld_part[1] = c.dG.n

        P.gate("tensor", ((c.sV, c.sV.n), (c.sS, c.sS.n)))
        prev_res = hn_done
        x_st = 0
        for tcc in range(T // TC):
            sl = slice(tcc * TC, (tcc + 1) * TC)
            for pt in range(NPT):
                P.y(lambda e, pt=pt, sl=sl: e.dma_start(out=xc[pt][:, :], in_=xT[pt * 128:(pt + 1) * 128, sl]),
                    wait=((c.sV, prev_res), (c.dY, x_st)), inc=c.dY)
            x_ld = c.dY.n
            afh = af_a if tcc < 2 else af_b
            asl = slice(sl.start - (0 if tcc < 2 else SC), sl.stop - (0 if tcc < 2 else SC))
            af_ok = af_ld_part[0 if tcc < 2 else 1]
            for ot in range(NPT):
                rg = ot % 6
                for kt in range(NPT):
                    P.t(lambda e, ot=ot, kt=kt, rg=rg, afh=afh, asl=asl: (
                        e.ldweights(wv_sb[kt][:, ot * 128:(ot + 1) * 128]),
                        e.matmul(c.reg(rg), wv_sb[kt][:, ot * 128:(ot + 1) * 128], afh[kt][:, asl],
                                 start=(kt == 0), stop=(kt == NPT - 1)))[-1],
                        wait=(((c.dG, wo_ld), (c.dG, af_ok), (c.sV, ev_done.get(rg, 0))) if kt == 0 else ()),
                        inc=c.sT)
                P.v(lambda e, ot=ot, rg=rg: e.tensor_scalar_add(yc[ot][:, :], c.reg(rg), outb_sb[:, ot:ot + 1]),
                    wait=((c.sT, c.sT.n),), inc=c.sV)
                ev_done[rg] = c.sV.n
            P.gate_all(((c.sV, c.sV.n), (c.dY, x_ld)), engines=("vector", "scalar", "tensor"))
            resid_done = emit_ln(P, c, [yc[pt][:, :] for pt in range(NPT)], TC,
                                 resid_aps=[xc[pt][:, :] for pt in range(NPT)])
            for pt in range(NPT):
                P.y(lambda e, pt=pt, sl=sl: e.dma_start(out=xO[pt * 128:(pt + 1) * 128, sl], in_=xc[pt][:, :]),
                    wait=((c.sV, resid_done),), inc=c.dY)
            x_st = c.dY.n
            prev_res = emit_ln(P, c, [xc[pt][:, :] for pt in range(NPT)], TC,
                               out_aps=[hn[pt][:, sl] for pt in range(NPT)])
        P.wait_only("sync", ((c.dY, c.dY.n),))


def emit_l0ffn(P, c, nc, io):
    xT = io["xb"]
    w1 = io["w1"]
    b1 = io["b1"]
    w2 = io["w2"]
    b2 = io["b2"]
    xO = io["xb"]
    TC = 512
    NH = FF // 128

    with contextlib.ExitStack() as stack:
        ec = stack.enter_context
        xc = [ec(nc.sbuf_tensor(f"F_xc{pt}", [128, TC], F32)) for pt in range(NPT)]
        yc = [ec(nc.sbuf_tensor(f"F_yc{pt}", [128, TC], BF16)) for pt in range(NPT)]
        g_sb = [ec(nc.sbuf_tensor(f"F_g{kt}", [128, TC], BF16)) for kt in range(NH)]
        w1b = [ec(nc.sbuf_tensor(f"F_w1b{pt}", [128, FF], BF16)) for pt in range(NPT)]
        w2b = [ec(nc.sbuf_tensor(f"F_w2b{i}", [128, D], BF16)) for i in range(16)]
        b1t = ec(nc.sbuf_tensor("F_b1t", [128, 32], F32))
        b2t = ec(nc.sbuf_tensor("F_b2t", [128, 8], F32))

        for pt in range(NPT):
            P.g(lambda e, pt=pt: e.dma_start(out=w1b[pt][:, :], in_=w1[pt * 128:(pt + 1) * 128, :]), inc=c.dG)
        P.y(lambda e: e.dma_start(out=b1t[:, :], in_=b1[:, :]), inc=c.dY)
        P.y(lambda e: e.dma_start(out=b2t[:, :], in_=b2[:, :]), inc=c.dY)
        w1_loaded = c.dG.n
        const_loaded = c.dY.n
        P.gate("scalar", ((c.dY, const_loaded),))
        P.gate("vector", ((c.dY, const_loaded),))

        x_stored = [0] * (T // TC)
        prev_resid_done = 0
        prev_lnps = 0
        w2_cons = 0
        for tc in range(T // TC):
            sl = slice(tc * TC, (tc + 1) * TC)
            prev_store = x_stored[tc - 1] if tc > 0 else 0
            for pt in range(NPT):
                P.y(lambda e, pt=pt, sl=sl: e.dma_start(out=xc[pt][:, :], in_=xT[pt * 128:(pt + 1) * 128, sl]),
                    wait=((c.sV, prev_resid_done), (c.dY, prev_store)), inc=c.dY)
            x_ld = c.dY.n
            evac_done_at = [0] * 8
            for ot in range(NH):
                rg = ot % 6
                for kt in range(NPT):
                    P.t(lambda e, ot=ot, kt=kt, rg=rg, sl=sl: (
                        e.ldweights(w1b[kt][:, ot * 128:(ot + 1) * 128]),
                        e.matmul(c.reg(rg), w1b[kt][:, ot * 128:(ot + 1) * 128], c.hs[kt][:, sl],
                                 start=(kt == 0), stop=(kt == NPT - 1)))[-1],
                        wait=(((c.dG, w1_loaded), (c.sS, evac_done_at[rg]))
                              if kt == 0 else ()),
                        inc=c.sT)
                mm_done = c.sT.n
                P.s(lambda e, ot=ot, rg=rg: e.activation(out=g_sb[ot][:, :], in_=c.reg(rg), func=AF.Gelu,
                                                         bias=b1t[:, ot:ot + 1]),
                    wait=((c.sT, mm_done),), inc=c.sS)
                evac_done_at[rg] = c.sS.n
            g_done = c.sS.n
            yev = 0
            for half in range(2):
                for i in range(16):
                    P.g(lambda e, half=half, i=i: e.dma_start(
                        out=w2b[i][:, :], in_=w2[(half * 16 + i) * 128:(half * 16 + i + 1) * 128, :]),
                        wait=((c.sT, w2_cons),) if i == 0 else (), inc=c.dG)
                w2_ld = c.dG.n
                for ot in range(NPT):
                    for i in range(16):
                        kt = half * 16 + i
                        P.t(lambda e, ot=ot, i=i, kt=kt: (
                            e.ldweights(w2b[i][:, ot * 128:(ot + 1) * 128]),
                            e.matmul(c.reg(ot), w2b[i][:, ot * 128:(ot + 1) * 128], g_sb[kt][:, :],
                                     start=(kt == 0), stop=(kt == NH - 1)))[-1],
                            wait=(((c.dG, w2_ld), (c.sS, g_done)) +
                                  (((c.sV, prev_lnps),) if (kt == 0 and ot >= 6) else ())
                                  if i == 0 else ()),
                            inc=c.sT)
                w2_cons = c.sT.n
            mm2_done = c.sT.n
            for ot in range(NPT):
                P.v(lambda e, ot=ot: e.tensor_scalar_add(yc[ot][:, :], c.reg(ot), b2t[:, ot:ot + 1]),
                    wait=((c.sT, mm2_done),), inc=c.sV)
            y_done_v = c.sV.n
            P.gate_all(((c.sV, y_done_v), (c.dY, x_ld)), engines=("vector", "tensor", "scalar"))
            resid_done = emit_ln(P, c, [yc[pt][:, :] for pt in range(NPT)], TC,
                                 resid_aps=[xc[pt][:, :] for pt in range(NPT)])
            for pt in range(NPT):
                P.y(lambda e, pt=pt, sl=sl: e.dma_start(out=xO[pt * 128:(pt + 1) * 128, sl], in_=xc[pt][:, :]),
                    wait=((c.sV, resid_done),), inc=c.dY)
            x_stored[tc] = c.dY.n
            prev_resid_done = emit_ln(P, c, [xc[pt][:, :] for pt in range(NPT)], TC,
                                      out_aps=[c.hs[pt][:, sl] for pt in range(NPT)])
            prev_lnps = prev_resid_done
        P.wait_only("sync", ((c.dY, c.dY.n),))


def wslice3(wT, ot):
    return wT[:, ot * 128:(ot + 1) * 128].rearrange("(k p) q -> p k q", k=8, p=128)


def emit_l1attn(P, c, nc, io):
    xT = io["xb"]
    wq = io["wq"]
    wk = io["wk"]
    wv = io["wv"]
    wo = io["wo"]
    hseld = io["hsel"]
    hbcd = io["hbc"]
    xO = io["xb"]
    TC = 512
    HT = T // 2

    with contextlib.ExitStack() as stack:
        ec = stack.enter_context
        xc = [ec(nc.sbuf_tensor(f"C_xc{pt}", [128, TC], F32)) for pt in range(NPT)]
        hn = c.hs
        bcol = ec(nc.sbuf_tensor("C_bcol", [128, 16], F32))
        v_sb = [ec(nc.sbuf_tensor(f"C_v{pt}", [128, T], BF16)) for pt in range(NPT)]
        yc = [ec(nc.sbuf_tensor(f"C_yc{pt}", [128, TC], BF16)) for pt in range(NPT)]
        wb = [ec(nc.sbuf_tensor(f"C_wb{i}", [128, 1024], BF16)) for i in range(2)]
        hsel_sb = ec(nc.sbuf_tensor("C_hsel_sb", [128, 128], BF16))
        hbc_sb = ec(nc.sbuf_tensor("C_hbc_sb", [16, 1024], BF16))
        prod = [ec(nc.sbuf_tensor(f"C_prod{i}", [128, HT], BF16)) for i in range(2)]
        s_sb = [ec(nc.sbuf_tensor(f"C_s{i}", [16, HT], BF16)) for i in range(4)]
        z_sb = [ec(nc.sbuf_tensor(f"C_z{i}", [16, HT], F32)) for i in range(2)]
        w_sb = [ec(nc.sbuf_tensor(f"C_wl{i}", [16, HT], BF16)) for i in range(4)]
        l1t = [ec(nc.sbuf_tensor(f"C_l1t{i}", [128, HT], BF16)) for i in range(2)]
        qk_scope = contextlib.ExitStack()
        eq = qk_scope.enter_context
        q_sb = [eq(nc.sbuf_tensor(f"C_q{pt}", [128, T], BF16)) for pt in range(NPT)]
        k_sb = [eq(nc.sbuf_tensor(f"C_k{pt}", [128, T], BF16)) for pt in range(NPT)]

        P.g(lambda e: e.dma_start(out=hsel_sb[:, :], in_=hseld[:, :]), inc=c.dG)
        P.g(lambda e: e.dma_start(out=hbc_sb[:, :], in_=hbcd[:, :]), inc=c.dG)
        consts_g = c.dG.n
        P.gate("tensor", ((c.dG, consts_g),))

        evS = {}
        evV = {}
        w_cons = [0, 0]
        for (wd, dst, actf) in ((wq, q_sb, AF.Relu), (wk, k_sb, AF.Relu), (wv, v_sb, None)):
            for ot in range(NPT):
                par = ot % 2
                P.g(lambda e, wd=wd, ot=ot, par=par: e.dma_start(
                    out=wb[par][:, :].rearrange("p (k q) -> p k q", k=8, q=128),
                    in_=wslice3(wd, ot)),
                    wait=((c.sT, w_cons[par]),), inc=c.dG)
                w_ld = c.dG.n
                for ch in range(4):
                    rg = (ot * 4 + ch) % 8
                    for kt in range(NPT):
                        P.t(lambda e, ot=ot, ch=ch, kt=kt, rg=rg, par=par: (
                            e.ldweights(wb[par][:, kt * 128:(kt + 1) * 128]),
                            e.matmul(c.reg(rg), wb[par][:, kt * 128:(kt + 1) * 128],
                                     hn[kt][:, ch * 512:(ch + 1) * 512],
                                     start=(kt == 0), stop=(kt == NPT - 1)))[-1],
                            wait=(((c.dG, w_ld),
                                   (c.sS, evS.get(rg, 0)), (c.sV, evV.get(rg, 0))) if kt == 0 else ()),
                            inc=c.sT)
                    if actf is not None:
                        P.s(lambda e, dst=dst, ot=ot, ch=ch, rg=rg, actf=actf: e.activation(
                            out=dst[ot][:, ch * 512:(ch + 1) * 512], in_=c.reg(rg), func=actf),
                            wait=((c.sT, c.sT.n),), inc=c.sS)
                        evS[rg] = c.sS.n
                    else:
                        P.v(lambda e, dst=dst, ot=ot, ch=ch, rg=rg: e.tensor_copy(
                            dst[ot][:, ch * 512:(ch + 1) * 512], c.reg(rg)),
                            wait=((c.sT, c.sT.n),), inc=c.sV)
                        evV[rg] = c.sV.n
                w_cons[par] = c.sT.n
        proj_done_v = c.sV.n
        proj_done_s = c.sS.n

        P.gate_all(((c.sV, proj_done_v), (c.sS, proj_done_s)), engines=("vector", "tensor"))
        combos = [(0, 0), (0, 1), (1, 0), (1, 1)]
        for ci, (l, lp) in enumerate(combos):
            for pt in range(NPT):
                par = pt % 2
                P.v(lambda e, pt=pt, par=par, l=l, lp=lp: e.tensor_tensor(
                    prod[par][:, :], q_sb[pt][:, l * HT:(l + 1) * HT],
                    k_sb[pt][:, lp * HT:(lp + 1) * HT], ALU.mult),
                    wait=((c.sT, evS.get(("p", par), 0)), (c.sV, c.sV.n)), inc=c.sV)
                pd = c.sV.n
                for ch in range(2):
                    P.t(lambda e, pt=pt, par=par, ch=ch: (
                        e.ldweights(hsel_sb[:, pt * 16:pt * 16 + 16]),
                        e.matmul(c.psBig[0:16, ch * 512:(ch + 1) * 512],
                                 hsel_sb[:, pt * 16:pt * 16 + 16],
                                 prod[par][:, ch * 512:(ch + 1) * 512],
                                 start=(pt == 0), stop=(pt == NPT - 1), skip_group_check=True))[-1],
                        wait=((c.sV, pd),), inc=c.sT)
                evS[("p", par)] = c.sT.n
            P.v(lambda e, ci=ci: e.tensor_copy(s_sb[ci][:, :], c.psBig[0:16, 0:HT]),
                wait=((c.sT, c.sT.n),), inc=c.sV)
        s_done = c.sV.n
        qk_scope.close()
        wo_sb = [ec(nc.sbuf_tensor(f"C_wo{kt}", [128, 1024], BF16)) for kt in range(NPT)]
        for kt in range(NPT):
            P.g(lambda e, kt=kt: e.dma_start(out=wo_sb[kt][:, :], in_=wo[kt * 128:(kt + 1) * 128, :]),
                wait=((c.sV, s_done),) if kt == 0 else (), inc=c.dG)
        wo_res = c.dG.n

        P.v(lambda e: e.scalar_tensor_tensor(z_sb[0][:, :], s_sb[1][:, :], C01, s_sb[0][:, :],
                                             ALU.mult, ALU.add),
            wait=((c.sV, s_done),), inc=c.sV)
        P.v(lambda e: e.scalar_tensor_tensor(z_sb[1][:, :], s_sb[2][:, :], C01, s_sb[3][:, :],
                                             ALU.mult, ALU.add),
            wait=((c.sV, c.sV.n),), inc=c.sV)
        P.v(lambda e: e.tensor_scalar_max(z_sb[0][:, :], z_sb[0][:, :], EPS_L1),
            wait=((c.sV, c.sV.n),), inc=c.sV)
        P.v(lambda e: e.tensor_scalar_max(z_sb[1][:, :], z_sb[1][:, :], EPS_L1),
            wait=((c.sV, c.sV.n),), inc=c.sV)
        P.v(lambda e: e.reciprocal(z_sb[0][:, :], z_sb[0][:, :]), wait=((c.sV, c.sV.n),), inc=c.sV)
        P.v(lambda e: e.reciprocal(z_sb[1][:, :], z_sb[1][:, :]), wait=((c.sV, c.sV.n),), inc=c.sV)
        z_done = c.sV.n
        P.v(lambda e: e.tensor_tensor(w_sb[0][:, :], s_sb[0][:, :], z_sb[0][:, :], ALU.mult),
            wait=((c.sV, z_done),), inc=c.sV)
        P.v(lambda e: e.tensor_tensor(w_sb[1][:, :], s_sb[1][:, :], z_sb[0][:, :], ALU.mult),
            inc=c.sV)
        P.v(lambda e: e.tensor_tensor(w_sb[2][:, :], s_sb[2][:, :], z_sb[1][:, :], ALU.mult),
            inc=c.sV)
        P.v(lambda e: e.tensor_tensor(w_sb[3][:, :], s_sb[3][:, :], z_sb[1][:, :], ALU.mult),
            inc=c.sV)
        P.v(lambda e: e.tensor_scalar_mul(w_sb[1][:, :], w_sb[1][:, :], C01),
            wait=((c.sV, c.sV.n),), inc=c.sV)
        P.v(lambda e: e.tensor_scalar_mul(w_sb[2][:, :], w_sb[2][:, :], C01),
            wait=((c.sV, c.sV.n),), inc=c.sV)
        w_done = c.sV.n

        P.gate("tensor", ((c.sV, w_done),))
        wbmm = 0
        for pt in range(NPT):
            for l in range(2):
                for lp in range(2):
                    wi = l * 2 + lp
                    for ch in range(2):
                        P.t(lambda e, pt=pt, lp=lp, wi=wi, ch=ch: (
                            e.ldweights(hbc_sb[:, pt * 128:(pt + 1) * 128]),
                            e.matmul(c.psBig[:, lp * 1024 + ch * 512: lp * 1024 + (ch + 1) * 512],
                                     hbc_sb[:, pt * 128:(pt + 1) * 128],
                                     w_sb[wi][:, ch * 512:(ch + 1) * 512],
                                     start=True, stop=True, skip_group_check=True))[-1],
                            wait=((c.sV, wbmm),), inc=c.sT)
                wb_T = c.sT.n
                P.v(lambda e, pt=pt, l=l: e.tensor_tensor(
                    l1t[0][:, :], c.psBig[:, 0:1024], v_sb[pt][:, 0:HT], ALU.mult),
                    wait=((c.sT, wb_T), (c.sV, c.sV.n)), inc=c.sV)
                P.v(lambda e, pt=pt, l=l: e.tensor_tensor(
                    l1t[1][:, :], c.psBig[:, 1024:2048], v_sb[pt][:, HT:T], ALU.mult),
                    wait=((c.sV, c.sV.n),), inc=c.sV)
                P.v(lambda e, pt=pt, l=l: e.tensor_tensor(
                    hn[pt][:, l * HT:(l + 1) * HT], l1t[0][:, :], l1t[1][:, :], ALU.add),
                    wait=((c.sV, c.sV.n),), inc=c.sV)
                wbmm = c.sV.n
        attn_done = c.sV.n

        aab_in = io["aab_in"]
        aab_out = io["aab_out"]
        P.gate("tensor", ((c.sV, attn_done),))
        prev_res = 0
        x_st = 0
        for ci, tcc in enumerate((1, 3, 0, 2)):
            sl = slice(tcc * TC, (tcc + 1) * TC)
            for pt in range(NPT):
                P.y(lambda e, pt=pt, sl=sl: e.dma_start(out=xc[pt][:, :], in_=xT[pt * 128:(pt + 1) * 128, sl]),
                    wait=((c.sV, prev_res), (c.dY, x_st)), inc=c.dY)
            x_ld = c.dY.n
            for ot in range(NPT):
                rg = ot % 6
                for kt in range(NPT):
                    P.t(lambda e, ot=ot, kt=kt, rg=rg, sl=sl: (
                        e.ldweights(wo_sb[kt][:, ot * 128:(ot + 1) * 128]),
                        e.matmul(c.reg(rg), wo_sb[kt][:, ot * 128:(ot + 1) * 128], hn[kt][:, sl],
                                 start=(kt == 0), stop=(kt == NPT - 1)))[-1],
                        wait=(((c.dG, wo_res), (c.sV, evV.get(("o", rg), 0))) if kt == 0 else ()),
                        inc=c.sT)
                P.v(lambda e, ot=ot, rg=rg: e.tensor_copy(yc[ot][:, :], c.reg(rg)),
                    wait=((c.sT, c.sT.n),), inc=c.sV)
                evV[("o", rg)] = c.sV.n
            P.gate_all(((c.sV, c.sV.n), (c.dY, x_ld)), engines=("vector", "scalar", "tensor"))
            resid_done = emit_ln(P, c, [yc[pt][:, :] for pt in range(NPT)], TC,
                                 resid_aps=[xc[pt][:, :] for pt in range(NPT)])
            for pt in range(NPT):
                P.y(lambda e, pt=pt, sl=sl: e.dma_start(out=xO[pt * 128:(pt + 1) * 128, sl], in_=xc[pt][:, :]),
                    wait=((c.sV, resid_done),), inc=c.dY)
            x_st = c.dY.n
            prev_res = emit_ln(P, c, [xc[pt][:, :] for pt in range(NPT)], TC,
                               out_aps=[hn[pt][:, sl] for pt in range(NPT)])
            if ci == 1:
                for pt in range(NPT):
                    for b in range(B):
                        col = pt * B + b
                        P.v(lambda e, pt=pt, b=b, col=col: e.tensor_copy(
                            bcol[:, col:col + 1], hn[pt][:, b * SC + SC - 1:b * SC + SC]),
                            wait=((c.sV, prev_res),) if col == 0 else (), inc=c.sV)
                bcol_done = c.sV.n
                for d in range(NCORES):
                    P.y(lambda e, d=d: e.dma_start(out=blk2d(aab_in, d, 0, 128, 16), in_=bcol[:, :]),
                        wait=((c.sV, bcol_done),) if d == 0 else (), inc=c.dY)
                ab_staged = c.dY.n

                def cc3(e):
                    with allow_ici_all_to_all():
                        return e.collective_compute(
                            "AllToAll", ALU.bypass, replica_groups=[list(range(NCORES))],
                            ins=[aab_in[:, :]], outs=[aab_out[:, :]])
                P.g(cc3, wait=((c.dY, ab_staged),), inc=c.sC, inck=1)
        P.wait_only("sync", ((c.dY, c.dY.n),))


def emit_l1ffn(P, c, nc, io):
    xT = io["xb"]
    wr = io["wr"]
    wkk = io["wkk"]
    wvv = io["wvv"]
    mur = io["mur"]
    muk = io["muk"]
    bmaskd = io["bmask"]
    xO = io["xO"]
    aab_out = io["aab_out"]
    TC = 512
    cc3_at = c.sC.n

    with contextlib.ExitStack() as stack:
        ec = stack.enter_context
        xc = [ec(nc.sbuf_tensor(f"R_xc{pt}", [128, TC], F32)) for pt in range(NPT)]
        hn = c.hs
        hp = [ec(nc.sbuf_tensor(f"R_hp{pt}", [128, T], BF16)) for pt in range(NPT)]
        xr = [ec(nc.sbuf_tensor(f"R_xr{pt}", [128, T], BF16)) for pt in range(NPT)]
        r_sb = [ec(nc.sbuf_tensor(f"R_r{pt}", [128, T], BF16)) for pt in range(NPT)]
        yc = [ec(nc.sbuf_tensor(f"R_yc{pt}", [128, TC], BF16)) for pt in range(NPT)]
        wb = [ec(nc.sbuf_tensor(f"R_wb{i}", [128, 1024], BF16)) for i in range(2)]
        murt = ec(nc.sbuf_tensor("R_murt", [128, 8], F32))
        mukt = ec(nc.sbuf_tensor("R_mukt", [128, 8], F32))
        dtb = [ec(nc.sbuf_tensor(f"R_dtb{i}", [128, T], BF16)) for i in range(1)]
        wvv_sb = [ec(nc.sbuf_tensor(f"R_wvv{kt}", [128, 1024], BF16)) for kt in range(NPT)]
        bndall = ec(nc.sbuf_tensor("R_bndall", [128, 128], F32))
        hpb = ec(nc.sbuf_tensor("R_hpb", [128, 16], F32))
        bmask_sb = ec(nc.sbuf_tensor("R_bmask_sb", [128, 128], F32))

        P.y(lambda e: e.dma_start(out=murt[:, :], in_=mur[:, :]), inc=c.dY)
        P.y(lambda e: e.dma_start(out=mukt[:, :], in_=muk[:, :]), inc=c.dY)
        P.y(lambda e: e.dma_start(out=bmask_sb[:, :], in_=bmaskd[:, :]), inc=c.dY)
        consts_y = c.dY.n

        for pt in range(NPT):
            for b in range(B):
                P.y(lambda e, pt=pt, b=b: e.dma_start(
                    out=hp[pt][:, b * SC + 1:(b + 1) * SC], in_=hn[pt][:, b * SC:(b + 1) * SC - 1]),
                    inc=c.dY)
        hp_shift = c.dY.n
        for s in range(NCORES):
            P.y(lambda e, s=s: e.dma_start(out=bndall[:, s * 16:(s + 1) * 16],
                                           in_=blk2d(aab_out, s, 0, 128, 16)),
                wait=((c.sC, cc3_at),) if s == 0 else (), inc=c.dY)
        bnd_ld = c.dY.n
        P.v(lambda e: e.tensor_tensor(bndall[:, :], bndall[:, :], bmask_sb[:, :], ALU.mult),
            wait=((c.dY, bnd_ld), (c.dY, consts_y), (c.sV, c.sV.n)), inc=c.sV)
        P.v(lambda e: e.tensor_copy(hpb[:, :], bndall[:, 0:16]), wait=((c.sV, c.sV.n),), inc=c.sV)
        for s in range(1, NCORES):
            P.v(lambda e, s=s: e.tensor_tensor(hpb[:, :], hpb[:, :], bndall[:, s * 16:(s + 1) * 16],
                                               ALU.add),
                wait=((c.sV, c.sV.n),), inc=c.sV)
        hpb_done = c.sV.n
        for pt in range(NPT):
            for b in range(B):
                col = pt * B + b
                P.v(lambda e, pt=pt, b=b, col=col: e.tensor_copy(
                    hp[pt][:, b * SC:b * SC + 1], hpb[:, col:col + 1]),
                    wait=((c.sV, hpb_done), (c.dY, hp_shift)) if col == 0 else (), inc=c.sV)
        hp_done = c.sV.n

        for pt in range(NPT):
            P.v(lambda e, pt=pt: e.tensor_sub(dtb[0][:, :], hn[pt][:, :], hp[pt][:, :]),
                wait=((c.sV, c.sV.n), (c.sV, hp_done), (c.dY, hp_shift)), inc=c.sV)
            P.v(lambda e, pt=pt: e.scalar_tensor_tensor(
                xr[pt][:, :], dtb[0][:, :], murt[:, pt:pt + 1], hp[pt][:, :], ALU.mult, ALU.add),
                wait=((c.sV, c.sV.n), (c.dY, consts_y)), inc=c.sV)
        xr_done = c.sV.n

        ev_done = {}
        w_cons = [0, 0]

        def proj(P, wd, rhs_tiles, rhs_ready, evac, name):
            for ot in range(NPT):
                par = ot % 2
                P.g(lambda e, wd=wd, ot=ot, par=par: e.dma_start(
                    out=wb[par][:, :].rearrange("p (k q) -> p k q", k=8, q=128),
                    in_=wslice3(wd, ot)),
                    wait=((c.sT, w_cons[par]),), inc=c.dG)
                w_ld = c.dG.n
                for ch in range(4):
                    rg = (ot * 4 + ch) % 8
                    for kt in range(NPT):
                        P.t(lambda e, ot=ot, ch=ch, kt=kt, rg=rg, par=par: (
                            e.ldweights(wb[par][:, kt * 128:(kt + 1) * 128]),
                            e.matmul(c.reg(rg), wb[par][:, kt * 128:(kt + 1) * 128],
                                     rhs_tiles[kt][:, ch * 512:(ch + 1) * 512],
                                     start=(kt == 0), stop=(kt == NPT - 1)))[-1],
                            wait=(((c.dG, w_ld), rhs_ready, (ev_done.get(rg, (c.sV, 0)))) if kt == 0 else ()),
                            inc=c.sT)
                    evac(ot, ch, rg)
                w_cons[par] = c.sT.n

        def evac_sig(ot, ch, rg):
            P.s(lambda e, ot=ot, ch=ch, rg=rg: e.activation(
                out=r_sb[ot][:, ch * 512:(ch + 1) * 512], in_=c.reg(rg), func=AF.Sigmoid),
                wait=((c.sT, c.sT.n),), inc=c.sS)
            ev_done[rg] = (c.sS, c.sS.n)
        proj(P, wr, xr, (c.sV, xr_done), evac_sig, "r")
        r_done_s = c.sS.n

        P.gate("vector", ((c.sT, c.sT.n),))
        for pt in range(NPT):
            P.v(lambda e, pt=pt: e.tensor_sub(dtb[0][:, :], hn[pt][:, :], hp[pt][:, :]),
                wait=((c.sV, c.sV.n),), inc=c.sV)
            P.v(lambda e, pt=pt: e.scalar_tensor_tensor(
                xr[pt][:, :], dtb[0][:, :], mukt[:, pt:pt + 1], hp[pt][:, :], ALU.mult, ALU.add),
                wait=((c.sV, c.sV.n),), inc=c.sV)
        xk_done = c.sV.n

        def evac_relu2(ot, ch, rg):
            P.s(lambda e, ot=ot, ch=ch, rg=rg: e.activation(
                out=hp[ot][:, ch * 512:(ch + 1) * 512], in_=c.reg(rg), func=AF.Relu),
                wait=((c.sT, c.sT.n), (c.sV, xk_done)), inc=c.sS)
            P.s(lambda e, ot=ot, ch=ch: e.activation(
                out=hp[ot][:, ch * 512:(ch + 1) * 512], in_=hp[ot][:, ch * 512:(ch + 1) * 512],
                func=AF.Square),
                wait=((c.sS, c.sS.n),), inc=c.sS)
            ev_done[rg] = (c.sS, c.sS.n)
        P.gate("tensor", ((c.sV, xk_done),))
        proj(P, wkk, xr, (c.sV, xk_done), evac_relu2, "ka")

        for kt in range(NPT):
            P.g(lambda e, kt=kt: e.dma_start(out=wvv_sb[kt][:, :], in_=wvv[kt * 128:(kt + 1) * 128, :]),
                inc=c.dG)
        wvv_res = c.dG.n
        P.gate("tensor", ((c.sS, c.sS.n),))
        prev_res = 0
        x_st = 0
        for tcc in range(T // TC):
            sl = slice(tcc * TC, (tcc + 1) * TC)
            for pt in range(NPT):
                P.y(lambda e, pt=pt, sl=sl: e.dma_start(out=xc[pt][:, :], in_=xT[pt * 128:(pt + 1) * 128, sl]),
                    wait=((c.sV, prev_res), (c.dY, x_st)), inc=c.dY)
            x_ld = c.dY.n
            for ot in range(NPT):
                rg = ot % 6
                for kt in range(NPT):
                    P.t(lambda e, ot=ot, kt=kt, rg=rg, sl=sl: (
                        e.ldweights(wvv_sb[kt][:, ot * 128:(ot + 1) * 128]),
                        e.matmul(c.reg(rg), wvv_sb[kt][:, ot * 128:(ot + 1) * 128], hp[kt][:, sl],
                                 start=(kt == 0), stop=(kt == NPT - 1)))[-1],
                        wait=(((c.dG, wvv_res), (ev_done.get(("y", rg), (c.sV, 0)))) if kt == 0 else ()),
                        inc=c.sT)
                P.v(lambda e, ot=ot, rg=rg, sl=sl: e.tensor_tensor(
                    yc[ot][:, :], c.reg(rg), r_sb[ot][:, sl], ALU.mult),
                    wait=((c.sT, c.sT.n), (c.sS, r_done_s)), inc=c.sV)
                ev_done[("y", rg)] = (c.sV, c.sV.n)
            P.gate_all(((c.sV, c.sV.n), (c.dY, x_ld)), engines=("vector", "scalar", "tensor"))
            prev_res = emit_ln(P, c, [yc[pt][:, :] for pt in range(NPT)], TC,
                               resid_aps=[xc[pt][:, :] for pt in range(NPT)])
            for pt in range(NPT):
                P.y(lambda e, pt=pt, sl=sl: e.dma_start(out=xO[pt * 128:(pt + 1) * 128, sl], in_=xc[pt][:, :]),
                    wait=((c.sV, prev_res),), inc=c.dY)
            x_st = c.dY.n
        P.wait_only("sync", ((c.dY, c.dY.n),))


def build_full():
    nc = bass.Bass("TRN2", target_bir_lowering=False)
    io = {}

    def inp(name, shape):
        io[name] = nc.dram_tensor(name, shape, F32, kind="ExternalInput").ap()
    inp("xT", [D, T])
    inp("qkvw4", [16, 8, 128, 128])
    inp("wvT", [D, D])
    inp("qk_b", [128, 16])
    inp("bvt", [128, 1])
    inp("owT", [D, D])
    inp("out_b", [128, 8])
    inp("CCS", [128, T])
    inp("SSG", [128, T])
    inp("w1", [D, FF])
    inp("b1", [128, 32])
    inp("w2", [FF, D])
    inp("b2", [128, 8])
    inp("wq", [D, D])
    inp("wk", [D, D])
    inp("wv", [D, D])
    inp("wo", [D, D])
    inp("hsel", [128, 128])
    inp("hbc", [16, 1024])
    inp("wr", [D, D])
    inp("wkk", [D, D])
    inp("wvv", [D, D])
    inp("mur", [128, 8])
    inp("muk", [128, 8])
    inp("bmask", [128, 128])
    inp("ones", [128, 128])
    io["xb"] = nc.dram_tensor("xb", [D, T], F32).ap()
    io["aab_in"] = nc.dram_tensor("aab_in", [NCORES, 128 * 16], F32).ap()
    io["aab_out"] = nc.dram_tensor("aab_out", [NCORES, 128 * 16], F32).ap()
    io["xO"] = nc.dram_tensor("xO", [D, T], F32, kind="ExternalOutput").ap()

    with contextlib.ExitStack() as stack:
        P = Prog(nc, stack)
        c = Ctx(P, 512)
        c.load_const(P, io["ones"])
        emit_l0attn(P, c, nc, io)
        barrier(P, c)
        emit_l0ffn(P, c, nc, io)
        barrier(P, c)
        emit_l1attn(P, c, nc, io)
        barrier(P, c)
        emit_l1ffn(P, c, nc, io)
        P.run_all()
    return nc


def kernel(**inputs):
    h = _host_prep(inputs)
    nc = build_full()
    in_maps = []
    for cid in range(NCORES):
        bmask = np.zeros((128, 128), np.float32)
        if cid > 0:
            bmask[:, (cid - 1) * 16:cid * 16] = 1.0
        in_maps.append({
            "xT": h["xT"][cid], "qkvw4": h["qkv_wt4"], "wvT": h["wvT"], "qk_b": h["qk_b"],
            "bvt": h["bvt"][cid], "owT": h["out_wT"], "out_b": h["out_b"],
            "CCS": h["CCS"][cid], "SSG": h["SSG"][cid], "ones": h["ones"],
            "w1": h["w1"], "b1": h["b1"], "w2": h["w2"], "b2": h["b2"],
            "wq": h["WqT"], "wk": h["WkT"], "wv": h["WvT"], "wo": h["WoT"],
            "hsel": h["hsel"], "hbc": h["hbc"],
            "wr": h["W_rT"], "wkk": h["W_kT"], "wvv": h["W_vT"],
            "mur": h["mu_r"], "muk": h["mu_k"], "bmask": bmask,
        })
    res = run_bass_kernel_spmd(nc, in_maps, core_ids=list(range(NCORES)))
    out = np.empty((B, S, D), np.float32)
    for cid in range(NCORES):
        xo = np.asarray(res.results[cid]["xO"])
        out[:, cid * SC:(cid + 1) * SC, :] = xo.T.reshape(B, SC, D)
    return out


# version 13
# speedup vs baseline: 1.0061x; 1.0061x over previous
import contextlib
import numpy as np
import ml_dtypes
import concourse.bass as bass
import concourse.mybir as mybir
from concourse.bass_utils import run_bass_kernel_spmd
from concourse.aot_env import allow_ici_all_to_all

F32 = mybir.dt.float32
BF16 = mybir.dt.bfloat16
AF = mybir.ActivationFunctionType
ALU = mybir.AluOpType

NCORES = 8
B = 2
S = 8192
D = 1024
NPT = D // 128
SC = S // NCORES
T = B * SC
FF = 4096
EPS_LN = 1e-5
EPS_L1 = 1e-6
C01 = float(np.sqrt(0.5))


class Sem:
    def __init__(self, h):
        self.h = h
        self.n = 0


class Prog:
    ENGINES = ("tensor", "vector", "scalar", "gpsimd", "sync")

    def __init__(self, nc, stack):
        self.nc = nc
        self.stack = stack
        self.q = {e: [] for e in self.ENGINES}
        self.pending = {e: [] for e in self.ENGINES}

    def sem(self, name):
        return Sem(self.stack.enter_context(self.nc.semaphore(name)))

    def gate(self, eng, wait):
        for w in wait:
            self.pending[eng].append(w if not isinstance(w, Sem) else (w, w.n))

    def gate_all(self, wait, engines=None):
        for e in (engines or self.ENGINES):
            self.gate(e, wait)

    def wait_only(self, eng, wait):
        waits = [(s.h, v) for s, v in ((w, w.n) if isinstance(w, Sem) else w for w in wait) if v > 0]
        self.q[eng].append(lambda e, waits=tuple(waits): [e.wait_ge(h, v) for h, v in waits])

    def emit(self, eng, fn, wait=(), inc=None, inck=1):
        waits = []
        wait = tuple(self.pending[eng]) + tuple(wait)
        self.pending[eng] = []
        for w in wait:
            if isinstance(w, Sem):
                if w.n > 0:
                    waits.append((w.h, w.n))
            else:
                s, v = w
                if v > 0:
                    waits.append((s.h, v))
        if inc is not None:
            inc.n += inck

        def run(e, waits=tuple(waits), fn=fn, inc=inc, inck=inck):
            for h, v in waits:
                e.wait_ge(h, v)
            r = fn(e)
            if inc is not None:
                r.then_inc(inc.h, inck)

        self.q[eng].append(run)

    def t(self, fn, wait=(), inc=None):
        self.emit("tensor", fn, wait, inc)

    def v(self, fn, wait=(), inc=None):
        self.emit("vector", fn, wait, inc)

    def s(self, fn, wait=(), inc=None):
        self.emit("scalar", fn, wait, inc)

    def g(self, fn, wait=(), inc=None, inck=16):
        self.emit("gpsimd", fn, wait, inc, inck)

    def y(self, fn, wait=(), inc=None, inck=16):
        self.emit("sync", fn, wait, inc, inck)

    def run_all(self):
        block = self.stack.enter_context(self.nc.Block())
        nc = self.nc

        def runner(queue):
            def go(e):
                with nc.allow_low_precision(reason="bf16 matmuls/elementwise blessed by problem tolerance"):
                    for f in queue:
                        f(e)
            return go

        block.tensor(runner(self.q["tensor"]))
        block.vector(runner(self.q["vector"]))
        block.scalar(runner(self.q["scalar"]))
        block.gpsimd(runner(self.q["gpsimd"]))
        block.sync(runner(self.q["sync"]))


def _rope_perm():
    perm = np.zeros(D, dtype=np.int64)
    for p in range(8):
        h0, h1 = 2 * p, 2 * p + 1
        base = p * 128
        perm[base + 0:base + 32] = h0 * 64 + np.arange(0, 64, 2)
        perm[base + 32:base + 64] = h0 * 64 + np.arange(1, 64, 2)
        perm[base + 64:base + 96] = h1 * 64 + np.arange(0, 64, 2)
        perm[base + 96:base + 128] = h1 * 64 + np.arange(1, 64, 2)
    return perm


def _host_prep(inputs):
    h = {}
    x = np.asarray(inputs["L0_attn_x"])
    perm = _rope_perm()

    qkv_w = np.asarray(inputs["L0_attn_qkv_w"])
    qkv_b = np.asarray(inputs["L0_attn_qkv_b"])
    qkv_perm = np.concatenate([perm, D + perm, 2 * D + np.arange(D)])
    h["qkv_wT"] = np.ascontiguousarray(qkv_w[qkv_perm].T)
    h["qkv_b"] = np.ascontiguousarray(qkv_b[qkv_perm].reshape(24, 128).T)
    h["out_wT"] = np.ascontiguousarray(np.asarray(inputs["L0_attn_out_w"]).T)
    h["out_b"] = np.ascontiguousarray(np.asarray(inputs["L0_attn_out_b"]).reshape(8, 128).T)

    h["w1"] = np.ascontiguousarray(np.asarray(inputs["L0_ffn_w1"]))
    h["b1"] = np.ascontiguousarray(np.asarray(inputs["L0_ffn_b1"]).reshape(32, 128).T)
    h["w2"] = np.ascontiguousarray(np.asarray(inputs["L0_ffn_w2"]))
    h["b2"] = np.ascontiguousarray(np.asarray(inputs["L0_ffn_b2"]).reshape(8, 128).T)

    for nm in ("Wq", "Wk", "Wv", "Wo"):
        h[nm + "T"] = np.ascontiguousarray(np.asarray(inputs["L1_attn_" + nm]).T)
    for nm in ("W_r", "W_k", "W_v"):
        h[nm + "T"] = np.ascontiguousarray(np.asarray(inputs["L1_ffn_" + nm]).T)
    h["mu_r"] = np.ascontiguousarray(np.asarray(inputs["L1_ffn_mu_r"]).reshape(8, 128).T)
    h["mu_k"] = np.ascontiguousarray(np.asarray(inputs["L1_ffn_mu_k"]).reshape(8, 128).T)

    h["ones"] = np.ones((128, 128), np.float32)
    hsel = np.zeros((128, 8 * 16), np.float32)
    hbc = np.zeros((16, 8 * 128), np.float32)
    for pt in range(8):
        for r in range(128):
            head = 2 * pt + (1 if r >= 64 else 0)
            hsel[r, pt * 16 + head] = 1.0
            hbc[head, pt * 128 + r] = 1.0
    h["hsel"] = hsel
    h["hbc"] = hbc

    h["qkv_wt4"] = np.ascontiguousarray(
        h["qkv_wT"][:, 0:2048].reshape(8, 128, 16, 128).transpose(2, 0, 1, 3))
    h["wvT"] = np.ascontiguousarray(h["qkv_wT"][:, 2048:3072])
    h["qk_b"] = np.ascontiguousarray(h["qkv_b"][:, 0:16])
    h["bvt"] = []
    vb = qkv_b[2 * D:3 * D]
    for c in range(NCORES):
        h["bvt"].append(np.ascontiguousarray(vb[c * 128:(c + 1) * 128].reshape(128, 1)))

    h["xT"] = []
    h["CCS"] = []
    h["SSG"] = []
    inv = (10000.0 ** (-(np.arange(0, 64, 2, dtype=np.float64)) / 64.0))
    for c in range(NCORES):
        xs = x[:, c * SC:(c + 1) * SC, :].reshape(T, D)
        h["xT"].append(np.ascontiguousarray(xs.T))
        pos = np.arange(c * SC, (c + 1) * SC, dtype=np.float64)
        th = np.outer(inv, pos)
        cos = np.cos(th).astype(np.float32)
        sin = np.sin(th).astype(np.float32)
        cosw = np.tile(cos, (1, 2))
        sinw = np.tile(sin, (1, 2))
        h["CCS"].append(np.ascontiguousarray(np.vstack([cosw] * 4)))
        h["SSG"].append(np.ascontiguousarray(np.vstack([-sinw, sinw, -sinw, sinw])))
    return h


class Ctx:

    def __init__(self, P, nt):
        self.nt = nt
        nc, ec = P.nc, P.stack.enter_context
        self.ones_bf = ec(nc.sbuf_tensor("ones_bf", [128, 128], BF16))
        self.ones_f = ec(nc.sbuf_tensor("ones_f", [128, 128], F32))
        self.eps_t = ec(nc.sbuf_tensor("eps_t", [128, 1], F32))
        self.invD = ec(nc.sbuf_tensor("invD", [1, 128], F32))
        self.row = ec(nc.sbuf_tensor("lnrow", [1, 2 * nt], F32))
        self.abc = ec(nc.sbuf_tensor("lnabc", [128, nt], F32))
        self.bbc = ec(nc.sbuf_tensor("lnbbc", [128, nt], F32))
        self.tmp = [ec(nc.sbuf_tensor(f"lntmp{i}", [128, nt], F32)) for i in range(2)]
        self.tmpg = ec(nc.sbuf_tensor("lntmpg", [128, nt], F32))
        self.sq = [ec(nc.sbuf_tensor(f"lnsq{i}", [128, nt], BF16)) for i in range(2)]
        self.hs = [ec(nc.sbuf_tensor(f"hs{pt}", [128, T], BF16)) for pt in range(NPT)]
        self.psBig = ec(nc.psum_tensor("psBig", [128, 2048], F32))
        self.ps = [ec(nc.psum_tensor(f"ps{i}", [128, 512], F32)) for i in range(4)]
        self.sT = P.sem("sT")
        self.sV = P.sem("sV")
        self.sS = P.sem("sS")
        self.sG = P.sem("sG")
        self.dG = P.sem("dG")
        self.dY = P.sem("dY")
        self.sC = P.sem("sC")

    def reg(self, i):
        if i < 4:
            return self.psBig[:, i * 512:(i + 1) * 512]
        return self.ps[i - 4][:, :]

    def load_const(self, P, ones_dram, eps_val=EPS_LN):
        P.y(lambda e: e.dma_start(out=self.ones_f[:, :], in_=ones_dram[:, :]), inc=self.dY)
        P.g(lambda e: e.dma_start(out=self.ones_bf[:, :], in_=ones_dram[:, :]), inc=self.dG)
        P.v(lambda e: e.memset(self.eps_t[:, :], eps_val), inc=self.sV)
        P.v(lambda e: e.memset(self.invD[:, :], 1.0 / D), inc=self.sV)


def emit_ln(P, c, x_aps, nt, out_aps=None, resid_aps=None, xb_stage=None):
    src_bf16 = resid_aps is not None
    v0 = c.sV.n
    if src_bf16:
        xb = x_aps
        xb_ready = [() for _ in range(NPT)]
    else:
        xb = xb_stage if xb_stage is not None else out_aps
        xb_ready = []
        for pt in range(NPT):
            P.s(lambda e, pt=pt: e.copy(xb[pt], x_aps[pt]),
                wait=((c.sS, c.sS.n), (c.sV, v0)), inc=c.sS)
            xb_ready.append(((c.sS, c.sS.n),))
    assert nt <= 512
    P.gate("scalar", ((c.sT, c.sT.n),))
    sq_free_at = [0, 0]
    for pt in range(NPT):
        par = pt % 2
        P.s(lambda e, pt=pt, par=par: e.activation(out=c.sq[par][:, 0:nt], in_=xb[pt], func=AF.Square),
            wait=xb_ready[pt] + ((c.sT, sq_free_at[par]),), inc=c.sS)
        sq_done = c.sS.n
        st, sp = (pt == 0), (pt == NPT - 1)
        P.t(lambda e, pt=pt, st=st, sp=sp: (
            e.ldweights(c.ones_bf[:, 0:1], tile_position=(0, 0)),
            e.matmul(c.ps[2][0:1, 0:nt], c.ones_bf[:, 0:1], xb[pt],
                     start=st, stop=sp, tile_position=(0, 0), skip_group_check=True))[-1],
            wait=xb_ready[pt] + (((c.sV, v0),) if pt == 0 else ()), inc=c.sT)
        P.t(lambda e, par=par, st=st, sp=sp: (
            e.ldweights(c.ones_bf[:, 0:1], tile_position=(0, 0)),
            e.matmul(c.ps[3][0:1, 0:nt], c.ones_bf[:, 0:1], c.sq[par][:, 0:nt],
                     start=st, stop=sp, tile_position=(0, 0), skip_group_check=True))[-1],
            wait=((c.sS, sq_done),), inc=c.sT)
        sq_free_at[par] = c.sT.n
    stats_done = c.sT.n
    r = c.row
    P.s(lambda e: e.copy(r[0:1, 0:nt], c.ps[2][0:1, 0:nt]), wait=((c.sT, stats_done),), inc=c.sS)
    P.s(lambda e: e.copy(r[0:1, nt:2 * nt], c.ps[3][0:1, 0:nt]), inc=c.sS)
    rows_ev = c.sS.n
    P.t(lambda e: e.matmul(c.ps[2][:, 0:nt], c.invD[0:1, :], r[0:1, 0:nt],
                           start=True, stop=True, skip_group_check=True),
        wait=((c.sS, rows_ev),), inc=c.sT)
    P.t(lambda e: e.matmul(c.ps[3][:, 0:nt], c.invD[0:1, :], r[0:1, nt:2 * nt],
                           start=True, stop=True, skip_group_check=True),
        inc=c.sT)
    bc_done = c.sT.n
    P.v(lambda e: e.tensor_copy(c.abc[:, 0:nt], c.ps[2][:, 0:nt]),
        wait=((c.sT, bc_done), (c.sV, c.sV.n)), inc=c.sV)
    P.v(lambda e: e.tensor_copy(c.bbc[:, 0:nt], c.ps[3][:, 0:nt]), inc=c.sV)
    P.v(lambda e: e.tensor_tensor(c.tmp[0][:, 0:nt], c.abc[:, 0:nt], c.abc[:, 0:nt], ALU.mult),
        wait=((c.sV, c.sV.n),), inc=c.sV)
    P.v(lambda e: e.tensor_tensor(c.bbc[:, 0:nt], c.bbc[:, 0:nt], c.tmp[0][:, 0:nt], ALU.subtract),
        wait=((c.sV, c.sV.n),), inc=c.sV)
    var_done = c.sV.n
    P.s(lambda e: e.activation(out=c.bbc[:, 0:nt], in_=c.bbc[:, 0:nt], func=AF.Ln,
                               bias=c.eps_t[:, :]), wait=((c.sV, var_done),), inc=c.sS)
    P.s(lambda e: e.activation(out=c.bbc[:, 0:nt], in_=c.bbc[:, 0:nt], func=AF.Exp,
                               scale=-0.5), wait=((c.sS, c.sS.n),), inc=c.sS)
    rstd_done = c.sS.n
    P.gate("vector", ((c.sS, rstd_done),))
    for pt in range(NPT):
        par = pt % 2
        if resid_aps is None:
            P.v(lambda e, pt=pt, par=par: e.tensor_tensor(c.tmp[par][:, 0:nt], x_aps[pt], c.abc[:, 0:nt],
                                                          ALU.subtract),
                wait=((c.sV, c.sV.n),), inc=c.sV)
            P.v(lambda e, pt=pt, par=par: e.tensor_tensor(out_aps[pt], c.tmp[par][:, 0:nt], c.bbc[:, 0:nt],
                                                          ALU.mult),
                wait=((c.sV, c.sV.n),), inc=c.sV)
        else:
            P.v(lambda e, pt=pt, par=par: e.tensor_tensor(c.tmp[par][:, 0:nt], x_aps[pt], c.abc[:, 0:nt],
                                                          ALU.subtract),
                wait=((c.sV, c.sV.n),), inc=c.sV)
            P.v(lambda e, pt=pt, par=par: e.tensor_tensor(c.tmp[par][:, 0:nt], c.tmp[par][:, 0:nt],
                                                          c.bbc[:, 0:nt], ALU.mult),
                wait=((c.sV, c.sV.n),), inc=c.sV)
            P.v(lambda e, pt=pt, par=par: e.tensor_tensor(resid_aps[pt], resid_aps[pt], c.tmp[par][:, 0:nt],
                                                          ALU.add),
                wait=((c.sV, c.sV.n),), inc=c.sV)
    return c.sV.n


def blk2d(t, d, off, p, n):
    return t[d:d + 1, off:off + p * n].rearrange("a (p t) -> (a p) t", p=p, t=n)


def barrier(P, c):
    P.gate_all(tuple((s, s.n) for s in (c.sT, c.sV, c.sS, c.dY, c.dG, c.sC)))


def emit_l0attn(P, c, nc, io, debug=False):
    xT = io["xT"]
    qkvw4 = io["qkvw4"]
    wvT = io["wvT"]
    qk_b = io["qk_b"]
    bvt = io["bvt"]
    owT = io["owT"]
    out_b = io["out_b"]
    CCSd = io["CCS"]
    SSGd = io["SSG"]
    xO = io["xb"]
    BLK = 3 * 128 * T
    QOFF, KOFF, VOFF = 0, 128 * T, 2 * 128 * T
    aain = nc.dram_tensor("aain", [NCORES, BLK], BF16).ap()
    aaout = nc.dram_tensor("aaout", [NCORES, BLK], BF16).ap()
    abin = [nc.dram_tensor(f"abin{b}", [NCORES, 128 * SC], BF16).ap() for b in range(B)]
    about = [nc.dram_tensor(f"about{b}", [NCORES, 128 * SC], BF16).ap() for b in range(B)]

    TC = 512
    NQC = S // TC
    NKT = S // 128
    sc0 = c.sC.n

    with contextlib.ExitStack() as stack:
        ec = stack.enter_context
        xc = [ec(nc.sbuf_tensor(f"A_xc{pt}", [128, TC], F32)) for pt in range(NPT)]
        yc = [ec(nc.sbuf_tensor(f"A_yc{pt}", [128, TC], BF16)) for pt in range(NPT)]
        wv_sb = [ec(nc.sbuf_tensor(f"A_wv{kt}", [128, 1024], BF16)) for kt in range(NPT)]
        outb_sb = ec(nc.sbuf_tensor("A_outb_sb", [128, 8], F32))
        hn = c.hs
        Kb = ec(nc.sbuf_tensor("A_Kb", [128, B * S], BF16))
        Vb = ec(nc.sbuf_tensor("A_Vb", [128, B * S], BF16))
        qsb = [ec(nc.sbuf_tensor(f"A_qsb{i}", [128, TC], BF16)) for i in range(2)]
        expb = [ec(nc.sbuf_tensor(f"A_expb{i}", [128, 1024], BF16)) for i in range(2)]
        exps = ec(nc.sbuf_tensor("A_exps", [128, 1024], BF16))
        astg = [ec(nc.sbuf_tensor(f"A_astg{i}", [128, TC], BF16)) for i in range(2)]
        denr = ec(nc.sbuf_tensor("A_denr", [1, 1024], BF16))
        denb_sb = ec(nc.sbuf_tensor("A_denb_sb", [128, TC], BF16))
        mt = [ec(nc.sbuf_tensor(f"A_mt{i}", [128, 1], F32)) for i in range(2)]
        nacc = ec(nc.sbuf_tensor("A_nacc", [1, 8], F32))
        qkbt = ec(nc.sbuf_tensor("A_qkbt", [128, 16], F32))
        bvt_sb = ec(nc.sbuf_tensor("A_bvt_sb", [128, 1], F32))
        attn_scope = contextlib.ExitStack()
        ea = attn_scope.enter_context
        wqb = [ea(nc.sbuf_tensor(f"A_wqb{i}", [128, 1024], BF16)) for i in range(2)]
        stg = [ea(nc.sbuf_tensor(f"A_stg{i}", [128, T], BF16)) for i in range(2)]
        swp = [ea(nc.sbuf_tensor(f"A_swp{i}", [128, T], BF16)) for i in range(2)]
        vstg = [ea(nc.sbuf_tensor(f"A_vstg{i}", [128, 1024], BF16)) for i in range(2)]
        ccs = ea(nc.sbuf_tensor("A_ccs", [128, T], BF16))
        ssg = ea(nc.sbuf_tensor("A_ssg", [128, T], BF16))

        P.g(lambda e: e.dma_start(out=ccs[:, :], in_=CCSd[:, :]), inc=c.dG)
        P.g(lambda e: e.dma_start(out=ssg[:, :], in_=SSGd[:, :]), inc=c.dG)
        for kt in range(NPT):
            P.g(lambda e, kt=kt: e.dma_start(out=wv_sb[kt][:, :], in_=wvT[kt * 128:(kt + 1) * 128, :]), inc=c.dG)
        P.y(lambda e: e.dma_start(out=qkbt[:, :], in_=qk_b[:, :]), inc=c.dY)
        P.y(lambda e: e.dma_start(out=bvt_sb[:, :], in_=bvt[:, :]), inc=c.dY)
        P.y(lambda e: e.dma_start(out=outb_sb[:, :], in_=out_b[:, :]), inc=c.dY)
        consts_g, consts_y = c.dG.n, c.dY.n

        prev_ln = 0
        ln_at = []
        for tcc in range(T // TC):
            sl = slice(tcc * TC, (tcc + 1) * TC)
            for pt in range(NPT):
                P.y(lambda e, pt=pt, sl=sl: e.dma_start(out=xc[pt][:, :], in_=xT[pt * 128:(pt + 1) * 128, sl]),
                    wait=((c.sV, prev_ln),), inc=c.dY)
            P.gate_all(((c.dY, c.dY.n), (c.dG, consts_g)), engines=("vector", "scalar", "tensor"))
            prev_ln = emit_ln(P, c, [xc[pt][:, :] for pt in range(NPT)], TC,
                              out_aps=[hn[pt][:, sl] for pt in range(NPT)])
            ln_at.append(prev_ln)
        hn_done = prev_ln

        wq_cons = [0, 0]
        stg_free = [0, 0]
        ev_done = {}
        for ot in range(16):
            par = ot % 2
            P.g(lambda e, ot=ot: e.dma_start(
                out=wqb[ot % 2][:, :].rearrange("p (k q) -> p k q", k=8, q=128),
                in_=qkvw4[ot:ot + 1, :, :, :].rearrange("a k p q -> p (a k) q")),
                wait=((c.sT, wq_cons[par]),), inc=c.dG)
            wq_ld = c.dG.n
            for ch in range(4):
                rg = (ot * 4 + ch) % 8
                for kt in range(NPT):
                    P.t(lambda e, ot=ot, ch=ch, kt=kt, rg=rg, par=par: (
                        e.ldweights(wqb[par][:, kt * 128:(kt + 1) * 128]),
                        e.matmul(c.reg(rg), wqb[par][:, kt * 128:(kt + 1) * 128],
                                 hn[kt][:, ch * 512:(ch + 1) * 512],
                                 start=(kt == 0), stop=(kt == NPT - 1)))[-1],
                        wait=(((c.dG, wq_ld), (c.sV, ln_at[ch]), (c.sV, ev_done.get(rg, 0)),
                               (c.dY, stg_free[par])) if kt == 0 else ()),
                        inc=c.sT)
                P.v(lambda e, ch=ch, rg=rg, par=par, ot=ot: e.tensor_scalar_add(
                    stg[par][:, ch * 512:(ch + 1) * 512], c.reg(rg), qkbt[:, ot:ot + 1]),
                    wait=((c.sT, c.sT.n),), inc=c.sV)
                ev_done[rg] = c.sV.n
            wq_cons[par] = c.sT.n
            evd = c.sV.n
            P.y(lambda e, par=par: e.dma_start(out=swp[par][0:32, :], in_=stg[par][32:64, :]),
                wait=((c.sV, evd),), inc=c.dY)
            P.y(lambda e, par=par: e.dma_start(out=swp[par][32:64, :], in_=stg[par][0:32, :]),
                inc=c.dY)
            P.y(lambda e, par=par: e.dma_start(out=swp[par][64:96, :], in_=stg[par][96:128, :]),
                inc=c.dY)
            P.y(lambda e, par=par: e.dma_start(out=swp[par][96:128, :], in_=stg[par][64:96, :]),
                inc=c.dY)
            swp_ld = c.dY.n
            P.v(lambda e, par=par: e.tensor_tensor(stg[par][:, :], stg[par][:, :], ccs[:, :], ALU.mult),
                wait=((c.dY, swp_ld),), inc=c.sV)
            P.v(lambda e, par=par: e.tensor_tensor(swp[par][:, :], swp[par][:, :], ssg[:, :], ALU.mult),
                inc=c.sV)
            P.v(lambda e, par=par: e.tensor_tensor(stg[par][:, :], stg[par][:, :], swp[par][:, :], ALU.add),
                wait=((c.sV, c.sV.n),), inc=c.sV)
            rope_done = c.sV.n
            dst, off = (ot, QOFF) if ot < 8 else (ot - 8, KOFF)
            P.y(lambda e, dst=dst, off=off, par=par: e.dma_start(
                out=blk2d(aain, dst, off, 128, T), in_=stg[par][:, :]),
                wait=((c.sV, rope_done),), inc=c.dY)
            stg_free[par] = c.dY.n

        vst_free = [0, 0]
        for tt in range(16):
            par = tt % 2
            for dvc in range(2):
                rg = (tt * 2 + dvc) % 8
                for kt in range(NPT):
                    P.t(lambda e, tt=tt, dvc=dvc, kt=kt, rg=rg: (
                        e.ldweights(hn[kt][:, tt * 128:(tt + 1) * 128]),
                        e.matmul(c.reg(rg), hn[kt][:, tt * 128:(tt + 1) * 128],
                                 wv_sb[kt][:, dvc * 512:(dvc + 1) * 512],
                                 start=(kt == 0), stop=(kt == NPT - 1)))[-1],
                        wait=(((c.sV, ev_done.get(rg, 0)), (c.dY, vst_free[par])) if kt == 0 else ()),
                        inc=c.sT)
                P.v(lambda e, dvc=dvc, rg=rg, par=par: e.tensor_copy(
                    vstg[par][:, dvc * 512:(dvc + 1) * 512], c.reg(rg)),
                    wait=((c.sT, c.sT.n),), inc=c.sV)
                ev_done[rg] = c.sV.n
            vev = c.sV.n
            for d in range(NCORES):
                P.y(lambda e, tt=tt, d=d, par=par: e.dma_start(
                    out=blk2d(aain, d, VOFF + tt * 128 * 128, 128, 128),
                    in_=vstg[par][:, d * 128:(d + 1) * 128]),
                    wait=((c.sV, vev),) if d == 0 else (), inc=c.dY)
            vst_free[par] = c.dY.n
        vproj_T = c.sT.n
        stage_done = c.dY.n

        for kt in range(NPT):
            P.g(lambda e, kt=kt: e.dma_start(out=wv_sb[kt][:, :], in_=owT[kt * 128:(kt + 1) * 128, :]),
                wait=((c.sT, vproj_T),) if kt == 0 else (), inc=c.dG)
        wo_ld = c.dG.n
        attn_scope.close()
        af_a = [ec(nc.sbuf_tensor(f"A_afa{pt}", [128, SC], BF16)) for pt in range(NPT)]
        af_b = [ec(nc.sbuf_tensor(f"A_afb{pt}", [128, SC], BF16)) for pt in range(NPT)]

        def cc1(e):
            with allow_ici_all_to_all():
                return e.collective_compute(
                    "AllToAll", ALU.bypass, replica_groups=[list(range(NCORES))],
                    ins=[aain[:, :]], outs=[aaout[:, :]])
        P.g(cc1, wait=((c.dY, stage_done),), inc=c.sC, inck=1)

        for src in range(NCORES):
            for b in range(B):
                P.y(lambda e, src=src, b=b: e.dma_start(
                    out=Kb[:, b * S + src * SC: b * S + (src + 1) * SC],
                    in_=blk2d(aaout, src, KOFF, 128, T)[:, b * SC:(b + 1) * SC]),
                    wait=((c.sC, sc0 + 1),), inc=c.dY)
                P.y(lambda e, src=src, b=b: e.dma_start(
                    out=Vb[:, (b * S + src * SC):(b * S + (src + 1) * SC)]
                        .rearrange("p (t c) -> p t c", t=SC // 128, c=128),
                    in_=aaout[src:src + 1, VOFF + b * SC * 128: VOFF + (b + 1) * SC * 128]
                        .rearrange("a (t p c) -> p (a t) c", t=SC // 128, p=128, c=128)),
                    wait=((c.sC, sc0 + 1),), inc=c.dY)
        kv_ld = c.dY.n

        P.v(lambda e: e.memset(nacc[:, :], 0.0), inc=c.sV)
        nacc0 = c.sV.n
        for b in range(B):
            for (acol, is_q) in ((1, False), (0, True)):
                for ch in range(16):
                    par = ch % 2
                    if is_q:
                        srcc = (ch * 512) // SC
                        o_in = (ch * 512) % SC
                        P.y(lambda e, b=b, srcc=srcc, o_in=o_in, par=par: e.dma_start(
                            out=qsb[par][:, :],
                            in_=blk2d(aaout, srcc, QOFF, 128, T)[:, b * SC + o_in: b * SC + o_in + 512]),
                            wait=((c.sC, sc0 + 1), (c.sS, c.sS.n)), inc=c.dY)
                        qld = c.dY.n
                        P.s(lambda e, par=par: e.activation(out=c.sq[par][:, :], in_=qsb[par][:, :],
                                                            func=AF.Square),
                            wait=((c.dY, qld), (c.sT, c.sT.n)), inc=c.sS)
                    else:
                        P.s(lambda e, b=b, ch=ch, par=par: e.activation(
                            out=c.sq[par][:, :], in_=Kb[:, b * S + ch * 512: b * S + (ch + 1) * 512],
                            func=AF.Square),
                            wait=((c.dY, kv_ld), (c.sT, c.sT.n)), inc=c.sS)
                    sqd = c.sS.n
                    P.t(lambda e, par=par: (
                        e.ldweights(c.ones_bf[:, 0:1]),
                        e.matmul(c.ps[2][0:1, :], c.ones_bf[:, 0:1],
                                 c.sq[par][:, :], start=True, stop=True,
                                 skip_group_check=True))[-1],
                        wait=((c.sS, sqd), (c.sV, c.sV.n)), inc=c.sT)
                    P.v(lambda e: e.tensor_reduce(
                        nacc[0:1, 6:7], c.ps[2][0:1, :], axis=mybir.AxisListType.X,
                        op=ALU.max), wait=((c.sT, c.sT.n),), inc=c.sV)
                    P.v(lambda e, acol=acol: e.tensor_tensor(
                        nacc[0:1, acol:acol + 1], nacc[0:1, acol:acol + 1], nacc[0:1, 6:7],
                        ALU.max), wait=((c.sV, c.sV.n), (c.sV, nacc0)), inc=c.sV)
            P.v(lambda e: e.tensor_copy(nacc[0:1, 4:5], nacc[0:1, 0:1]),
                wait=((c.sV, c.sV.n),), inc=c.sV)
            P.v(lambda e: e.tensor_copy(nacc[0:1, 5:6], nacc[0:1, 1:2]),
                wait=((c.sV, c.sV.n),), inc=c.sV)
            mx_done = c.sV.n
            P.s(lambda e: e.activation(out=nacc[0:1, 4:5], in_=nacc[0:1, 4:5], func=AF.Ln),
                wait=((c.sV, mx_done),), inc=c.sS)
            P.s(lambda e: e.activation(out=nacc[0:1, 5:6], in_=nacc[0:1, 5:6], func=AF.Ln),
                wait=((c.sS, c.sS.n),), inc=c.sS)
            P.v(lambda e: e.tensor_tensor(nacc[0:1, 4:5], nacc[0:1, 4:5], nacc[0:1, 5:6], ALU.add),
                wait=((c.sS, c.sS.n),), inc=c.sV)
            P.s(lambda e: e.activation(out=nacc[0:1, 4:5], in_=nacc[0:1, 4:5], func=AF.Exp, scale=0.5),
                wait=((c.sV, c.sV.n),), inc=c.sS)
            P.v(lambda e: e.tensor_scalar_mul(nacc[0:1, 4:5], nacc[0:1, 4:5], -0.125),
                wait=((c.sS, c.sS.n),), inc=c.sV)
            mrow = c.sV.n
            P.t(lambda e: e.matmul(c.ps[2][:, 0:1], c.ones_f[0:1, :], nacc[0:1, 4:5],
                                   start=True, stop=True),
                wait=((c.sV, mrow),), inc=c.sT)
            P.v(lambda e, b=b: e.tensor_copy(mt[b][:, :], c.ps[2][:, 0:1]),
                wait=((c.sT, c.sT.n),), inc=c.sV)
            P.v(lambda e: e.memset(nacc[0:1, 0:2], 0.0), wait=((c.sV, c.sV.n),), inc=c.sV)
        mt_done = c.sV.n

        phG_sS = c.sS.n
        abst = {}
        scores_end = {}
        av_evac_at = {}
        den_evac_at = {}
        recip_at = {}
        prev_exp_tail = [0, 0]
        prev_av_tail = [0, 0]
        prev_add_tail = [0, 0]
        prev_denq = 0
        DQ = 4
        pending_tail = [None]
        af_ld_part = [0, 0]

        def make_tail(gi, qpar, gpos, b, denq_last):
            def tail():
                P.v(lambda e: e.tensor_copy(denr[0:1, 0:512], c.ps[2][0:1, :]),
                    wait=((c.sT, denq_last), (c.sV, c.sV.n)), inc=c.sV)
                P.v(lambda e: e.tensor_copy(denr[0:1, 512:1024], c.ps[3][0:1, :]), inc=c.sV)
                recip_at[gi] = c.sV.n
                P.t(lambda e: (
                    e.ldweights(c.ones_bf[0:1, 0:64], tile_position=(0, 64)),
                    e.matmul(c.ps[2][64:128, :], c.ones_bf[0:1, 0:64], denr[0:1, 0:512],
                             start=True, stop=True, tile_position=(0, 64), skip_group_check=True))[-1],
                    wait=((c.sV, recip_at[gi]), (c.sV, den_evac_at.get(gi - 1, 0))), inc=c.sT)
                P.t(lambda e: (
                    e.ldweights(c.ones_bf[0:1, 0:64], tile_position=(0, 64)),
                    e.matmul(c.ps[3][64:128, :], c.ones_bf[0:1, 0:64], denr[0:1, 512:1024],
                             start=True, stop=True, tile_position=(0, 64), skip_group_check=True))[-1],
                    inc=c.sT)
                denb_T = c.sT.n
                P.v(lambda e: e.reciprocal(denb_sb[0:64, :], c.ps[2][64:128, :]),
                    wait=((c.sT, denb_T), (c.sV, c.sV.n)), inc=c.sV)
                P.v(lambda e: e.reciprocal(denb_sb[64:128, :], c.ps[3][64:128, :]), inc=c.sV)
                den_evac_at[gi] = c.sV.n
                P.v(lambda e: e.tensor_tensor(astg[qpar][:, :], c.ps[qpar][:, :],
                                              denb_sb[:, :], ALU.mult),
                    wait=((c.sV, c.sV.n), (c.dY, abst.get(gi - 2, 0))), inc=c.sV)
                av_evac_at[gi] = c.sV.n
                P.v(lambda e: e.tensor_scalar_add(astg[qpar][:, :], astg[qpar][:, :],
                                                  bvt_sb[:, :]),
                    wait=((c.sV, c.sV.n), (c.dY, consts_y)), inc=c.sV)
                a_done = c.sV.n
                dstc = gpos // SC
                o_in = gpos % SC
                P.y(lambda e, dstc=dstc, o_in=o_in, b=b: e.dma_start(
                    out=blk2d(abin[b], dstc, 0, 128, SC)[:, o_in:o_in + 512],
                    in_=astg[qpar][:, :]),
                    wait=((c.sV, a_done),), inc=c.dY)
                abst[gi] = c.dY.n
            return tail

        for b in range(B):
            for qc in range(NQC):
                gi = b * NQC + qc
                qpar = gi % 2
                gpos = qc * TC
                src = gpos // SC
                o_in = gpos % SC
                P.y(lambda e, src=src, o_in=o_in, b=b, qpar=qpar: e.dma_start(
                    out=qsb[qpar][:, :],
                    in_=blk2d(aaout, src, QOFF, 128, T)[:, b * SC + o_in: b * SC + o_in + 512]),
                    wait=((c.sT, scores_end.get(gi - 2, 0)), (c.sS, phG_sS if gi < 2 else 0)),
                    inc=c.dY)
                q_ld = c.dY.n
                exp_at = {-2: prev_exp_tail[0], -1: prev_exp_tail[1]}
                av_at = {-2: prev_av_tail[0], -1: prev_av_tail[1]}
                add_at = {-2: prev_add_tail[0], -1: prev_add_tail[1]}
                scores_at = {}
                exp0_at = {}
                denq_at = {-1: prev_denq}
                for step in range(NKT + 2):
                    if step < NKT:
                        kt = step
                        p = kt % 2
                        kcol = b * S + kt * 128
                        P.t(lambda e, kcol=kcol, p=p, qpar=qpar: (
                            e.ldweights(Kb[0:64, kcol:kcol + 128], tile_position=(0, 0)),
                            e.matmul(c.psBig[:, p * 1024:p * 1024 + 512], Kb[0:64, kcol:kcol + 128],
                                     qsb[qpar][0:64, :], start=True, stop=True, tile_position=(0, 0)))[-1],
                            wait=(((c.dY, q_ld), (c.sV, mt_done)) if kt == 0 else
                                  ((c.sS, exp_at.get(kt - 2, 0)),)),
                            inc=c.sT)
                        P.t(lambda e, kcol=kcol, p=p, qpar=qpar: (
                            e.ldweights(Kb[64:128, kcol:kcol + 128], tile_position=(64, 0)),
                            e.matmul(c.psBig[:, p * 1024 + 512:p * 1024 + 1024], Kb[64:128, kcol:kcol + 128],
                                     qsb[qpar][64:128, :], start=True, stop=True, tile_position=(64, 0),
                                     skip_group_check=True))[-1],
                            inc=c.sT)
                        scores_at[kt] = c.sT.n
                    if step == 2 and pending_tail[0] is not None:
                        pending_tail[0]()
                        pending_tail[0] = None
                        if gi == NQC:
                            ab0_done = c.dY.n

                            def cc2a(e):
                                with allow_ici_all_to_all():
                                    return e.collective_compute(
                                        "AllToAll", ALU.bypass, replica_groups=[list(range(NCORES))],
                                        ins=[abin[0][:, :]], outs=[about[0][:, :]])
                            P.g(cc2a, wait=((c.dY, ab0_done),), inc=c.sC, inck=1)
                    if step >= 2:
                        kt = step - 2
                        p = kt % 2
                        first, last = (kt == 0), (kt == NKT - 1)
                        vcol = (b * 64 + kt) * 128
                        P.t(lambda e, vcol=vcol, p=p, qpar=qpar, first=first, last=last: (
                            e.ldweights(Vb[:, vcol:vcol + 64], tile_position=(0, 0)),
                            e.matmul(c.ps[qpar][0:64, :], Vb[:, vcol:vcol + 64], expb[p][:, 0:512],
                                     start=first, stop=last, tile_position=(0, 0), skip_group_check=True))[-1],
                            wait=(((c.sS, exp0_at[kt]),) + (((c.sV, av_evac_at.get(gi - 2, 0)),)
                                                            if first else ())),
                            inc=c.sT)
                        P.t(lambda e, vcol=vcol, p=p, qpar=qpar, first=first, last=last: (
                            e.ldweights(Vb[:, vcol + 64:vcol + 128], tile_position=(0, 64)),
                            e.matmul(c.ps[qpar][64:128, :], Vb[:, vcol + 64:vcol + 128], expb[p][:, 512:1024],
                                     start=first, stop=last, tile_position=(0, 64), skip_group_check=True))[-1],
                            wait=((c.sS, exp_at[kt]),),
                            inc=c.sT)
                        av_at[kt] = c.sT.n
                        if kt % DQ == 0:
                            P.v(lambda e, p=p: e.tensor_copy(exps[:, :], expb[p][:, :]),
                                wait=((c.sS, exp_at[kt]), (c.sT, denq_at[kt // DQ - 1]),
                                      (c.sV, c.sV.n)), inc=c.sV)
                        else:
                            P.v(lambda e, p=p: e.tensor_tensor(exps[:, :], exps[:, :], expb[p][:, :],
                                                               ALU.add),
                                wait=((c.sS, exp_at[kt]), (c.sV, c.sV.n)), inc=c.sV)
                        add_at[kt] = c.sV.n
                        if kt % DQ == DQ - 1:
                            st, sp = (kt == DQ - 1), (kt == NKT - 1)
                            P.t(lambda e, st=st, sp=sp: (
                                e.ldweights(c.ones_bf[:, 0:1], tile_position=(0, 0)),
                                e.matmul(c.ps[2][0:1, :], c.ones_bf[:, 0:1], exps[:, 0:512],
                                         start=st, stop=sp, tile_position=(0, 0), skip_group_check=True),
                                e.matmul(c.ps[3][0:1, :], c.ones_bf[:, 0:1], exps[:, 512:1024],
                                         start=st, stop=sp, tile_position=(0, 0), skip_group_check=True))[-1],
                                wait=(((c.sV, add_at[kt]),) + (((c.sV, recip_at.get(gi - 1, 0)),)
                                                               if st else ())),
                                inc=c.sT)
                            denq_at[kt // DQ] = c.sT.n
                    if step < NKT:
                        kt = step
                        p = kt % 2
                        P.s(lambda e, p=p, b=b: e.activation(
                            out=expb[p][:, 0:512], in_=c.psBig[:, p * 1024:p * 1024 + 512], func=AF.Exp,
                            scale=0.125, bias=mt[b][:, :]),
                            wait=((c.sT, scores_at[kt]), (c.sT, av_at.get(kt - 2, 0)),
                                  (c.sV, add_at.get(kt - 2, 0))), inc=c.sS)
                        exp0_at[kt] = c.sS.n
                        P.s(lambda e, p=p, b=b: e.activation(
                            out=expb[p][:, 512:1024], in_=c.psBig[:, p * 1024 + 512:(p + 1) * 1024],
                            func=AF.Exp, scale=0.125, bias=mt[b][:, :]), inc=c.sS)
                        exp_at[kt] = c.sS.n
                prev_exp_tail = [exp_at[NKT - 2], exp_at[NKT - 1]]
                prev_av_tail = [av_at[NKT - 2], av_at[NKT - 1]]
                prev_add_tail = [add_at[NKT - 2], add_at[NKT - 1]]
                prev_denq = denq_at[NKT // DQ - 1]
                scores_end[gi] = scores_at[NKT - 1]
                pending_tail[0] = make_tail(gi, qpar, gpos, b, prev_denq)
        pending_tail[0]()
        pending_tail[0] = None
        ab_stage_done = c.dY.n

        def cc2b(e):
            with allow_ici_all_to_all():
                return e.collective_compute(
                    "AllToAll", ALU.bypass, replica_groups=[list(range(NCORES))],
                    ins=[abin[1][:, :]], outs=[about[1][:, :]])
        P.g(cc2b, wait=((c.dY, ab_stage_done),), inc=c.sC, inck=1)
        for srcc in range(NCORES):
            P.g(lambda e, srcc=srcc: e.dma_start(
                out=af_a[srcc][:, :], in_=blk2d(about[0], srcc, 0, 128, SC)),
                wait=((c.sC, sc0 + 2),) if srcc == 0 else (), inc=c.dG)
        af_ld_part[0] = c.dG.n
        for srcc in range(NCORES):
            P.g(lambda e, srcc=srcc: e.dma_start(
                out=af_b[srcc][:, :], in_=blk2d(about[1], srcc, 0, 128, SC)),
                wait=((c.sC, sc0 + 3),) if srcc == 0 else (), inc=c.dG)
        af_ld_part[1] = c.dG.n

        P.gate("tensor", ((c.sV, c.sV.n), (c.sS, c.sS.n)))
        prev_res = hn_done
        x_st = 0
        for tcc in range(T // TC):
            sl = slice(tcc * TC, (tcc + 1) * TC)
            for pt in range(NPT):
                P.y(lambda e, pt=pt, sl=sl: e.dma_start(out=xc[pt][:, :], in_=xT[pt * 128:(pt + 1) * 128, sl]),
                    wait=((c.sV, prev_res), (c.dY, x_st)), inc=c.dY)
            x_ld = c.dY.n
            afh = af_a if tcc < 2 else af_b
            asl = slice(sl.start - (0 if tcc < 2 else SC), sl.stop - (0 if tcc < 2 else SC))
            af_ok = af_ld_part[0 if tcc < 2 else 1]
            for ot in range(NPT):
                rg = ot % 6
                for kt in range(NPT):
                    P.t(lambda e, ot=ot, kt=kt, rg=rg, afh=afh, asl=asl: (
                        e.ldweights(wv_sb[kt][:, ot * 128:(ot + 1) * 128]),
                        e.matmul(c.reg(rg), wv_sb[kt][:, ot * 128:(ot + 1) * 128], afh[kt][:, asl],
                                 start=(kt == 0), stop=(kt == NPT - 1)))[-1],
                        wait=(((c.dG, wo_ld), (c.dG, af_ok), (c.sV, ev_done.get(rg, 0))) if kt == 0 else ()),
                        inc=c.sT)
                P.v(lambda e, ot=ot, rg=rg: e.tensor_scalar_add(yc[ot][:, :], c.reg(rg), outb_sb[:, ot:ot + 1]),
                    wait=((c.sT, c.sT.n),), inc=c.sV)
                ev_done[rg] = c.sV.n
            P.gate_all(((c.sV, c.sV.n), (c.dY, x_ld)), engines=("vector", "scalar", "tensor"))
            resid_done = emit_ln(P, c, [yc[pt][:, :] for pt in range(NPT)], TC,
                                 resid_aps=[xc[pt][:, :] for pt in range(NPT)])
            for pt in range(NPT):
                P.y(lambda e, pt=pt, sl=sl: e.dma_start(out=xO[pt * 128:(pt + 1) * 128, sl], in_=xc[pt][:, :]),
                    wait=((c.sV, resid_done),), inc=c.dY)
            x_st = c.dY.n
            prev_res = emit_ln(P, c, [xc[pt][:, :] for pt in range(NPT)], TC,
                               out_aps=[hn[pt][:, sl] for pt in range(NPT)])
        P.wait_only("sync", ((c.dY, c.dY.n),))


def emit_l0ffn(P, c, nc, io):
    xT = io["xb"]
    w1 = io["w1"]
    b1 = io["b1"]
    w2 = io["w2"]
    b2 = io["b2"]
    xO = io["xb"]
    TC = 512
    NH = FF // 128

    with contextlib.ExitStack() as stack:
        ec = stack.enter_context
        xc = [ec(nc.sbuf_tensor(f"F_xc{pt}", [128, TC], F32)) for pt in range(NPT)]
        yc = [ec(nc.sbuf_tensor(f"F_yc{pt}", [128, TC], BF16)) for pt in range(NPT)]
        g_sb = [ec(nc.sbuf_tensor(f"F_g{kt}", [128, TC], BF16)) for kt in range(NH)]
        w1b = [ec(nc.sbuf_tensor(f"F_w1b{pt}", [128, FF], BF16)) for pt in range(NPT)]
        w2b = [ec(nc.sbuf_tensor(f"F_w2b{i}", [128, D], BF16)) for i in range(16)]
        b1t = ec(nc.sbuf_tensor("F_b1t", [128, 32], F32))
        b2t = ec(nc.sbuf_tensor("F_b2t", [128, 8], F32))

        for pt in range(NPT):
            P.g(lambda e, pt=pt: e.dma_start(out=w1b[pt][:, :], in_=w1[pt * 128:(pt + 1) * 128, :]), inc=c.dG)
        P.y(lambda e: e.dma_start(out=b1t[:, :], in_=b1[:, :]), inc=c.dY)
        P.y(lambda e: e.dma_start(out=b2t[:, :], in_=b2[:, :]), inc=c.dY)
        w1_loaded = c.dG.n
        const_loaded = c.dY.n
        P.gate("scalar", ((c.dY, const_loaded),))
        P.gate("vector", ((c.dY, const_loaded),))

        x_stored = [0] * (T // TC)
        prev_resid_done = 0
        prev_lnps = 0
        w2_cons = 0
        for tc in range(T // TC):
            sl = slice(tc * TC, (tc + 1) * TC)
            prev_store = x_stored[tc - 1] if tc > 0 else 0
            for pt in range(NPT):
                P.y(lambda e, pt=pt, sl=sl: e.dma_start(out=xc[pt][:, :], in_=xT[pt * 128:(pt + 1) * 128, sl]),
                    wait=((c.sV, prev_resid_done), (c.dY, prev_store)), inc=c.dY)
            x_ld = c.dY.n
            evac_done_at = [0] * 8
            for ot in range(NH):
                rg = ot % 6
                for kt in range(NPT):
                    P.t(lambda e, ot=ot, kt=kt, rg=rg, sl=sl: (
                        e.ldweights(w1b[kt][:, ot * 128:(ot + 1) * 128]),
                        e.matmul(c.reg(rg), w1b[kt][:, ot * 128:(ot + 1) * 128], c.hs[kt][:, sl],
                                 start=(kt == 0), stop=(kt == NPT - 1)))[-1],
                        wait=(((c.dG, w1_loaded), (c.sS, evac_done_at[rg]))
                              if kt == 0 else ()),
                        inc=c.sT)
                mm_done = c.sT.n
                P.s(lambda e, ot=ot, rg=rg: e.activation(out=g_sb[ot][:, :], in_=c.reg(rg), func=AF.Gelu,
                                                         bias=b1t[:, ot:ot + 1]),
                    wait=((c.sT, mm_done),), inc=c.sS)
                evac_done_at[rg] = c.sS.n
            g_done = c.sS.n
            yev = 0
            for half in range(2):
                for i in range(16):
                    P.g(lambda e, half=half, i=i: e.dma_start(
                        out=w2b[i][:, :], in_=w2[(half * 16 + i) * 128:(half * 16 + i + 1) * 128, :]),
                        wait=((c.sT, w2_cons),) if i == 0 else (), inc=c.dG)
                w2_ld = c.dG.n
                for ot in range(NPT):
                    for i in range(16):
                        kt = half * 16 + i
                        P.t(lambda e, ot=ot, i=i, kt=kt: (
                            e.ldweights(w2b[i][:, ot * 128:(ot + 1) * 128]),
                            e.matmul(c.reg(ot), w2b[i][:, ot * 128:(ot + 1) * 128], g_sb[kt][:, :],
                                     start=(kt == 0), stop=(kt == NH - 1)))[-1],
                            wait=(((c.dG, w2_ld), (c.sS, g_done)) +
                                  (((c.sV, prev_lnps),) if (kt == 0 and ot >= 6) else ())
                                  if i == 0 else ()),
                            inc=c.sT)
                w2_cons = c.sT.n
            mm2_done = c.sT.n
            for ot in range(NPT):
                P.v(lambda e, ot=ot: e.tensor_scalar_add(yc[ot][:, :], c.reg(ot), b2t[:, ot:ot + 1]),
                    wait=((c.sT, mm2_done),), inc=c.sV)
            y_done_v = c.sV.n
            P.gate_all(((c.sV, y_done_v), (c.dY, x_ld)), engines=("vector", "tensor", "scalar"))
            resid_done = emit_ln(P, c, [yc[pt][:, :] for pt in range(NPT)], TC,
                                 resid_aps=[xc[pt][:, :] for pt in range(NPT)])
            for pt in range(NPT):
                P.y(lambda e, pt=pt, sl=sl: e.dma_start(out=xO[pt * 128:(pt + 1) * 128, sl], in_=xc[pt][:, :]),
                    wait=((c.sV, resid_done),), inc=c.dY)
            x_stored[tc] = c.dY.n
            prev_resid_done = emit_ln(P, c, [xc[pt][:, :] for pt in range(NPT)], TC,
                                      out_aps=[c.hs[pt][:, sl] for pt in range(NPT)])
            prev_lnps = prev_resid_done
        P.wait_only("sync", ((c.dY, c.dY.n),))


def wslice3(wT, ot):
    return wT[:, ot * 128:(ot + 1) * 128].rearrange("(k p) q -> p k q", k=8, p=128)


def emit_l1attn(P, c, nc, io):
    xT = io["xb"]
    wq = io["wq"]
    wk = io["wk"]
    wv = io["wv"]
    wo = io["wo"]
    hseld = io["hsel"]
    hbcd = io["hbc"]
    xO = io["xb"]
    TC = 512
    HT = T // 2

    with contextlib.ExitStack() as stack:
        ec = stack.enter_context
        xc = [ec(nc.sbuf_tensor(f"C_xc{pt}", [128, TC], F32)) for pt in range(NPT)]
        hn = c.hs
        bcol = ec(nc.sbuf_tensor("C_bcol", [128, 16], F32))
        v_sb = [ec(nc.sbuf_tensor(f"C_v{pt}", [128, T], BF16)) for pt in range(NPT)]
        yc = [ec(nc.sbuf_tensor(f"C_yc{pt}", [128, TC], BF16)) for pt in range(NPT)]
        wb = [ec(nc.sbuf_tensor(f"C_wb{i}", [128, 1024], BF16)) for i in range(2)]
        hsel_sb = ec(nc.sbuf_tensor("C_hsel_sb", [128, 128], BF16))
        hbc_sb = ec(nc.sbuf_tensor("C_hbc_sb", [16, 1024], BF16))
        prod = [ec(nc.sbuf_tensor(f"C_prod{i}", [128, HT], BF16)) for i in range(2)]
        s_sb = [ec(nc.sbuf_tensor(f"C_s{i}", [16, HT], BF16)) for i in range(4)]
        z_sb = [ec(nc.sbuf_tensor(f"C_z{i}", [16, HT], F32)) for i in range(2)]
        w_sb = [ec(nc.sbuf_tensor(f"C_wl{i}", [16, HT], BF16)) for i in range(4)]
        l1t = [ec(nc.sbuf_tensor(f"C_l1t{i}", [128, HT], BF16)) for i in range(2)]
        qk_scope = contextlib.ExitStack()
        eq = qk_scope.enter_context
        q_sb = [eq(nc.sbuf_tensor(f"C_q{pt}", [128, T], BF16)) for pt in range(NPT)]
        k_sb = [eq(nc.sbuf_tensor(f"C_k{pt}", [128, T], BF16)) for pt in range(NPT)]

        P.g(lambda e: e.dma_start(out=hsel_sb[:, :], in_=hseld[:, :]), inc=c.dG)
        P.g(lambda e: e.dma_start(out=hbc_sb[:, :], in_=hbcd[:, :]), inc=c.dG)
        consts_g = c.dG.n
        P.gate("tensor", ((c.dG, consts_g),))

        evS = {}
        evV = {}
        w_cons = [0, 0]
        for (wd, dst, actf) in ((wq, q_sb, AF.Relu), (wk, k_sb, AF.Relu), (wv, v_sb, None)):
            for ot in range(NPT):
                par = ot % 2
                P.g(lambda e, wd=wd, ot=ot, par=par: e.dma_start(
                    out=wb[par][:, :].rearrange("p (k q) -> p k q", k=8, q=128),
                    in_=wslice3(wd, ot)),
                    wait=((c.sT, w_cons[par]),), inc=c.dG)
                w_ld = c.dG.n
                for ch in range(4):
                    rg = (ot * 4 + ch) % 8
                    for kt in range(NPT):
                        P.t(lambda e, ot=ot, ch=ch, kt=kt, rg=rg, par=par: (
                            e.ldweights(wb[par][:, kt * 128:(kt + 1) * 128]),
                            e.matmul(c.reg(rg), wb[par][:, kt * 128:(kt + 1) * 128],
                                     hn[kt][:, ch * 512:(ch + 1) * 512],
                                     start=(kt == 0), stop=(kt == NPT - 1)))[-1],
                            wait=(((c.dG, w_ld),
                                   (c.sS, evS.get(rg, 0)), (c.sV, evV.get(rg, 0))) if kt == 0 else ()),
                            inc=c.sT)
                    if actf is not None:
                        P.s(lambda e, dst=dst, ot=ot, ch=ch, rg=rg, actf=actf: e.activation(
                            out=dst[ot][:, ch * 512:(ch + 1) * 512], in_=c.reg(rg), func=actf),
                            wait=((c.sT, c.sT.n),), inc=c.sS)
                        evS[rg] = c.sS.n
                    else:
                        P.v(lambda e, dst=dst, ot=ot, ch=ch, rg=rg: e.tensor_copy(
                            dst[ot][:, ch * 512:(ch + 1) * 512], c.reg(rg)),
                            wait=((c.sT, c.sT.n),), inc=c.sV)
                        evV[rg] = c.sV.n
                w_cons[par] = c.sT.n
        proj_done_v = c.sV.n
        proj_done_s = c.sS.n

        P.gate_all(((c.sV, proj_done_v), (c.sS, proj_done_s)), engines=("vector", "tensor"))
        combos = [(0, 0), (0, 1), (1, 0), (1, 1)]
        for ci, (l, lp) in enumerate(combos):
            for pt in range(NPT):
                par = pt % 2
                P.v(lambda e, pt=pt, par=par, l=l, lp=lp: e.tensor_tensor(
                    prod[par][:, :], q_sb[pt][:, l * HT:(l + 1) * HT],
                    k_sb[pt][:, lp * HT:(lp + 1) * HT], ALU.mult),
                    wait=((c.sT, evS.get(("p", par), 0)), (c.sV, c.sV.n)), inc=c.sV)
                pd = c.sV.n
                for ch in range(2):
                    P.t(lambda e, pt=pt, par=par, ch=ch: (
                        e.ldweights(hsel_sb[:, pt * 16:pt * 16 + 16]),
                        e.matmul(c.psBig[0:16, ch * 512:(ch + 1) * 512],
                                 hsel_sb[:, pt * 16:pt * 16 + 16],
                                 prod[par][:, ch * 512:(ch + 1) * 512],
                                 start=(pt == 0), stop=(pt == NPT - 1), skip_group_check=True))[-1],
                        wait=((c.sV, pd),), inc=c.sT)
                evS[("p", par)] = c.sT.n
            P.v(lambda e, ci=ci: e.tensor_copy(s_sb[ci][:, :], c.psBig[0:16, 0:HT]),
                wait=((c.sT, c.sT.n),), inc=c.sV)
        s_done = c.sV.n
        qk_scope.close()
        wo_sb = [ec(nc.sbuf_tensor(f"C_wo{kt}", [128, 1024], BF16)) for kt in range(NPT)]
        for kt in range(NPT):
            P.g(lambda e, kt=kt: e.dma_start(out=wo_sb[kt][:, :], in_=wo[kt * 128:(kt + 1) * 128, :]),
                wait=((c.sV, s_done),) if kt == 0 else (), inc=c.dG)
        wo_res = c.dG.n

        P.v(lambda e: e.scalar_tensor_tensor(z_sb[0][:, :], s_sb[1][:, :], C01, s_sb[0][:, :],
                                             ALU.mult, ALU.add),
            wait=((c.sV, s_done),), inc=c.sV)
        P.v(lambda e: e.scalar_tensor_tensor(z_sb[1][:, :], s_sb[2][:, :], C01, s_sb[3][:, :],
                                             ALU.mult, ALU.add),
            wait=((c.sV, c.sV.n),), inc=c.sV)
        P.v(lambda e: e.tensor_scalar_max(z_sb[0][:, :], z_sb[0][:, :], EPS_L1),
            wait=((c.sV, c.sV.n),), inc=c.sV)
        P.v(lambda e: e.tensor_scalar_max(z_sb[1][:, :], z_sb[1][:, :], EPS_L1),
            wait=((c.sV, c.sV.n),), inc=c.sV)
        P.v(lambda e: e.reciprocal(z_sb[0][:, :], z_sb[0][:, :]), wait=((c.sV, c.sV.n),), inc=c.sV)
        P.v(lambda e: e.reciprocal(z_sb[1][:, :], z_sb[1][:, :]), wait=((c.sV, c.sV.n),), inc=c.sV)
        z_done = c.sV.n
        P.v(lambda e: e.tensor_tensor(w_sb[0][:, :], s_sb[0][:, :], z_sb[0][:, :], ALU.mult),
            wait=((c.sV, z_done),), inc=c.sV)
        P.v(lambda e: e.tensor_tensor(w_sb[1][:, :], s_sb[1][:, :], z_sb[0][:, :], ALU.mult),
            inc=c.sV)
        P.v(lambda e: e.tensor_tensor(w_sb[2][:, :], s_sb[2][:, :], z_sb[1][:, :], ALU.mult),
            inc=c.sV)
        P.v(lambda e: e.tensor_tensor(w_sb[3][:, :], s_sb[3][:, :], z_sb[1][:, :], ALU.mult),
            inc=c.sV)
        P.v(lambda e: e.tensor_scalar_mul(w_sb[1][:, :], w_sb[1][:, :], C01),
            wait=((c.sV, c.sV.n),), inc=c.sV)
        P.v(lambda e: e.tensor_scalar_mul(w_sb[2][:, :], w_sb[2][:, :], C01),
            wait=((c.sV, c.sV.n),), inc=c.sV)
        w_done = c.sV.n

        P.gate("tensor", ((c.sV, w_done),))
        wbmm = 0
        for pt in range(NPT):
            for l in range(2):
                for lp in range(2):
                    wi = l * 2 + lp
                    for ch in range(2):
                        P.t(lambda e, pt=pt, lp=lp, wi=wi, ch=ch: (
                            e.ldweights(hbc_sb[:, pt * 128:(pt + 1) * 128]),
                            e.matmul(c.psBig[:, lp * 1024 + ch * 512: lp * 1024 + (ch + 1) * 512],
                                     hbc_sb[:, pt * 128:(pt + 1) * 128],
                                     w_sb[wi][:, ch * 512:(ch + 1) * 512],
                                     start=True, stop=True, skip_group_check=True))[-1],
                            wait=((c.sV, wbmm),), inc=c.sT)
                wb_T = c.sT.n
                P.v(lambda e, pt=pt, l=l: e.tensor_tensor(
                    l1t[0][:, :], c.psBig[:, 0:1024], v_sb[pt][:, 0:HT], ALU.mult),
                    wait=((c.sT, wb_T), (c.sV, c.sV.n)), inc=c.sV)
                P.v(lambda e, pt=pt, l=l: e.tensor_tensor(
                    l1t[1][:, :], c.psBig[:, 1024:2048], v_sb[pt][:, HT:T], ALU.mult),
                    wait=((c.sV, c.sV.n),), inc=c.sV)
                P.v(lambda e, pt=pt, l=l: e.tensor_tensor(
                    hn[pt][:, l * HT:(l + 1) * HT], l1t[0][:, :], l1t[1][:, :], ALU.add),
                    wait=((c.sV, c.sV.n),), inc=c.sV)
                wbmm = c.sV.n
        attn_done = c.sV.n

        aab_in = io["aab_in"]
        aab_out = io["aab_out"]
        P.gate("tensor", ((c.sV, attn_done),))
        prev_res = 0
        x_st = 0
        for ci, tcc in enumerate((1, 3, 0, 2)):
            sl = slice(tcc * TC, (tcc + 1) * TC)
            for pt in range(NPT):
                P.y(lambda e, pt=pt, sl=sl: e.dma_start(out=xc[pt][:, :], in_=xT[pt * 128:(pt + 1) * 128, sl]),
                    wait=((c.sV, prev_res), (c.dY, x_st)), inc=c.dY)
            x_ld = c.dY.n
            for ot in range(NPT):
                rg = ot % 6
                for kt in range(NPT):
                    P.t(lambda e, ot=ot, kt=kt, rg=rg, sl=sl: (
                        e.ldweights(wo_sb[kt][:, ot * 128:(ot + 1) * 128]),
                        e.matmul(c.reg(rg), wo_sb[kt][:, ot * 128:(ot + 1) * 128], hn[kt][:, sl],
                                 start=(kt == 0), stop=(kt == NPT - 1)))[-1],
                        wait=(((c.dG, wo_res), (c.sV, evV.get(("o", rg), 0))) if kt == 0 else ()),
                        inc=c.sT)
                P.v(lambda e, ot=ot, rg=rg: e.tensor_copy(yc[ot][:, :], c.reg(rg)),
                    wait=((c.sT, c.sT.n),), inc=c.sV)
                evV[("o", rg)] = c.sV.n
            P.gate_all(((c.sV, c.sV.n), (c.dY, x_ld)), engines=("vector", "scalar", "tensor"))
            resid_done = emit_ln(P, c, [yc[pt][:, :] for pt in range(NPT)], TC,
                                 resid_aps=[xc[pt][:, :] for pt in range(NPT)])
            for pt in range(NPT):
                P.y(lambda e, pt=pt, sl=sl: e.dma_start(out=xO[pt * 128:(pt + 1) * 128, sl], in_=xc[pt][:, :]),
                    wait=((c.sV, resid_done),), inc=c.dY)
            x_st = c.dY.n
            prev_res = emit_ln(P, c, [xc[pt][:, :] for pt in range(NPT)], TC,
                               out_aps=[hn[pt][:, sl] for pt in range(NPT)])
            if ci == 1:
                for pt in range(NPT):
                    for b in range(B):
                        col = pt * B + b
                        P.v(lambda e, pt=pt, b=b, col=col: e.tensor_copy(
                            bcol[:, col:col + 1], hn[pt][:, b * SC + SC - 1:b * SC + SC]),
                            wait=((c.sV, prev_res),) if col == 0 else (), inc=c.sV)
                bcol_done = c.sV.n
                for d in range(NCORES):
                    P.y(lambda e, d=d: e.dma_start(out=blk2d(aab_in, d, 0, 128, 16), in_=bcol[:, :]),
                        wait=((c.sV, bcol_done),) if d == 0 else (), inc=c.dY)
                ab_staged = c.dY.n

                def cc3(e):
                    with allow_ici_all_to_all():
                        return e.collective_compute(
                            "AllToAll", ALU.bypass, replica_groups=[list(range(NCORES))],
                            ins=[aab_in[:, :]], outs=[aab_out[:, :]])
                P.g(cc3, wait=((c.dY, ab_staged),), inc=c.sC, inck=1)
        P.wait_only("sync", ((c.dY, c.dY.n),))


def emit_l1ffn(P, c, nc, io):
    xT = io["xb"]
    wr = io["wr"]
    wkk = io["wkk"]
    wvv = io["wvv"]
    mur = io["mur"]
    muk = io["muk"]
    bmaskd = io["bmask"]
    xO = io["xO"]
    aab_out = io["aab_out"]
    TC = 512
    cc3_at = c.sC.n

    with contextlib.ExitStack() as stack:
        ec = stack.enter_context
        xc = [ec(nc.sbuf_tensor(f"R_xc{pt}", [128, TC], F32)) for pt in range(NPT)]
        hn = c.hs
        hp = [ec(nc.sbuf_tensor(f"R_hp{pt}", [128, T], BF16)) for pt in range(NPT)]
        xr = [ec(nc.sbuf_tensor(f"R_xr{pt}", [128, T], BF16)) for pt in range(NPT)]
        r_sb = [ec(nc.sbuf_tensor(f"R_r{pt}", [128, T], BF16)) for pt in range(NPT)]
        yc = [ec(nc.sbuf_tensor(f"R_yc{pt}", [128, TC], BF16)) for pt in range(NPT)]
        wb = [ec(nc.sbuf_tensor(f"R_wb{i}", [128, 1024], BF16)) for i in range(2)]
        murt = ec(nc.sbuf_tensor("R_murt", [128, 8], F32))
        mukt = ec(nc.sbuf_tensor("R_mukt", [128, 8], F32))
        dtb = [ec(nc.sbuf_tensor(f"R_dtb{i}", [128, T], BF16)) for i in range(1)]
        wvv_sb = [ec(nc.sbuf_tensor(f"R_wvv{kt}", [128, 1024], BF16)) for kt in range(NPT)]
        bndall = ec(nc.sbuf_tensor("R_bndall", [128, 128], F32))
        hpb = ec(nc.sbuf_tensor("R_hpb", [128, 16], F32))
        bmask_sb = ec(nc.sbuf_tensor("R_bmask_sb", [128, 128], F32))

        P.y(lambda e: e.dma_start(out=murt[:, :], in_=mur[:, :]), inc=c.dY)
        P.y(lambda e: e.dma_start(out=mukt[:, :], in_=muk[:, :]), inc=c.dY)
        P.y(lambda e: e.dma_start(out=bmask_sb[:, :], in_=bmaskd[:, :]), inc=c.dY)
        consts_y = c.dY.n

        for pt in range(NPT):
            for b in range(B):
                P.y(lambda e, pt=pt, b=b: e.dma_start(
                    out=hp[pt][:, b * SC + 1:(b + 1) * SC], in_=hn[pt][:, b * SC:(b + 1) * SC - 1]),
                    inc=c.dY)
        hp_shift = c.dY.n
        for s in range(NCORES):
            P.y(lambda e, s=s: e.dma_start(out=bndall[:, s * 16:(s + 1) * 16],
                                           in_=blk2d(aab_out, s, 0, 128, 16)),
                wait=((c.sC, cc3_at),) if s == 0 else (), inc=c.dY)
        bnd_ld = c.dY.n
        P.v(lambda e: e.tensor_tensor(bndall[:, :], bndall[:, :], bmask_sb[:, :], ALU.mult),
            wait=((c.dY, bnd_ld), (c.dY, consts_y), (c.sV, c.sV.n)), inc=c.sV)
        P.v(lambda e: e.tensor_copy(hpb[:, :], bndall[:, 0:16]), wait=((c.sV, c.sV.n),), inc=c.sV)
        for s in range(1, NCORES):
            P.v(lambda e, s=s: e.tensor_tensor(hpb[:, :], hpb[:, :], bndall[:, s * 16:(s + 1) * 16],
                                               ALU.add),
                wait=((c.sV, c.sV.n),), inc=c.sV)
        hpb_done = c.sV.n
        for pt in range(NPT):
            for b in range(B):
                col = pt * B + b
                P.v(lambda e, pt=pt, b=b, col=col: e.tensor_copy(
                    hp[pt][:, b * SC:b * SC + 1], hpb[:, col:col + 1]),
                    wait=((c.sV, hpb_done), (c.dY, hp_shift)) if col == 0 else (), inc=c.sV)
        hp_done = c.sV.n

        for pt in range(NPT):
            P.v(lambda e, pt=pt: e.tensor_sub(dtb[0][:, :], hn[pt][:, :], hp[pt][:, :]),
                wait=((c.sV, c.sV.n), (c.sV, hp_done), (c.dY, hp_shift)), inc=c.sV)
            P.v(lambda e, pt=pt: e.scalar_tensor_tensor(
                xr[pt][:, :], dtb[0][:, :], murt[:, pt:pt + 1], hp[pt][:, :], ALU.mult, ALU.add),
                wait=((c.sV, c.sV.n), (c.dY, consts_y)), inc=c.sV)
        xr_done = c.sV.n

        ev_done = {}
        w_cons = [0, 0]

        def proj(P, wd, rhs_tiles, rhs_ready, evac, name):
            for ot in range(NPT):
                par = ot % 2
                P.g(lambda e, wd=wd, ot=ot, par=par: e.dma_start(
                    out=wb[par][:, :].rearrange("p (k q) -> p k q", k=8, q=128),
                    in_=wslice3(wd, ot)),
                    wait=((c.sT, w_cons[par]),), inc=c.dG)
                w_ld = c.dG.n
                for ch in range(4):
                    rg = (ot * 4 + ch) % 8
                    for kt in range(NPT):
                        P.t(lambda e, ot=ot, ch=ch, kt=kt, rg=rg, par=par: (
                            e.ldweights(wb[par][:, kt * 128:(kt + 1) * 128]),
                            e.matmul(c.reg(rg), wb[par][:, kt * 128:(kt + 1) * 128],
                                     rhs_tiles[kt][:, ch * 512:(ch + 1) * 512],
                                     start=(kt == 0), stop=(kt == NPT - 1)))[-1],
                            wait=(((c.dG, w_ld), rhs_ready, (ev_done.get(rg, (c.sV, 0)))) if kt == 0 else ()),
                            inc=c.sT)
                    evac(ot, ch, rg)
                w_cons[par] = c.sT.n

        def evac_sig(ot, ch, rg):
            P.s(lambda e, ot=ot, ch=ch, rg=rg: e.activation(
                out=r_sb[ot][:, ch * 512:(ch + 1) * 512], in_=c.reg(rg), func=AF.Sigmoid),
                wait=((c.sT, c.sT.n),), inc=c.sS)
            ev_done[rg] = (c.sS, c.sS.n)
        proj(P, wr, xr, (c.sV, xr_done), evac_sig, "r")
        r_done_s = c.sS.n

        P.gate("vector", ((c.sT, c.sT.n),))
        for pt in range(NPT):
            P.v(lambda e, pt=pt: e.tensor_sub(dtb[0][:, :], hn[pt][:, :], hp[pt][:, :]),
                wait=((c.sV, c.sV.n),), inc=c.sV)
            P.v(lambda e, pt=pt: e.scalar_tensor_tensor(
                xr[pt][:, :], dtb[0][:, :], mukt[:, pt:pt + 1], hp[pt][:, :], ALU.mult, ALU.add),
                wait=((c.sV, c.sV.n),), inc=c.sV)
        xk_done = c.sV.n

        def evac_relu2(ot, ch, rg):
            P.s(lambda e, ot=ot, ch=ch, rg=rg: e.activation(
                out=hp[ot][:, ch * 512:(ch + 1) * 512], in_=c.reg(rg), func=AF.Relu),
                wait=((c.sT, c.sT.n), (c.sV, xk_done)), inc=c.sS)
            P.s(lambda e, ot=ot, ch=ch: e.activation(
                out=hp[ot][:, ch * 512:(ch + 1) * 512], in_=hp[ot][:, ch * 512:(ch + 1) * 512],
                func=AF.Square),
                wait=((c.sS, c.sS.n),), inc=c.sS)
            ev_done[rg] = (c.sS, c.sS.n)
        P.gate("tensor", ((c.sV, xk_done),))
        proj(P, wkk, xr, (c.sV, xk_done), evac_relu2, "ka")

        for kt in range(NPT):
            P.g(lambda e, kt=kt: e.dma_start(out=wvv_sb[kt][:, :], in_=wvv[kt * 128:(kt + 1) * 128, :]),
                inc=c.dG)
        wvv_res = c.dG.n
        P.gate("tensor", ((c.sS, c.sS.n),))
        prev_res = 0
        x_st = 0
        for tcc in range(T // TC):
            sl = slice(tcc * TC, (tcc + 1) * TC)
            for pt in range(NPT):
                P.y(lambda e, pt=pt, sl=sl: e.dma_start(out=xc[pt][:, :], in_=xT[pt * 128:(pt + 1) * 128, sl]),
                    wait=((c.sV, prev_res), (c.dY, x_st)), inc=c.dY)
            x_ld = c.dY.n
            for ot in range(NPT):
                rg = ot % 6
                for kt in range(NPT):
                    P.t(lambda e, ot=ot, kt=kt, rg=rg, sl=sl: (
                        e.ldweights(wvv_sb[kt][:, ot * 128:(ot + 1) * 128]),
                        e.matmul(c.reg(rg), wvv_sb[kt][:, ot * 128:(ot + 1) * 128], hp[kt][:, sl],
                                 start=(kt == 0), stop=(kt == NPT - 1)))[-1],
                        wait=(((c.dG, wvv_res), (ev_done.get(("y", rg), (c.sV, 0)))) if kt == 0 else ()),
                        inc=c.sT)
                P.v(lambda e, ot=ot, rg=rg, sl=sl: e.tensor_tensor(
                    yc[ot][:, :], c.reg(rg), r_sb[ot][:, sl], ALU.mult),
                    wait=((c.sT, c.sT.n), (c.sS, r_done_s)), inc=c.sV)
                ev_done[("y", rg)] = (c.sV, c.sV.n)
            P.gate_all(((c.sV, c.sV.n), (c.dY, x_ld)), engines=("vector", "scalar", "tensor"))
            prev_res = emit_ln(P, c, [yc[pt][:, :] for pt in range(NPT)], TC,
                               resid_aps=[xc[pt][:, :] for pt in range(NPT)])
            for pt in range(NPT):
                P.y(lambda e, pt=pt, sl=sl: e.dma_start(out=xO[pt * 128:(pt + 1) * 128, sl], in_=xc[pt][:, :]),
                    wait=((c.sV, prev_res),), inc=c.dY)
            x_st = c.dY.n
        P.wait_only("sync", ((c.dY, c.dY.n),))


def build_full():
    nc = bass.Bass("TRN2", target_bir_lowering=False)
    io = {}

    def inp(name, shape):
        io[name] = nc.dram_tensor(name, shape, F32, kind="ExternalInput").ap()
    inp("xT", [D, T])
    inp("qkvw4", [16, 8, 128, 128])
    inp("wvT", [D, D])
    inp("qk_b", [128, 16])
    inp("bvt", [128, 1])
    inp("owT", [D, D])
    inp("out_b", [128, 8])
    inp("CCS", [128, T])
    inp("SSG", [128, T])
    inp("w1", [D, FF])
    inp("b1", [128, 32])
    inp("w2", [FF, D])
    inp("b2", [128, 8])
    inp("wq", [D, D])
    inp("wk", [D, D])
    inp("wv", [D, D])
    inp("wo", [D, D])
    inp("hsel", [128, 128])
    inp("hbc", [16, 1024])
    inp("wr", [D, D])
    inp("wkk", [D, D])
    inp("wvv", [D, D])
    inp("mur", [128, 8])
    inp("muk", [128, 8])
    inp("bmask", [128, 128])
    inp("ones", [128, 128])
    io["xb"] = nc.dram_tensor("xb", [D, T], F32).ap()
    io["aab_in"] = nc.dram_tensor("aab_in", [NCORES, 128 * 16], F32).ap()
    io["aab_out"] = nc.dram_tensor("aab_out", [NCORES, 128 * 16], F32).ap()
    io["xO"] = nc.dram_tensor("xO", [D, T], F32, kind="ExternalOutput").ap()

    with contextlib.ExitStack() as stack:
        P = Prog(nc, stack)
        c = Ctx(P, 512)
        c.load_const(P, io["ones"])
        emit_l0attn(P, c, nc, io)
        barrier(P, c)
        emit_l0ffn(P, c, nc, io)
        barrier(P, c)
        emit_l1attn(P, c, nc, io)
        barrier(P, c)
        emit_l1ffn(P, c, nc, io)
        P.run_all()
    return nc


def kernel(**inputs):
    h = _host_prep(inputs)
    nc = build_full()
    in_maps = []
    for cid in range(NCORES):
        bmask = np.zeros((128, 128), np.float32)
        if cid > 0:
            bmask[:, (cid - 1) * 16:cid * 16] = 1.0
        in_maps.append({
            "xT": h["xT"][cid], "qkvw4": h["qkv_wt4"], "wvT": h["wvT"], "qk_b": h["qk_b"],
            "bvt": h["bvt"][cid], "owT": h["out_wT"], "out_b": h["out_b"],
            "CCS": h["CCS"][cid], "SSG": h["SSG"][cid], "ones": h["ones"],
            "w1": h["w1"], "b1": h["b1"], "w2": h["w2"], "b2": h["b2"],
            "wq": h["WqT"], "wk": h["WkT"], "wv": h["WvT"], "wo": h["WoT"],
            "hsel": h["hsel"], "hbc": h["hbc"],
            "wr": h["W_rT"], "wkk": h["W_kT"], "wvv": h["W_vT"],
            "mur": h["mu_r"], "muk": h["mu_k"], "bmask": bmask,
        })
    res = run_bass_kernel_spmd(nc, in_maps, core_ids=list(range(NCORES)))
    out = np.empty((B, S, D), np.float32)
    for cid in range(NCORES):
        xo = np.asarray(res.results[cid]["xO"])
        out[:, cid * SC:(cid + 1) * SC, :] = xo.T.reshape(B, SC, D)
    return out


# version 14
# speedup vs baseline: 1.0219x; 1.0157x over previous
import contextlib
import numpy as np
import ml_dtypes
import concourse.bass as bass
import concourse.mybir as mybir
from concourse.bass_utils import run_bass_kernel_spmd
from concourse.aot_env import allow_ici_all_to_all

F32 = mybir.dt.float32
BF16 = mybir.dt.bfloat16
AF = mybir.ActivationFunctionType
ALU = mybir.AluOpType

NCORES = 8
B = 2
S = 8192
D = 1024
NPT = D // 128
SC = S // NCORES
T = B * SC
FF = 4096
EPS_LN = 1e-5
EPS_L1 = 1e-6
C01 = float(np.sqrt(0.5))


class Sem:
    def __init__(self, h):
        self.h = h
        self.n = 0


class Prog:
    ENGINES = ("tensor", "vector", "scalar", "gpsimd", "sync")

    def __init__(self, nc, stack):
        self.nc = nc
        self.stack = stack
        self.q = {e: [] for e in self.ENGINES}
        self.pending = {e: [] for e in self.ENGINES}

    def sem(self, name):
        return Sem(self.stack.enter_context(self.nc.semaphore(name)))

    def gate(self, eng, wait):
        for w in wait:
            self.pending[eng].append(w if not isinstance(w, Sem) else (w, w.n))

    def gate_all(self, wait, engines=None):
        for e in (engines or self.ENGINES):
            self.gate(e, wait)

    def wait_only(self, eng, wait):
        waits = [(s.h, v) for s, v in ((w, w.n) if isinstance(w, Sem) else w for w in wait) if v > 0]
        self.q[eng].append(lambda e, waits=tuple(waits): [e.wait_ge(h, v) for h, v in waits])

    def emit(self, eng, fn, wait=(), inc=None, inck=1):
        waits = []
        wait = tuple(self.pending[eng]) + tuple(wait)
        self.pending[eng] = []
        for w in wait:
            if isinstance(w, Sem):
                if w.n > 0:
                    waits.append((w.h, w.n))
            else:
                s, v = w
                if v > 0:
                    waits.append((s.h, v))
        if inc is not None:
            inc.n += inck

        def run(e, waits=tuple(waits), fn=fn, inc=inc, inck=inck):
            for h, v in waits:
                e.wait_ge(h, v)
            r = fn(e)
            if inc is not None:
                r.then_inc(inc.h, inck)

        self.q[eng].append(run)

    def t(self, fn, wait=(), inc=None):
        self.emit("tensor", fn, wait, inc)

    def v(self, fn, wait=(), inc=None):
        self.emit("vector", fn, wait, inc)

    def s(self, fn, wait=(), inc=None):
        self.emit("scalar", fn, wait, inc)

    def g(self, fn, wait=(), inc=None, inck=16):
        self.emit("gpsimd", fn, wait, inc, inck)

    def y(self, fn, wait=(), inc=None, inck=16):
        self.emit("sync", fn, wait, inc, inck)

    def run_all(self):
        block = self.stack.enter_context(self.nc.Block())
        nc = self.nc

        def runner(queue):
            def go(e):
                with nc.allow_low_precision(reason="bf16 matmuls/elementwise blessed by problem tolerance"):
                    for f in queue:
                        f(e)
            return go

        block.tensor(runner(self.q["tensor"]))
        block.vector(runner(self.q["vector"]))
        block.scalar(runner(self.q["scalar"]))
        block.gpsimd(runner(self.q["gpsimd"]))
        block.sync(runner(self.q["sync"]))


def _rope_perm():
    perm = np.zeros(D, dtype=np.int64)
    for p in range(8):
        h0, h1 = 2 * p, 2 * p + 1
        base = p * 128
        perm[base + 0:base + 32] = h0 * 64 + np.arange(0, 64, 2)
        perm[base + 32:base + 64] = h0 * 64 + np.arange(1, 64, 2)
        perm[base + 64:base + 96] = h1 * 64 + np.arange(0, 64, 2)
        perm[base + 96:base + 128] = h1 * 64 + np.arange(1, 64, 2)
    return perm


def _host_prep(inputs):
    h = {}
    x = np.asarray(inputs["L0_attn_x"])
    perm = _rope_perm()

    qkv_w = np.asarray(inputs["L0_attn_qkv_w"])
    qkv_b = np.asarray(inputs["L0_attn_qkv_b"])
    qkv_perm = np.concatenate([perm, D + perm, 2 * D + np.arange(D)])
    h["qkv_wT"] = np.ascontiguousarray(qkv_w[qkv_perm].T)
    h["qkv_b"] = np.ascontiguousarray(qkv_b[qkv_perm].reshape(24, 128).T)
    h["out_wT"] = np.ascontiguousarray(np.asarray(inputs["L0_attn_out_w"]).T)
    h["out_b"] = np.ascontiguousarray(np.asarray(inputs["L0_attn_out_b"]).reshape(8, 128).T)

    h["w1"] = np.ascontiguousarray(np.asarray(inputs["L0_ffn_w1"]))
    h["b1"] = np.ascontiguousarray(np.asarray(inputs["L0_ffn_b1"]).reshape(32, 128).T)
    h["w2"] = np.ascontiguousarray(np.asarray(inputs["L0_ffn_w2"]))
    h["b2"] = np.ascontiguousarray(np.asarray(inputs["L0_ffn_b2"]).reshape(8, 128).T)

    for nm in ("Wq", "Wk", "Wv", "Wo"):
        h[nm + "T"] = np.ascontiguousarray(np.asarray(inputs["L1_attn_" + nm]).T)
    for nm in ("W_r", "W_k", "W_v"):
        h[nm + "T"] = np.ascontiguousarray(np.asarray(inputs["L1_ffn_" + nm]).T)
    h["mu_r"] = np.ascontiguousarray(np.asarray(inputs["L1_ffn_mu_r"]).reshape(8, 128).T)
    h["mu_k"] = np.ascontiguousarray(np.asarray(inputs["L1_ffn_mu_k"]).reshape(8, 128).T)

    h["ones"] = np.ones((128, 128), np.float32)
    hsel = np.zeros((128, 8 * 16), np.float32)
    hbc = np.zeros((16, 8 * 128), np.float32)
    for pt in range(8):
        for r in range(128):
            head = 2 * pt + (1 if r >= 64 else 0)
            hsel[r, pt * 16 + head] = 1.0
            hbc[head, pt * 128 + r] = 1.0
    h["hsel"] = hsel
    h["hbc"] = hbc

    h["qkv_wt4"] = np.ascontiguousarray(
        h["qkv_wT"][:, 0:2048].reshape(8, 128, 16, 128).transpose(2, 0, 1, 3))
    h["wvT"] = np.ascontiguousarray(h["qkv_wT"][:, 2048:3072])
    h["qk_b"] = np.ascontiguousarray(h["qkv_b"][:, 0:16])
    h["bvt"] = []
    vb = qkv_b[2 * D:3 * D]
    for c in range(NCORES):
        h["bvt"].append(np.ascontiguousarray(vb[c * 128:(c + 1) * 128].reshape(128, 1)))

    h["xT"] = []
    h["CCS"] = []
    h["SSG"] = []
    inv = (10000.0 ** (-(np.arange(0, 64, 2, dtype=np.float64)) / 64.0))
    for c in range(NCORES):
        xs = x[:, c * SC:(c + 1) * SC, :].reshape(T, D)
        h["xT"].append(np.ascontiguousarray(xs.T))
        pos = np.arange(c * SC, (c + 1) * SC, dtype=np.float64)
        th = np.outer(inv, pos)
        cos = np.cos(th).astype(np.float32)
        sin = np.sin(th).astype(np.float32)
        cosw = np.tile(cos, (1, 2))
        sinw = np.tile(sin, (1, 2))
        h["CCS"].append(np.ascontiguousarray(np.vstack([cosw] * 4)))
        h["SSG"].append(np.ascontiguousarray(np.vstack([-sinw, sinw, -sinw, sinw])))
    return h


class Ctx:

    def __init__(self, P, nt):
        self.nt = nt
        nc, ec = P.nc, P.stack.enter_context
        self.ones_bf = ec(nc.sbuf_tensor("ones_bf", [128, 128], BF16))
        self.ones_f = ec(nc.sbuf_tensor("ones_f", [128, 128], F32))
        self.eps_t = ec(nc.sbuf_tensor("eps_t", [128, 1], F32))
        self.invD = ec(nc.sbuf_tensor("invD", [1, 128], F32))
        self.row = ec(nc.sbuf_tensor("lnrow", [1, 2 * nt], F32))
        self.abc = ec(nc.sbuf_tensor("lnabc", [128, nt], F32))
        self.bbc = ec(nc.sbuf_tensor("lnbbc", [128, nt], F32))
        self.tmp = [ec(nc.sbuf_tensor(f"lntmp{i}", [128, nt], F32)) for i in range(2)]
        self.tmpg = ec(nc.sbuf_tensor("lntmpg", [128, nt], F32))
        self.sq = [ec(nc.sbuf_tensor(f"lnsq{i}", [128, nt], BF16)) for i in range(2)]
        self.hs = [ec(nc.sbuf_tensor(f"hs{pt}", [128, T], BF16)) for pt in range(NPT)]
        self.psBig = ec(nc.psum_tensor("psBig", [128, 2048], F32))
        self.ps = [ec(nc.psum_tensor(f"ps{i}", [128, 512], F32)) for i in range(4)]
        self.sT = P.sem("sT")
        self.sV = P.sem("sV")
        self.sS = P.sem("sS")
        self.sG = P.sem("sG")
        self.dG = P.sem("dG")
        self.dY = P.sem("dY")
        self.sC = P.sem("sC")

    def reg(self, i):
        if i < 4:
            return self.psBig[:, i * 512:(i + 1) * 512]
        return self.ps[i - 4][:, :]

    def load_const(self, P, ones_dram, eps_val=EPS_LN):
        P.y(lambda e: e.dma_start(out=self.ones_f[:, :], in_=ones_dram[:, :]), inc=self.dY)
        P.g(lambda e: e.dma_start(out=self.ones_bf[:, :], in_=ones_dram[:, :]), inc=self.dG)
        P.v(lambda e: e.memset(self.eps_t[:, :], eps_val), inc=self.sV)
        P.v(lambda e: e.memset(self.invD[:, :], 1.0 / D), inc=self.sV)


def emit_ln(P, c, x_aps, nt, out_aps=None, resid_aps=None, xb_stage=None):
    src_bf16 = resid_aps is not None
    v0 = c.sV.n
    if src_bf16:
        xb = x_aps
        xb_ready = [() for _ in range(NPT)]
    else:
        xb = xb_stage if xb_stage is not None else out_aps
        xb_ready = []
        for pt in range(NPT):
            P.s(lambda e, pt=pt: e.copy(xb[pt], x_aps[pt]),
                wait=((c.sS, c.sS.n), (c.sV, v0)), inc=c.sS)
            xb_ready.append(((c.sS, c.sS.n),))
    assert nt <= 512
    P.gate("scalar", ((c.sT, c.sT.n),))
    sq_free_at = [0, 0]
    for pt in range(NPT):
        par = pt % 2
        P.s(lambda e, pt=pt, par=par: e.activation(out=c.sq[par][:, 0:nt], in_=xb[pt], func=AF.Square),
            wait=xb_ready[pt] + ((c.sT, sq_free_at[par]),), inc=c.sS)
        sq_done = c.sS.n
        st, sp = (pt == 0), (pt == NPT - 1)
        P.t(lambda e, pt=pt, st=st, sp=sp: (
            e.ldweights(c.ones_bf[:, 0:1], tile_position=(0, 0)),
            e.matmul(c.ps[2][0:1, 0:nt], c.ones_bf[:, 0:1], xb[pt],
                     start=st, stop=sp, tile_position=(0, 0), skip_group_check=True))[-1],
            wait=xb_ready[pt] + (((c.sV, v0),) if pt == 0 else ()), inc=c.sT)
        P.t(lambda e, par=par, st=st, sp=sp: (
            e.ldweights(c.ones_bf[:, 0:1], tile_position=(0, 0)),
            e.matmul(c.ps[3][0:1, 0:nt], c.ones_bf[:, 0:1], c.sq[par][:, 0:nt],
                     start=st, stop=sp, tile_position=(0, 0), skip_group_check=True))[-1],
            wait=((c.sS, sq_done),), inc=c.sT)
        sq_free_at[par] = c.sT.n
    stats_done = c.sT.n
    r = c.row
    P.s(lambda e: e.copy(r[0:1, 0:nt], c.ps[2][0:1, 0:nt]), wait=((c.sT, stats_done),), inc=c.sS)
    P.s(lambda e: e.copy(r[0:1, nt:2 * nt], c.ps[3][0:1, 0:nt]), inc=c.sS)
    rows_ev = c.sS.n
    P.t(lambda e: e.matmul(c.ps[2][:, 0:nt], c.invD[0:1, :], r[0:1, 0:nt],
                           start=True, stop=True, skip_group_check=True),
        wait=((c.sS, rows_ev),), inc=c.sT)
    P.t(lambda e: e.matmul(c.ps[3][:, 0:nt], c.invD[0:1, :], r[0:1, nt:2 * nt],
                           start=True, stop=True, skip_group_check=True),
        inc=c.sT)
    bc_done = c.sT.n
    P.v(lambda e: e.tensor_copy(c.abc[:, 0:nt], c.ps[2][:, 0:nt]),
        wait=((c.sT, bc_done), (c.sV, c.sV.n)), inc=c.sV)
    P.v(lambda e: e.tensor_copy(c.bbc[:, 0:nt], c.ps[3][:, 0:nt]), inc=c.sV)
    P.v(lambda e: e.tensor_tensor(c.tmp[0][:, 0:nt], c.abc[:, 0:nt], c.abc[:, 0:nt], ALU.mult),
        wait=((c.sV, c.sV.n),), inc=c.sV)
    P.v(lambda e: e.tensor_tensor(c.bbc[:, 0:nt], c.bbc[:, 0:nt], c.tmp[0][:, 0:nt], ALU.subtract),
        wait=((c.sV, c.sV.n),), inc=c.sV)
    var_done = c.sV.n
    P.s(lambda e: e.activation(out=c.bbc[:, 0:nt], in_=c.bbc[:, 0:nt], func=AF.Ln,
                               bias=c.eps_t[:, :]), wait=((c.sV, var_done),), inc=c.sS)
    P.s(lambda e: e.activation(out=c.bbc[:, 0:nt], in_=c.bbc[:, 0:nt], func=AF.Exp,
                               scale=-0.5), wait=((c.sS, c.sS.n),), inc=c.sS)
    rstd_done = c.sS.n
    P.gate("vector", ((c.sS, rstd_done),))
    for pt in range(NPT):
        par = pt % 2
        if resid_aps is None:
            P.v(lambda e, pt=pt, par=par: e.tensor_tensor(c.tmp[par][:, 0:nt], x_aps[pt], c.abc[:, 0:nt],
                                                          ALU.subtract),
                wait=((c.sV, c.sV.n),), inc=c.sV)
            P.v(lambda e, pt=pt, par=par: e.tensor_tensor(out_aps[pt], c.tmp[par][:, 0:nt], c.bbc[:, 0:nt],
                                                          ALU.mult),
                wait=((c.sV, c.sV.n),), inc=c.sV)
        else:
            P.v(lambda e, pt=pt, par=par: e.tensor_tensor(c.tmp[par][:, 0:nt], x_aps[pt], c.abc[:, 0:nt],
                                                          ALU.subtract),
                wait=((c.sV, c.sV.n),), inc=c.sV)
            P.v(lambda e, pt=pt, par=par: e.tensor_tensor(c.tmp[par][:, 0:nt], c.tmp[par][:, 0:nt],
                                                          c.bbc[:, 0:nt], ALU.mult),
                wait=((c.sV, c.sV.n),), inc=c.sV)
            P.v(lambda e, pt=pt, par=par: e.tensor_tensor(resid_aps[pt], resid_aps[pt], c.tmp[par][:, 0:nt],
                                                          ALU.add),
                wait=((c.sV, c.sV.n),), inc=c.sV)
    return c.sV.n


def blk2d(t, d, off, p, n):
    return t[d:d + 1, off:off + p * n].rearrange("a (p t) -> (a p) t", p=p, t=n)


def barrier(P, c):
    P.gate_all(tuple((s, s.n) for s in (c.sT, c.sV, c.sS, c.dY, c.dG, c.sC)))


def emit_l0attn(P, c, nc, io, debug=False):
    xT = io["xT"]
    qkvw4 = io["qkvw4"]
    wvT = io["wvT"]
    qk_b = io["qk_b"]
    bvt = io["bvt"]
    owT = io["owT"]
    out_b = io["out_b"]
    CCSd = io["CCS"]
    SSGd = io["SSG"]
    xO = io["xb"]
    BLK = 3 * 128 * T
    QOFF, KOFF, VOFF = 0, 128 * T, 2 * 128 * T
    aain = nc.dram_tensor("aain", [NCORES, BLK], BF16).ap()
    aaout = nc.dram_tensor("aaout", [NCORES, BLK], BF16).ap()
    abin = [nc.dram_tensor(f"abin{b}", [NCORES, 128 * SC], BF16).ap() for b in range(B)]
    about = [nc.dram_tensor(f"about{b}", [NCORES, 128 * SC], BF16).ap() for b in range(B)]

    TC = 512
    NQC = S // TC
    NKT = S // 128
    sc0 = c.sC.n

    with contextlib.ExitStack() as stack:
        ec = stack.enter_context
        xc = [ec(nc.sbuf_tensor(f"A_xc{pt}", [128, TC], F32)) for pt in range(NPT)]
        yc = [ec(nc.sbuf_tensor(f"A_yc{pt}", [128, TC], BF16)) for pt in range(NPT)]
        wv_sb = [ec(nc.sbuf_tensor(f"A_wv{kt}", [128, 1024], BF16)) for kt in range(NPT)]
        outb_sb = ec(nc.sbuf_tensor("A_outb_sb", [128, 8], F32))
        hn = c.hs
        Kb = ec(nc.sbuf_tensor("A_Kb", [128, B * S], BF16))
        Vb = ec(nc.sbuf_tensor("A_Vb", [128, B * S], BF16))
        qsb = [ec(nc.sbuf_tensor(f"A_qsb{i}", [128, TC], BF16)) for i in range(2)]
        expb = [ec(nc.sbuf_tensor(f"A_expb{i}", [128, 1024], BF16)) for i in range(2)]
        exps = ec(nc.sbuf_tensor("A_exps", [128, 1024], BF16))
        astg = [ec(nc.sbuf_tensor(f"A_astg{i}", [128, TC], BF16)) for i in range(2)]
        denr = ec(nc.sbuf_tensor("A_denr", [1, 1024], BF16))
        denb_sb = ec(nc.sbuf_tensor("A_denb_sb", [128, TC], BF16))
        mt = [ec(nc.sbuf_tensor(f"A_mt{i}", [128, 1], F32)) for i in range(2)]
        nacc = ec(nc.sbuf_tensor("A_nacc", [1, 8], F32))
        qkbt = ec(nc.sbuf_tensor("A_qkbt", [128, 16], F32))
        bvt_sb = ec(nc.sbuf_tensor("A_bvt_sb", [128, 1], F32))
        attn_scope = contextlib.ExitStack()
        ea = attn_scope.enter_context
        wqb = [ea(nc.sbuf_tensor(f"A_wqb{i}", [128, 1024], BF16)) for i in range(2)]
        stg = [ea(nc.sbuf_tensor(f"A_stg{i}", [128, T], BF16)) for i in range(2)]
        swp = [ea(nc.sbuf_tensor(f"A_swp{i}", [128, T], BF16)) for i in range(2)]
        vstg = [ea(nc.sbuf_tensor(f"A_vstg{i}", [128, 1024], BF16)) for i in range(2)]
        ccs = ea(nc.sbuf_tensor("A_ccs", [128, T], BF16))
        ssg = ea(nc.sbuf_tensor("A_ssg", [128, T], BF16))

        P.g(lambda e: e.dma_start(out=ccs[:, :], in_=CCSd[:, :]), inc=c.dG)
        P.g(lambda e: e.dma_start(out=ssg[:, :], in_=SSGd[:, :]), inc=c.dG)
        for kt in range(NPT):
            P.g(lambda e, kt=kt: e.dma_start(out=wv_sb[kt][:, :], in_=wvT[kt * 128:(kt + 1) * 128, :]), inc=c.dG)
        P.y(lambda e: e.dma_start(out=qkbt[:, :], in_=qk_b[:, :]), inc=c.dY)
        P.y(lambda e: e.dma_start(out=bvt_sb[:, :], in_=bvt[:, :]), inc=c.dY)
        P.y(lambda e: e.dma_start(out=outb_sb[:, :], in_=out_b[:, :]), inc=c.dY)
        consts_g, consts_y = c.dG.n, c.dY.n

        prev_ln = 0
        ln_at = []
        for tcc in range(T // TC):
            sl = slice(tcc * TC, (tcc + 1) * TC)
            for pt in range(NPT):
                P.y(lambda e, pt=pt, sl=sl: e.dma_start(out=xc[pt][:, :], in_=xT[pt * 128:(pt + 1) * 128, sl]),
                    wait=((c.sV, prev_ln),), inc=c.dY)
            P.gate_all(((c.dY, c.dY.n), (c.dG, consts_g)), engines=("vector", "scalar", "tensor"))
            prev_ln = emit_ln(P, c, [xc[pt][:, :] for pt in range(NPT)], TC,
                              out_aps=[hn[pt][:, sl] for pt in range(NPT)])
            ln_at.append(prev_ln)
        hn_done = prev_ln

        wq_cons = [0, 0]
        stg_free = [0, 0]
        ev_done = {}
        for ot in range(16):
            par = ot % 2
            P.g(lambda e, ot=ot: e.dma_start(
                out=wqb[ot % 2][:, :].rearrange("p (k q) -> p k q", k=8, q=128),
                in_=qkvw4[ot:ot + 1, :, :, :].rearrange("a k p q -> p (a k) q")),
                wait=((c.sT, wq_cons[par]),), inc=c.dG)
            wq_ld = c.dG.n
            for ch in range(4):
                rg = (ot * 4 + ch) % 8
                for kt in range(NPT):
                    P.t(lambda e, ot=ot, ch=ch, kt=kt, rg=rg, par=par: (
                        e.ldweights(wqb[par][:, kt * 128:(kt + 1) * 128]),
                        e.matmul(c.reg(rg), wqb[par][:, kt * 128:(kt + 1) * 128],
                                 hn[kt][:, ch * 512:(ch + 1) * 512],
                                 start=(kt == 0), stop=(kt == NPT - 1)))[-1],
                        wait=(((c.dG, wq_ld), (c.sV, hn_done if rg >= 6 else ln_at[ch]),
                               (c.sV, ev_done.get(rg, 0)),
                               (c.dY, stg_free[par])) if kt == 0 else ()),
                        inc=c.sT)
                P.v(lambda e, ch=ch, rg=rg, par=par, ot=ot: e.tensor_scalar_add(
                    stg[par][:, ch * 512:(ch + 1) * 512], c.reg(rg), qkbt[:, ot:ot + 1]),
                    wait=((c.sT, c.sT.n),), inc=c.sV)
                ev_done[rg] = c.sV.n
            wq_cons[par] = c.sT.n
            evd = c.sV.n
            P.y(lambda e, par=par: e.dma_start(out=swp[par][0:32, :], in_=stg[par][32:64, :]),
                wait=((c.sV, evd),), inc=c.dY)
            P.y(lambda e, par=par: e.dma_start(out=swp[par][32:64, :], in_=stg[par][0:32, :]),
                inc=c.dY)
            P.y(lambda e, par=par: e.dma_start(out=swp[par][64:96, :], in_=stg[par][96:128, :]),
                inc=c.dY)
            P.y(lambda e, par=par: e.dma_start(out=swp[par][96:128, :], in_=stg[par][64:96, :]),
                inc=c.dY)
            swp_ld = c.dY.n
            P.v(lambda e, par=par: e.tensor_tensor(stg[par][:, :], stg[par][:, :], ccs[:, :], ALU.mult),
                wait=((c.dY, swp_ld),), inc=c.sV)
            P.v(lambda e, par=par: e.tensor_tensor(swp[par][:, :], swp[par][:, :], ssg[:, :], ALU.mult),
                inc=c.sV)
            P.v(lambda e, par=par: e.tensor_tensor(stg[par][:, :], stg[par][:, :], swp[par][:, :], ALU.add),
                wait=((c.sV, c.sV.n),), inc=c.sV)
            rope_done = c.sV.n
            dst, off = (ot, QOFF) if ot < 8 else (ot - 8, KOFF)
            P.y(lambda e, dst=dst, off=off, par=par: e.dma_start(
                out=blk2d(aain, dst, off, 128, T), in_=stg[par][:, :]),
                wait=((c.sV, rope_done),), inc=c.dY)
            stg_free[par] = c.dY.n

        vst_free = [0, 0]
        for tt in range(16):
            par = tt % 2
            for dvc in range(2):
                rg = (tt * 2 + dvc) % 8
                for kt in range(NPT):
                    P.t(lambda e, tt=tt, dvc=dvc, kt=kt, rg=rg: (
                        e.ldweights(hn[kt][:, tt * 128:(tt + 1) * 128]),
                        e.matmul(c.reg(rg), hn[kt][:, tt * 128:(tt + 1) * 128],
                                 wv_sb[kt][:, dvc * 512:(dvc + 1) * 512],
                                 start=(kt == 0), stop=(kt == NPT - 1)))[-1],
                        wait=(((c.sV, ev_done.get(rg, 0)), (c.dY, vst_free[par])) if kt == 0 else ()),
                        inc=c.sT)
                P.v(lambda e, dvc=dvc, rg=rg, par=par: e.tensor_copy(
                    vstg[par][:, dvc * 512:(dvc + 1) * 512], c.reg(rg)),
                    wait=((c.sT, c.sT.n),), inc=c.sV)
                ev_done[rg] = c.sV.n
            vev = c.sV.n
            for d in range(NCORES):
                P.y(lambda e, tt=tt, d=d, par=par: e.dma_start(
                    out=blk2d(aain, d, VOFF + tt * 128 * 128, 128, 128),
                    in_=vstg[par][:, d * 128:(d + 1) * 128]),
                    wait=((c.sV, vev),) if d == 0 else (), inc=c.dY)
            vst_free[par] = c.dY.n
        vproj_T = c.sT.n
        stage_done = c.dY.n

        for kt in range(NPT):
            P.g(lambda e, kt=kt: e.dma_start(out=wv_sb[kt][:, :], in_=owT[kt * 128:(kt + 1) * 128, :]),
                wait=((c.sT, vproj_T),) if kt == 0 else (), inc=c.dG)
        wo_ld = c.dG.n
        attn_scope.close()
        af_a = [ec(nc.sbuf_tensor(f"A_afa{pt}", [128, SC], BF16)) for pt in range(NPT)]
        af_b = [ec(nc.sbuf_tensor(f"A_afb{pt}", [128, SC], BF16)) for pt in range(NPT)]

        def cc1(e):
            with allow_ici_all_to_all():
                return e.collective_compute(
                    "AllToAll", ALU.bypass, replica_groups=[list(range(NCORES))],
                    ins=[aain[:, :]], outs=[aaout[:, :]])
        P.g(cc1, wait=((c.dY, stage_done),), inc=c.sC, inck=1)

        for src in range(NCORES):
            for b in range(B):
                P.y(lambda e, src=src, b=b: e.dma_start(
                    out=Kb[:, b * S + src * SC: b * S + (src + 1) * SC],
                    in_=blk2d(aaout, src, KOFF, 128, T)[:, b * SC:(b + 1) * SC]),
                    wait=((c.sC, sc0 + 1),), inc=c.dY)
                P.y(lambda e, src=src, b=b: e.dma_start(
                    out=Vb[:, (b * S + src * SC):(b * S + (src + 1) * SC)]
                        .rearrange("p (t c) -> p t c", t=SC // 128, c=128),
                    in_=aaout[src:src + 1, VOFF + b * SC * 128: VOFF + (b + 1) * SC * 128]
                        .rearrange("a (t p c) -> p (a t) c", t=SC // 128, p=128, c=128)),
                    wait=((c.sC, sc0 + 1),), inc=c.dY)
        kv_ld = c.dY.n

        P.v(lambda e: e.memset(nacc[:, :], 0.0), inc=c.sV)
        nacc0 = c.sV.n
        for b in range(B):
            for (acol, is_q) in ((1, False), (0, True)):
                for ch in range(16):
                    par = ch % 2
                    if is_q:
                        srcc = (ch * 512) // SC
                        o_in = (ch * 512) % SC
                        P.y(lambda e, b=b, srcc=srcc, o_in=o_in, par=par: e.dma_start(
                            out=qsb[par][:, :],
                            in_=blk2d(aaout, srcc, QOFF, 128, T)[:, b * SC + o_in: b * SC + o_in + 512]),
                            wait=((c.sC, sc0 + 1), (c.sS, c.sS.n)), inc=c.dY)
                        qld = c.dY.n
                        P.s(lambda e, par=par: e.activation(out=c.sq[par][:, :], in_=qsb[par][:, :],
                                                            func=AF.Square),
                            wait=((c.dY, qld), (c.sT, c.sT.n)), inc=c.sS)
                    else:
                        P.s(lambda e, b=b, ch=ch, par=par: e.activation(
                            out=c.sq[par][:, :], in_=Kb[:, b * S + ch * 512: b * S + (ch + 1) * 512],
                            func=AF.Square),
                            wait=((c.dY, kv_ld), (c.sT, c.sT.n)), inc=c.sS)
                    sqd = c.sS.n
                    P.t(lambda e, par=par: (
                        e.ldweights(c.ones_bf[:, 0:1]),
                        e.matmul(c.ps[2][0:1, :], c.ones_bf[:, 0:1],
                                 c.sq[par][:, :], start=True, stop=True,
                                 skip_group_check=True))[-1],
                        wait=((c.sS, sqd), (c.sV, c.sV.n)), inc=c.sT)
                    P.v(lambda e: e.tensor_reduce(
                        nacc[0:1, 6:7], c.ps[2][0:1, :], axis=mybir.AxisListType.X,
                        op=ALU.max), wait=((c.sT, c.sT.n),), inc=c.sV)
                    P.v(lambda e, acol=acol: e.tensor_tensor(
                        nacc[0:1, acol:acol + 1], nacc[0:1, acol:acol + 1], nacc[0:1, 6:7],
                        ALU.max), wait=((c.sV, c.sV.n), (c.sV, nacc0)), inc=c.sV)
            P.v(lambda e: e.tensor_copy(nacc[0:1, 4:5], nacc[0:1, 0:1]),
                wait=((c.sV, c.sV.n),), inc=c.sV)
            P.v(lambda e: e.tensor_copy(nacc[0:1, 5:6], nacc[0:1, 1:2]),
                wait=((c.sV, c.sV.n),), inc=c.sV)
            mx_done = c.sV.n
            P.s(lambda e: e.activation(out=nacc[0:1, 4:5], in_=nacc[0:1, 4:5], func=AF.Ln),
                wait=((c.sV, mx_done),), inc=c.sS)
            P.s(lambda e: e.activation(out=nacc[0:1, 5:6], in_=nacc[0:1, 5:6], func=AF.Ln),
                wait=((c.sS, c.sS.n),), inc=c.sS)
            P.v(lambda e: e.tensor_tensor(nacc[0:1, 4:5], nacc[0:1, 4:5], nacc[0:1, 5:6], ALU.add),
                wait=((c.sS, c.sS.n),), inc=c.sV)
            P.s(lambda e: e.activation(out=nacc[0:1, 4:5], in_=nacc[0:1, 4:5], func=AF.Exp, scale=0.5),
                wait=((c.sV, c.sV.n),), inc=c.sS)
            P.v(lambda e: e.tensor_scalar_mul(nacc[0:1, 4:5], nacc[0:1, 4:5], -0.125),
                wait=((c.sS, c.sS.n),), inc=c.sV)
            mrow = c.sV.n
            P.t(lambda e: e.matmul(c.ps[2][:, 0:1], c.ones_f[0:1, :], nacc[0:1, 4:5],
                                   start=True, stop=True),
                wait=((c.sV, mrow),), inc=c.sT)
            P.v(lambda e, b=b: e.tensor_copy(mt[b][:, :], c.ps[2][:, 0:1]),
                wait=((c.sT, c.sT.n),), inc=c.sV)
            P.v(lambda e: e.memset(nacc[0:1, 0:2], 0.0), wait=((c.sV, c.sV.n),), inc=c.sV)
        mt_done = c.sV.n

        phG_sS = c.sS.n
        abst = {}
        scores_end = {}
        av_evac_at = {}
        den_evac_at = {}
        recip_at = {}
        prev_exp_tail = [0, 0]
        prev_av_tail = [0, 0]
        prev_add_tail = [0, 0]
        prev_denq = 0
        DQ = 4
        pending_tail = [None]
        af_ld_part = [0, 0]

        def make_tail(gi, qpar, gpos, b, denq_last):
            def tail():
                P.v(lambda e: e.tensor_copy(denr[0:1, 0:512], c.ps[2][0:1, :]),
                    wait=((c.sT, denq_last), (c.sV, c.sV.n)), inc=c.sV)
                P.v(lambda e: e.tensor_copy(denr[0:1, 512:1024], c.ps[3][0:1, :]), inc=c.sV)
                recip_at[gi] = c.sV.n
                P.t(lambda e: (
                    e.ldweights(c.ones_bf[0:1, 0:64], tile_position=(0, 64)),
                    e.matmul(c.ps[2][64:128, :], c.ones_bf[0:1, 0:64], denr[0:1, 0:512],
                             start=True, stop=True, tile_position=(0, 64), skip_group_check=True))[-1],
                    wait=((c.sV, recip_at[gi]), (c.sV, den_evac_at.get(gi - 1, 0))), inc=c.sT)
                P.t(lambda e: (
                    e.ldweights(c.ones_bf[0:1, 0:64], tile_position=(0, 64)),
                    e.matmul(c.ps[3][64:128, :], c.ones_bf[0:1, 0:64], denr[0:1, 512:1024],
                             start=True, stop=True, tile_position=(0, 64), skip_group_check=True))[-1],
                    inc=c.sT)
                denb_T = c.sT.n
                P.v(lambda e: e.reciprocal(denb_sb[0:64, :], c.ps[2][64:128, :]),
                    wait=((c.sT, denb_T), (c.sV, c.sV.n)), inc=c.sV)
                P.v(lambda e: e.reciprocal(denb_sb[64:128, :], c.ps[3][64:128, :]), inc=c.sV)
                den_evac_at[gi] = c.sV.n
                P.v(lambda e: e.tensor_tensor(astg[qpar][:, :], c.ps[qpar][:, :],
                                              denb_sb[:, :], ALU.mult),
                    wait=((c.sV, c.sV.n), (c.dY, abst.get(gi - 2, 0))), inc=c.sV)
                av_evac_at[gi] = c.sV.n
                P.v(lambda e: e.tensor_scalar_add(astg[qpar][:, :], astg[qpar][:, :],
                                                  bvt_sb[:, :]),
                    wait=((c.sV, c.sV.n), (c.dY, consts_y)), inc=c.sV)
                a_done = c.sV.n
                dstc = gpos // SC
                o_in = gpos % SC
                P.y(lambda e, dstc=dstc, o_in=o_in, b=b: e.dma_start(
                    out=blk2d(abin[b], dstc, 0, 128, SC)[:, o_in:o_in + 512],
                    in_=astg[qpar][:, :]),
                    wait=((c.sV, a_done),), inc=c.dY)
                abst[gi] = c.dY.n
            return tail

        for b in range(B):
            for qc in range(NQC):
                gi = b * NQC + qc
                qpar = gi % 2
                gpos = qc * TC
                src = gpos // SC
                o_in = gpos % SC
                P.y(lambda e, src=src, o_in=o_in, b=b, qpar=qpar: e.dma_start(
                    out=qsb[qpar][:, :],
                    in_=blk2d(aaout, src, QOFF, 128, T)[:, b * SC + o_in: b * SC + o_in + 512]),
                    wait=((c.sT, scores_end.get(gi - 2, 0)), (c.sS, phG_sS if gi < 2 else 0)),
                    inc=c.dY)
                q_ld = c.dY.n
                exp_at = {-2: prev_exp_tail[0], -1: prev_exp_tail[1]}
                av_at = {-2: prev_av_tail[0], -1: prev_av_tail[1]}
                add_at = {-2: prev_add_tail[0], -1: prev_add_tail[1]}
                scores_at = {}
                exp0_at = {}
                denq_at = {-1: prev_denq}
                for step in range(NKT + 2):
                    if step < NKT:
                        kt = step
                        p = kt % 2
                        kcol = b * S + kt * 128
                        P.t(lambda e, kcol=kcol, p=p, qpar=qpar: (
                            e.ldweights(Kb[0:64, kcol:kcol + 128], tile_position=(0, 0)),
                            e.matmul(c.psBig[:, p * 1024:p * 1024 + 512], Kb[0:64, kcol:kcol + 128],
                                     qsb[qpar][0:64, :], start=True, stop=True, tile_position=(0, 0)))[-1],
                            wait=(((c.dY, q_ld), (c.sV, mt_done)) if kt == 0 else
                                  ((c.sS, exp_at.get(kt - 2, 0)),)),
                            inc=c.sT)
                        P.t(lambda e, kcol=kcol, p=p, qpar=qpar: (
                            e.ldweights(Kb[64:128, kcol:kcol + 128], tile_position=(64, 0)),
                            e.matmul(c.psBig[:, p * 1024 + 512:p * 1024 + 1024], Kb[64:128, kcol:kcol + 128],
                                     qsb[qpar][64:128, :], start=True, stop=True, tile_position=(64, 0),
                                     skip_group_check=True))[-1],
                            inc=c.sT)
                        scores_at[kt] = c.sT.n
                    if step == 2 and pending_tail[0] is not None:
                        pending_tail[0]()
                        pending_tail[0] = None
                        if gi == NQC:
                            ab0_done = c.dY.n

                            def cc2a(e):
                                with allow_ici_all_to_all():
                                    return e.collective_compute(
                                        "AllToAll", ALU.bypass, replica_groups=[list(range(NCORES))],
                                        ins=[abin[0][:, :]], outs=[about[0][:, :]])
                            P.g(cc2a, wait=((c.dY, ab0_done),), inc=c.sC, inck=1)
                    if step >= 2:
                        kt = step - 2
                        p = kt % 2
                        first, last = (kt == 0), (kt == NKT - 1)
                        vcol = (b * 64 + kt) * 128
                        P.t(lambda e, vcol=vcol, p=p, qpar=qpar, first=first, last=last: (
                            e.ldweights(Vb[:, vcol:vcol + 64], tile_position=(0, 0)),
                            e.matmul(c.ps[qpar][0:64, :], Vb[:, vcol:vcol + 64], expb[p][:, 0:512],
                                     start=first, stop=last, tile_position=(0, 0), skip_group_check=True))[-1],
                            wait=(((c.sS, exp0_at[kt]),) + (((c.sV, av_evac_at.get(gi - 2, 0)),)
                                                            if first else ())),
                            inc=c.sT)
                        P.t(lambda e, vcol=vcol, p=p, qpar=qpar, first=first, last=last: (
                            e.ldweights(Vb[:, vcol + 64:vcol + 128], tile_position=(0, 64)),
                            e.matmul(c.ps[qpar][64:128, :], Vb[:, vcol + 64:vcol + 128], expb[p][:, 512:1024],
                                     start=first, stop=last, tile_position=(0, 64), skip_group_check=True))[-1],
                            wait=((c.sS, exp_at[kt]),),
                            inc=c.sT)
                        av_at[kt] = c.sT.n
                        if kt % DQ == 0:
                            P.v(lambda e, p=p: e.tensor_copy(exps[:, :], expb[p][:, :]),
                                wait=((c.sS, exp_at[kt]), (c.sT, denq_at[kt // DQ - 1]),
                                      (c.sV, c.sV.n)), inc=c.sV)
                        else:
                            P.v(lambda e, p=p: e.tensor_tensor(exps[:, :], exps[:, :], expb[p][:, :],
                                                               ALU.add),
                                wait=((c.sS, exp_at[kt]), (c.sV, c.sV.n)), inc=c.sV)
                        add_at[kt] = c.sV.n
                        if kt % DQ == DQ - 1:
                            st, sp = (kt == DQ - 1), (kt == NKT - 1)
                            P.t(lambda e, st=st, sp=sp: (
                                e.ldweights(c.ones_bf[:, 0:1], tile_position=(0, 0)),
                                e.matmul(c.ps[2][0:1, :], c.ones_bf[:, 0:1], exps[:, 0:512],
                                         start=st, stop=sp, tile_position=(0, 0), skip_group_check=True),
                                e.matmul(c.ps[3][0:1, :], c.ones_bf[:, 0:1], exps[:, 512:1024],
                                         start=st, stop=sp, tile_position=(0, 0), skip_group_check=True))[-1],
                                wait=(((c.sV, add_at[kt]),) + (((c.sV, recip_at.get(gi - 1, 0)),)
                                                               if st else ())),
                                inc=c.sT)
                            denq_at[kt // DQ] = c.sT.n
                    if step < NKT:
                        kt = step
                        p = kt % 2
                        P.s(lambda e, p=p, b=b: e.activation(
                            out=expb[p][:, 0:512], in_=c.psBig[:, p * 1024:p * 1024 + 512], func=AF.Exp,
                            scale=0.125, bias=mt[b][:, :]),
                            wait=((c.sT, scores_at[kt]), (c.sT, av_at.get(kt - 2, 0)),
                                  (c.sV, add_at.get(kt - 2, 0))), inc=c.sS)
                        exp0_at[kt] = c.sS.n
                        P.s(lambda e, p=p, b=b: e.activation(
                            out=expb[p][:, 512:1024], in_=c.psBig[:, p * 1024 + 512:(p + 1) * 1024],
                            func=AF.Exp, scale=0.125, bias=mt[b][:, :]), inc=c.sS)
                        exp_at[kt] = c.sS.n
                prev_exp_tail = [exp_at[NKT - 2], exp_at[NKT - 1]]
                prev_av_tail = [av_at[NKT - 2], av_at[NKT - 1]]
                prev_add_tail = [add_at[NKT - 2], add_at[NKT - 1]]
                prev_denq = denq_at[NKT // DQ - 1]
                scores_end[gi] = scores_at[NKT - 1]
                pending_tail[0] = make_tail(gi, qpar, gpos, b, prev_denq)
        pending_tail[0]()
        pending_tail[0] = None
        ab_stage_done = c.dY.n

        def cc2b(e):
            with allow_ici_all_to_all():
                return e.collective_compute(
                    "AllToAll", ALU.bypass, replica_groups=[list(range(NCORES))],
                    ins=[abin[1][:, :]], outs=[about[1][:, :]])
        P.g(cc2b, wait=((c.dY, ab_stage_done),), inc=c.sC, inck=1)
        for srcc in range(NCORES):
            P.g(lambda e, srcc=srcc: e.dma_start(
                out=af_a[srcc][:, :], in_=blk2d(about[0], srcc, 0, 128, SC)),
                wait=((c.sC, sc0 + 2),) if srcc == 0 else (), inc=c.dG)
        af_ld_part[0] = c.dG.n
        for srcc in range(NCORES):
            P.g(lambda e, srcc=srcc: e.dma_start(
                out=af_b[srcc][:, :], in_=blk2d(about[1], srcc, 0, 128, SC)),
                wait=((c.sC, sc0 + 3),) if srcc == 0 else (), inc=c.dG)
        af_ld_part[1] = c.dG.n

        P.gate("tensor", ((c.sV, c.sV.n), (c.sS, c.sS.n)))
        prev_res = hn_done
        x_st = 0
        for tcc in range(T // TC):
            sl = slice(tcc * TC, (tcc + 1) * TC)
            for pt in range(NPT):
                P.y(lambda e, pt=pt, sl=sl: e.dma_start(out=xc[pt][:, :], in_=xT[pt * 128:(pt + 1) * 128, sl]),
                    wait=((c.sV, prev_res), (c.dY, x_st)), inc=c.dY)
            x_ld = c.dY.n
            afh = af_a if tcc < 2 else af_b
            asl = slice(sl.start - (0 if tcc < 2 else SC), sl.stop - (0 if tcc < 2 else SC))
            af_ok = af_ld_part[0 if tcc < 2 else 1]
            for ot in range(NPT):
                rg = ot % 6
                for kt in range(NPT):
                    P.t(lambda e, ot=ot, kt=kt, rg=rg, afh=afh, asl=asl: (
                        e.ldweights(wv_sb[kt][:, ot * 128:(ot + 1) * 128]),
                        e.matmul(c.reg(rg), wv_sb[kt][:, ot * 128:(ot + 1) * 128], afh[kt][:, asl],
                                 start=(kt == 0), stop=(kt == NPT - 1)))[-1],
                        wait=(((c.dG, wo_ld), (c.dG, af_ok), (c.sV, ev_done.get(rg, 0))) if kt == 0 else ()),
                        inc=c.sT)
                P.v(lambda e, ot=ot, rg=rg: e.tensor_scalar_add(yc[ot][:, :], c.reg(rg), outb_sb[:, ot:ot + 1]),
                    wait=((c.sT, c.sT.n),), inc=c.sV)
                ev_done[rg] = c.sV.n
            P.gate_all(((c.sV, c.sV.n), (c.dY, x_ld)), engines=("vector", "scalar", "tensor"))
            resid_done = emit_ln(P, c, [yc[pt][:, :] for pt in range(NPT)], TC,
                                 resid_aps=[xc[pt][:, :] for pt in range(NPT)])
            for pt in range(NPT):
                P.y(lambda e, pt=pt, sl=sl: e.dma_start(out=xO[pt * 128:(pt + 1) * 128, sl], in_=xc[pt][:, :]),
                    wait=((c.sV, resid_done),), inc=c.dY)
            x_st = c.dY.n
            prev_res = emit_ln(P, c, [xc[pt][:, :] for pt in range(NPT)], TC,
                               out_aps=[hn[pt][:, sl] for pt in range(NPT)])
        P.wait_only("sync", ((c.dY, c.dY.n),))


def emit_l0ffn(P, c, nc, io):
    xT = io["xb"]
    w1 = io["w1"]
    b1 = io["b1"]
    w2 = io["w2"]
    b2 = io["b2"]
    xO = io["xb"]
    TC = 512
    NH = FF // 128

    with contextlib.ExitStack() as stack:
        ec = stack.enter_context
        xc = [ec(nc.sbuf_tensor(f"F_xc{pt}", [128, TC], F32)) for pt in range(NPT)]
        yc = [ec(nc.sbuf_tensor(f"F_yc{pt}", [128, TC], BF16)) for pt in range(NPT)]
        g_sb = [ec(nc.sbuf_tensor(f"F_g{kt}", [128, TC], BF16)) for kt in range(NH)]
        w1b = [ec(nc.sbuf_tensor(f"F_w1b{pt}", [128, FF], BF16)) for pt in range(NPT)]
        w2b = [ec(nc.sbuf_tensor(f"F_w2b{i}", [128, D], BF16)) for i in range(16)]
        b1t = ec(nc.sbuf_tensor("F_b1t", [128, 32], F32))
        b2t = ec(nc.sbuf_tensor("F_b2t", [128, 8], F32))

        for pt in range(NPT):
            P.g(lambda e, pt=pt: e.dma_start(out=w1b[pt][:, :], in_=w1[pt * 128:(pt + 1) * 128, :]), inc=c.dG)
        P.y(lambda e: e.dma_start(out=b1t[:, :], in_=b1[:, :]), inc=c.dY)
        P.y(lambda e: e.dma_start(out=b2t[:, :], in_=b2[:, :]), inc=c.dY)
        w1_loaded = c.dG.n
        const_loaded = c.dY.n
        P.gate("scalar", ((c.dY, const_loaded),))
        P.gate("vector", ((c.dY, const_loaded),))

        x_stored = [0] * (T // TC)
        prev_resid_done = 0
        prev_lnps = 0
        w2_cons = 0
        for tc in range(T // TC):
            sl = slice(tc * TC, (tc + 1) * TC)
            prev_store = x_stored[tc - 1] if tc > 0 else 0
            for pt in range(NPT):
                P.y(lambda e, pt=pt, sl=sl: e.dma_start(out=xc[pt][:, :], in_=xT[pt * 128:(pt + 1) * 128, sl]),
                    wait=((c.sV, prev_resid_done), (c.dY, prev_store)), inc=c.dY)
            x_ld = c.dY.n
            evac_done_at = [0] * 8
            for ot in range(NH):
                rg = ot % 6
                for kt in range(NPT):
                    P.t(lambda e, ot=ot, kt=kt, rg=rg, sl=sl: (
                        e.ldweights(w1b[kt][:, ot * 128:(ot + 1) * 128]),
                        e.matmul(c.reg(rg), w1b[kt][:, ot * 128:(ot + 1) * 128], c.hs[kt][:, sl],
                                 start=(kt == 0), stop=(kt == NPT - 1)))[-1],
                        wait=(((c.dG, w1_loaded), (c.sS, evac_done_at[rg]))
                              if kt == 0 else ()),
                        inc=c.sT)
                mm_done = c.sT.n
                P.s(lambda e, ot=ot, rg=rg: e.activation(out=g_sb[ot][:, :], in_=c.reg(rg), func=AF.Gelu,
                                                         bias=b1t[:, ot:ot + 1]),
                    wait=((c.sT, mm_done),), inc=c.sS)
                evac_done_at[rg] = c.sS.n
            g_done = c.sS.n
            yev = 0
            for half in range(2):
                for i in range(16):
                    P.g(lambda e, half=half, i=i: e.dma_start(
                        out=w2b[i][:, :], in_=w2[(half * 16 + i) * 128:(half * 16 + i + 1) * 128, :]),
                        wait=((c.sT, w2_cons),) if i == 0 else (), inc=c.dG)
                w2_ld = c.dG.n
                for ot in range(NPT):
                    for i in range(16):
                        kt = half * 16 + i
                        P.t(lambda e, ot=ot, i=i, kt=kt: (
                            e.ldweights(w2b[i][:, ot * 128:(ot + 1) * 128]),
                            e.matmul(c.reg(ot), w2b[i][:, ot * 128:(ot + 1) * 128], g_sb[kt][:, :],
                                     start=(kt == 0), stop=(kt == NH - 1)))[-1],
                            wait=(((c.dG, w2_ld), (c.sS, g_done)) +
                                  (((c.sV, prev_lnps),) if (kt == 0 and ot >= 6) else ())
                                  if i == 0 else ()),
                            inc=c.sT)
                w2_cons = c.sT.n
            mm2_done = c.sT.n
            for ot in range(NPT):
                P.v(lambda e, ot=ot: e.tensor_scalar_add(yc[ot][:, :], c.reg(ot), b2t[:, ot:ot + 1]),
                    wait=((c.sT, mm2_done),), inc=c.sV)
            y_done_v = c.sV.n
            P.gate_all(((c.sV, y_done_v), (c.dY, x_ld)), engines=("vector", "tensor", "scalar"))
            resid_done = emit_ln(P, c, [yc[pt][:, :] for pt in range(NPT)], TC,
                                 resid_aps=[xc[pt][:, :] for pt in range(NPT)])
            for pt in range(NPT):
                P.y(lambda e, pt=pt, sl=sl: e.dma_start(out=xO[pt * 128:(pt + 1) * 128, sl], in_=xc[pt][:, :]),
                    wait=((c.sV, resid_done),), inc=c.dY)
            x_stored[tc] = c.dY.n
            prev_resid_done = emit_ln(P, c, [xc[pt][:, :] for pt in range(NPT)], TC,
                                      out_aps=[c.hs[pt][:, sl] for pt in range(NPT)])
            prev_lnps = prev_resid_done
        P.wait_only("sync", ((c.dY, c.dY.n),))


def wslice3(wT, ot):
    return wT[:, ot * 128:(ot + 1) * 128].rearrange("(k p) q -> p k q", k=8, p=128)


def emit_l1attn(P, c, nc, io):
    xT = io["xb"]
    wq = io["wq"]
    wk = io["wk"]
    wv = io["wv"]
    wo = io["wo"]
    hseld = io["hsel"]
    hbcd = io["hbc"]
    xO = io["xb"]
    TC = 512
    HT = T // 2

    with contextlib.ExitStack() as stack:
        ec = stack.enter_context
        xc = [ec(nc.sbuf_tensor(f"C_xc{pt}", [128, TC], F32)) for pt in range(NPT)]
        hn = c.hs
        bcol = ec(nc.sbuf_tensor("C_bcol", [128, 16], F32))
        v_sb = [ec(nc.sbuf_tensor(f"C_v{pt}", [128, T], BF16)) for pt in range(NPT)]
        yc = [ec(nc.sbuf_tensor(f"C_yc{pt}", [128, TC], BF16)) for pt in range(NPT)]
        wb = [ec(nc.sbuf_tensor(f"C_wb{i}", [128, 1024], BF16)) for i in range(2)]
        hsel_sb = ec(nc.sbuf_tensor("C_hsel_sb", [128, 128], BF16))
        hbc_sb = ec(nc.sbuf_tensor("C_hbc_sb", [16, 1024], BF16))
        prod = [ec(nc.sbuf_tensor(f"C_prod{i}", [128, HT], BF16)) for i in range(2)]
        s_sb = [ec(nc.sbuf_tensor(f"C_s{i}", [16, HT], BF16)) for i in range(4)]
        z_sb = [ec(nc.sbuf_tensor(f"C_z{i}", [16, HT], F32)) for i in range(2)]
        w_sb = [ec(nc.sbuf_tensor(f"C_wl{i}", [16, HT], BF16)) for i in range(4)]
        l1t = [ec(nc.sbuf_tensor(f"C_l1t{i}", [128, HT], BF16)) for i in range(2)]
        qk_scope = contextlib.ExitStack()
        eq = qk_scope.enter_context
        q_sb = [eq(nc.sbuf_tensor(f"C_q{pt}", [128, T], BF16)) for pt in range(NPT)]
        k_sb = [eq(nc.sbuf_tensor(f"C_k{pt}", [128, T], BF16)) for pt in range(NPT)]

        P.g(lambda e: e.dma_start(out=hsel_sb[:, :], in_=hseld[:, :]), inc=c.dG)
        P.g(lambda e: e.dma_start(out=hbc_sb[:, :], in_=hbcd[:, :]), inc=c.dG)
        consts_g = c.dG.n
        P.gate("tensor", ((c.dG, consts_g),))

        evS = {}
        evV = {}
        w_cons = [0, 0]
        for (wd, dst, actf) in ((wq, q_sb, AF.Relu), (wk, k_sb, AF.Relu), (wv, v_sb, None)):
            for ot in range(NPT):
                par = ot % 2
                P.g(lambda e, wd=wd, ot=ot, par=par: e.dma_start(
                    out=wb[par][:, :].rearrange("p (k q) -> p k q", k=8, q=128),
                    in_=wslice3(wd, ot)),
                    wait=((c.sT, w_cons[par]),), inc=c.dG)
                w_ld = c.dG.n
                for ch in range(4):
                    rg = (ot * 4 + ch) % 8
                    for kt in range(NPT):
                        P.t(lambda e, ot=ot, ch=ch, kt=kt, rg=rg, par=par: (
                            e.ldweights(wb[par][:, kt * 128:(kt + 1) * 128]),
                            e.matmul(c.reg(rg), wb[par][:, kt * 128:(kt + 1) * 128],
                                     hn[kt][:, ch * 512:(ch + 1) * 512],
                                     start=(kt == 0), stop=(kt == NPT - 1)))[-1],
                            wait=(((c.dG, w_ld),
                                   (c.sS, evS.get(rg, 0)), (c.sV, evV.get(rg, 0))) if kt == 0 else ()),
                            inc=c.sT)
                    if actf is not None:
                        P.s(lambda e, dst=dst, ot=ot, ch=ch, rg=rg, actf=actf: e.activation(
                            out=dst[ot][:, ch * 512:(ch + 1) * 512], in_=c.reg(rg), func=actf),
                            wait=((c.sT, c.sT.n),), inc=c.sS)
                        evS[rg] = c.sS.n
                    else:
                        P.v(lambda e, dst=dst, ot=ot, ch=ch, rg=rg: e.tensor_copy(
                            dst[ot][:, ch * 512:(ch + 1) * 512], c.reg(rg)),
                            wait=((c.sT, c.sT.n),), inc=c.sV)
                        evV[rg] = c.sV.n
                w_cons[par] = c.sT.n
        proj_done_v = c.sV.n
        proj_done_s = c.sS.n

        P.gate_all(((c.sV, proj_done_v), (c.sS, proj_done_s)), engines=("vector", "tensor"))
        combos = [(0, 0), (0, 1), (1, 0), (1, 1)]
        for ci, (l, lp) in enumerate(combos):
            for pt in range(NPT):
                par = pt % 2
                P.v(lambda e, pt=pt, par=par, l=l, lp=lp: e.tensor_tensor(
                    prod[par][:, :], q_sb[pt][:, l * HT:(l + 1) * HT],
                    k_sb[pt][:, lp * HT:(lp + 1) * HT], ALU.mult),
                    wait=((c.sT, evS.get(("p", par), 0)), (c.sV, c.sV.n)), inc=c.sV)
                pd = c.sV.n
                for ch in range(2):
                    P.t(lambda e, pt=pt, par=par, ch=ch: (
                        e.ldweights(hsel_sb[:, pt * 16:pt * 16 + 16]),
                        e.matmul(c.psBig[0:16, ch * 512:(ch + 1) * 512],
                                 hsel_sb[:, pt * 16:pt * 16 + 16],
                                 prod[par][:, ch * 512:(ch + 1) * 512],
                                 start=(pt == 0), stop=(pt == NPT - 1), skip_group_check=True))[-1],
                        wait=((c.sV, pd),), inc=c.sT)
                evS[("p", par)] = c.sT.n
            P.v(lambda e, ci=ci: e.tensor_copy(s_sb[ci][:, :], c.psBig[0:16, 0:HT]),
                wait=((c.sT, c.sT.n),), inc=c.sV)
        s_done = c.sV.n
        qk_scope.close()
        wo_sb = [ec(nc.sbuf_tensor(f"C_wo{kt}", [128, 1024], BF16)) for kt in range(NPT)]
        for kt in range(NPT):
            P.g(lambda e, kt=kt: e.dma_start(out=wo_sb[kt][:, :], in_=wo[kt * 128:(kt + 1) * 128, :]),
                wait=((c.sV, s_done),) if kt == 0 else (), inc=c.dG)
        wo_res = c.dG.n

        P.v(lambda e: e.scalar_tensor_tensor(z_sb[0][:, :], s_sb[1][:, :], C01, s_sb[0][:, :],
                                             ALU.mult, ALU.add),
            wait=((c.sV, s_done),), inc=c.sV)
        P.v(lambda e: e.scalar_tensor_tensor(z_sb[1][:, :], s_sb[2][:, :], C01, s_sb[3][:, :],
                                             ALU.mult, ALU.add),
            wait=((c.sV, c.sV.n),), inc=c.sV)
        P.v(lambda e: e.tensor_scalar_max(z_sb[0][:, :], z_sb[0][:, :], EPS_L1),
            wait=((c.sV, c.sV.n),), inc=c.sV)
        P.v(lambda e: e.tensor_scalar_max(z_sb[1][:, :], z_sb[1][:, :], EPS_L1),
            wait=((c.sV, c.sV.n),), inc=c.sV)
        P.v(lambda e: e.reciprocal(z_sb[0][:, :], z_sb[0][:, :]), wait=((c.sV, c.sV.n),), inc=c.sV)
        P.v(lambda e: e.reciprocal(z_sb[1][:, :], z_sb[1][:, :]), wait=((c.sV, c.sV.n),), inc=c.sV)
        z_done = c.sV.n
        P.v(lambda e: e.tensor_tensor(w_sb[0][:, :], s_sb[0][:, :], z_sb[0][:, :], ALU.mult),
            wait=((c.sV, z_done),), inc=c.sV)
        P.v(lambda e: e.tensor_tensor(w_sb[1][:, :], s_sb[1][:, :], z_sb[0][:, :], ALU.mult),
            inc=c.sV)
        P.v(lambda e: e.tensor_tensor(w_sb[2][:, :], s_sb[2][:, :], z_sb[1][:, :], ALU.mult),
            inc=c.sV)
        P.v(lambda e: e.tensor_tensor(w_sb[3][:, :], s_sb[3][:, :], z_sb[1][:, :], ALU.mult),
            inc=c.sV)
        P.v(lambda e: e.tensor_scalar_mul(w_sb[1][:, :], w_sb[1][:, :], C01),
            wait=((c.sV, c.sV.n),), inc=c.sV)
        P.v(lambda e: e.tensor_scalar_mul(w_sb[2][:, :], w_sb[2][:, :], C01),
            wait=((c.sV, c.sV.n),), inc=c.sV)
        w_done = c.sV.n

        P.gate("tensor", ((c.sV, w_done),))
        wbmm = 0
        for pt in range(NPT):
            for l in range(2):
                for lp in range(2):
                    wi = l * 2 + lp
                    for ch in range(2):
                        P.t(lambda e, pt=pt, lp=lp, wi=wi, ch=ch: (
                            e.ldweights(hbc_sb[:, pt * 128:(pt + 1) * 128]),
                            e.matmul(c.psBig[:, lp * 1024 + ch * 512: lp * 1024 + (ch + 1) * 512],
                                     hbc_sb[:, pt * 128:(pt + 1) * 128],
                                     w_sb[wi][:, ch * 512:(ch + 1) * 512],
                                     start=True, stop=True, skip_group_check=True))[-1],
                            wait=((c.sV, wbmm),), inc=c.sT)
                wb_T = c.sT.n
                P.v(lambda e, pt=pt, l=l: e.tensor_tensor(
                    l1t[0][:, :], c.psBig[:, 0:1024], v_sb[pt][:, 0:HT], ALU.mult),
                    wait=((c.sT, wb_T), (c.sV, c.sV.n)), inc=c.sV)
                P.v(lambda e, pt=pt, l=l: e.tensor_tensor(
                    l1t[1][:, :], c.psBig[:, 1024:2048], v_sb[pt][:, HT:T], ALU.mult),
                    wait=((c.sV, c.sV.n),), inc=c.sV)
                P.v(lambda e, pt=pt, l=l: e.tensor_tensor(
                    hn[pt][:, l * HT:(l + 1) * HT], l1t[0][:, :], l1t[1][:, :], ALU.add),
                    wait=((c.sV, c.sV.n),), inc=c.sV)
                wbmm = c.sV.n
        attn_done = c.sV.n

        aab_in = io["aab_in"]
        aab_out = io["aab_out"]
        P.gate("tensor", ((c.sV, attn_done),))
        prev_res = 0
        x_st = 0
        for ci, tcc in enumerate((1, 3, 0, 2)):
            sl = slice(tcc * TC, (tcc + 1) * TC)
            for pt in range(NPT):
                P.y(lambda e, pt=pt, sl=sl: e.dma_start(out=xc[pt][:, :], in_=xT[pt * 128:(pt + 1) * 128, sl]),
                    wait=((c.sV, prev_res), (c.dY, x_st)), inc=c.dY)
            x_ld = c.dY.n
            for ot in range(NPT):
                rg = ot % 6
                for kt in range(NPT):
                    P.t(lambda e, ot=ot, kt=kt, rg=rg, sl=sl: (
                        e.ldweights(wo_sb[kt][:, ot * 128:(ot + 1) * 128]),
                        e.matmul(c.reg(rg), wo_sb[kt][:, ot * 128:(ot + 1) * 128], hn[kt][:, sl],
                                 start=(kt == 0), stop=(kt == NPT - 1)))[-1],
                        wait=(((c.dG, wo_res), (c.sV, evV.get(("o", rg), 0))) if kt == 0 else ()),
                        inc=c.sT)
                P.v(lambda e, ot=ot, rg=rg: e.tensor_copy(yc[ot][:, :], c.reg(rg)),
                    wait=((c.sT, c.sT.n),), inc=c.sV)
                evV[("o", rg)] = c.sV.n
            P.gate_all(((c.sV, c.sV.n), (c.dY, x_ld)), engines=("vector", "scalar", "tensor"))
            resid_done = emit_ln(P, c, [yc[pt][:, :] for pt in range(NPT)], TC,
                                 resid_aps=[xc[pt][:, :] for pt in range(NPT)])
            for pt in range(NPT):
                P.y(lambda e, pt=pt, sl=sl: e.dma_start(out=xO[pt * 128:(pt + 1) * 128, sl], in_=xc[pt][:, :]),
                    wait=((c.sV, resid_done),), inc=c.dY)
            x_st = c.dY.n
            prev_res = emit_ln(P, c, [xc[pt][:, :] for pt in range(NPT)], TC,
                               out_aps=[hn[pt][:, sl] for pt in range(NPT)])
            if ci == 1:
                for pt in range(NPT):
                    for b in range(B):
                        col = pt * B + b
                        P.v(lambda e, pt=pt, b=b, col=col: e.tensor_copy(
                            bcol[:, col:col + 1], hn[pt][:, b * SC + SC - 1:b * SC + SC]),
                            wait=((c.sV, prev_res),) if col == 0 else (), inc=c.sV)
                bcol_done = c.sV.n
                for d in range(NCORES):
                    P.y(lambda e, d=d: e.dma_start(out=blk2d(aab_in, d, 0, 128, 16), in_=bcol[:, :]),
                        wait=((c.sV, bcol_done),) if d == 0 else (), inc=c.dY)
                ab_staged = c.dY.n

                def cc3(e):
                    with allow_ici_all_to_all():
                        return e.collective_compute(
                            "AllToAll", ALU.bypass, replica_groups=[list(range(NCORES))],
                            ins=[aab_in[:, :]], outs=[aab_out[:, :]])
                P.g(cc3, wait=((c.dY, ab_staged),), inc=c.sC, inck=1)
        P.wait_only("sync", ((c.dY, c.dY.n),))


def emit_l1ffn(P, c, nc, io):
    xT = io["xb"]
    wr = io["wr"]
    wkk = io["wkk"]
    wvv = io["wvv"]
    mur = io["mur"]
    muk = io["muk"]
    bmaskd = io["bmask"]
    xO = io["xO"]
    aab_out = io["aab_out"]
    TC = 512
    cc3_at = c.sC.n

    with contextlib.ExitStack() as stack:
        ec = stack.enter_context
        xc = [ec(nc.sbuf_tensor(f"R_xc{pt}", [128, TC], F32)) for pt in range(NPT)]
        hn = c.hs
        hp = [ec(nc.sbuf_tensor(f"R_hp{pt}", [128, T], BF16)) for pt in range(NPT)]
        xr = [ec(nc.sbuf_tensor(f"R_xr{pt}", [128, T], BF16)) for pt in range(NPT)]
        r_sb = [ec(nc.sbuf_tensor(f"R_r{pt}", [128, T], BF16)) for pt in range(NPT)]
        yc = [ec(nc.sbuf_tensor(f"R_yc{pt}", [128, TC], BF16)) for pt in range(NPT)]
        wb = [ec(nc.sbuf_tensor(f"R_wb{i}", [128, 1024], BF16)) for i in range(2)]
        murt = ec(nc.sbuf_tensor("R_murt", [128, 8], F32))
        mukt = ec(nc.sbuf_tensor("R_mukt", [128, 8], F32))
        dtb = [ec(nc.sbuf_tensor(f"R_dtb{i}", [128, T], BF16)) for i in range(1)]
        wvv_sb = [ec(nc.sbuf_tensor(f"R_wvv{kt}", [128, 1024], BF16)) for kt in range(NPT)]
        bndall = ec(nc.sbuf_tensor("R_bndall", [128, 128], F32))
        hpb = ec(nc.sbuf_tensor("R_hpb", [128, 16], F32))
        bmask_sb = ec(nc.sbuf_tensor("R_bmask_sb", [128, 128], F32))

        P.y(lambda e: e.dma_start(out=murt[:, :], in_=mur[:, :]), inc=c.dY)
        P.y(lambda e: e.dma_start(out=mukt[:, :], in_=muk[:, :]), inc=c.dY)
        P.y(lambda e: e.dma_start(out=bmask_sb[:, :], in_=bmaskd[:, :]), inc=c.dY)
        consts_y = c.dY.n

        for pt in range(NPT):
            for b in range(B):
                P.y(lambda e, pt=pt, b=b: e.dma_start(
                    out=hp[pt][:, b * SC + 1:(b + 1) * SC], in_=hn[pt][:, b * SC:(b + 1) * SC - 1]),
                    inc=c.dY)
        hp_shift = c.dY.n
        for s in range(NCORES):
            P.y(lambda e, s=s: e.dma_start(out=bndall[:, s * 16:(s + 1) * 16],
                                           in_=blk2d(aab_out, s, 0, 128, 16)),
                wait=((c.sC, cc3_at),) if s == 0 else (), inc=c.dY)
        bnd_ld = c.dY.n
        P.v(lambda e: e.tensor_tensor(bndall[:, :], bndall[:, :], bmask_sb[:, :], ALU.mult),
            wait=((c.dY, bnd_ld), (c.dY, consts_y), (c.sV, c.sV.n)), inc=c.sV)
        P.v(lambda e: e.tensor_copy(hpb[:, :], bndall[:, 0:16]), wait=((c.sV, c.sV.n),), inc=c.sV)
        for s in range(1, NCORES):
            P.v(lambda e, s=s: e.tensor_tensor(hpb[:, :], hpb[:, :], bndall[:, s * 16:(s + 1) * 16],
                                               ALU.add),
                wait=((c.sV, c.sV.n),), inc=c.sV)
        hpb_done = c.sV.n
        for pt in range(NPT):
            for b in range(B):
                col = pt * B + b
                P.v(lambda e, pt=pt, b=b, col=col: e.tensor_copy(
                    hp[pt][:, b * SC:b * SC + 1], hpb[:, col:col + 1]),
                    wait=((c.sV, hpb_done), (c.dY, hp_shift)) if col == 0 else (), inc=c.sV)
        hp_done = c.sV.n

        for pt in range(NPT):
            P.v(lambda e, pt=pt: e.tensor_sub(dtb[0][:, :], hn[pt][:, :], hp[pt][:, :]),
                wait=((c.sV, c.sV.n), (c.sV, hp_done), (c.dY, hp_shift)), inc=c.sV)
            P.v(lambda e, pt=pt: e.scalar_tensor_tensor(
                xr[pt][:, :], dtb[0][:, :], murt[:, pt:pt + 1], hp[pt][:, :], ALU.mult, ALU.add),
                wait=((c.sV, c.sV.n), (c.dY, consts_y)), inc=c.sV)
        xr_done = c.sV.n

        ev_done = {}
        w_cons = [0, 0]

        def proj(P, wd, rhs_tiles, rhs_ready, evac, name):
            for ot in range(NPT):
                par = ot % 2
                P.g(lambda e, wd=wd, ot=ot, par=par: e.dma_start(
                    out=wb[par][:, :].rearrange("p (k q) -> p k q", k=8, q=128),
                    in_=wslice3(wd, ot)),
                    wait=((c.sT, w_cons[par]),), inc=c.dG)
                w_ld = c.dG.n
                for ch in range(4):
                    rg = (ot * 4 + ch) % 8
                    for kt in range(NPT):
                        P.t(lambda e, ot=ot, ch=ch, kt=kt, rg=rg, par=par: (
                            e.ldweights(wb[par][:, kt * 128:(kt + 1) * 128]),
                            e.matmul(c.reg(rg), wb[par][:, kt * 128:(kt + 1) * 128],
                                     rhs_tiles[kt][:, ch * 512:(ch + 1) * 512],
                                     start=(kt == 0), stop=(kt == NPT - 1)))[-1],
                            wait=(((c.dG, w_ld), rhs_ready, (ev_done.get(rg, (c.sV, 0)))) if kt == 0 else ()),
                            inc=c.sT)
                    evac(ot, ch, rg)
                w_cons[par] = c.sT.n

        def evac_sig(ot, ch, rg):
            P.s(lambda e, ot=ot, ch=ch, rg=rg: e.activation(
                out=r_sb[ot][:, ch * 512:(ch + 1) * 512], in_=c.reg(rg), func=AF.Sigmoid),
                wait=((c.sT, c.sT.n),), inc=c.sS)
            ev_done[rg] = (c.sS, c.sS.n)
        proj(P, wr, xr, (c.sV, xr_done), evac_sig, "r")
        r_done_s = c.sS.n

        P.gate("vector", ((c.sT, c.sT.n),))
        for pt in range(NPT):
            P.v(lambda e, pt=pt: e.tensor_sub(dtb[0][:, :], hn[pt][:, :], hp[pt][:, :]),
                wait=((c.sV, c.sV.n),), inc=c.sV)
            P.v(lambda e, pt=pt: e.scalar_tensor_tensor(
                xr[pt][:, :], dtb[0][:, :], mukt[:, pt:pt + 1], hp[pt][:, :], ALU.mult, ALU.add),
                wait=((c.sV, c.sV.n),), inc=c.sV)
        xk_done = c.sV.n

        def evac_relu2(ot, ch, rg):
            P.s(lambda e, ot=ot, ch=ch, rg=rg: e.activation(
                out=hp[ot][:, ch * 512:(ch + 1) * 512], in_=c.reg(rg), func=AF.Relu),
                wait=((c.sT, c.sT.n), (c.sV, xk_done)), inc=c.sS)
            P.s(lambda e, ot=ot, ch=ch: e.activation(
                out=hp[ot][:, ch * 512:(ch + 1) * 512], in_=hp[ot][:, ch * 512:(ch + 1) * 512],
                func=AF.Square),
                wait=((c.sS, c.sS.n),), inc=c.sS)
            ev_done[rg] = (c.sS, c.sS.n)
        P.gate("tensor", ((c.sV, xk_done),))
        proj(P, wkk, xr, (c.sV, xk_done), evac_relu2, "ka")

        for kt in range(NPT):
            P.g(lambda e, kt=kt: e.dma_start(out=wvv_sb[kt][:, :], in_=wvv[kt * 128:(kt + 1) * 128, :]),
                inc=c.dG)
        wvv_res = c.dG.n
        P.gate("tensor", ((c.sS, c.sS.n),))
        prev_res = 0
        x_st = 0
        for tcc in range(T // TC):
            sl = slice(tcc * TC, (tcc + 1) * TC)
            for pt in range(NPT):
                P.y(lambda e, pt=pt, sl=sl: e.dma_start(out=xc[pt][:, :], in_=xT[pt * 128:(pt + 1) * 128, sl]),
                    wait=((c.sV, prev_res), (c.dY, x_st)), inc=c.dY)
            x_ld = c.dY.n
            for ot in range(NPT):
                rg = ot % 6
                for kt in range(NPT):
                    P.t(lambda e, ot=ot, kt=kt, rg=rg, sl=sl: (
                        e.ldweights(wvv_sb[kt][:, ot * 128:(ot + 1) * 128]),
                        e.matmul(c.reg(rg), wvv_sb[kt][:, ot * 128:(ot + 1) * 128], hp[kt][:, sl],
                                 start=(kt == 0), stop=(kt == NPT - 1)))[-1],
                        wait=(((c.dG, wvv_res), (ev_done.get(("y", rg), (c.sV, 0)))) if kt == 0 else ()),
                        inc=c.sT)
                P.v(lambda e, ot=ot, rg=rg, sl=sl: e.tensor_tensor(
                    yc[ot][:, :], c.reg(rg), r_sb[ot][:, sl], ALU.mult),
                    wait=((c.sT, c.sT.n), (c.sS, r_done_s)), inc=c.sV)
                ev_done[("y", rg)] = (c.sV, c.sV.n)
            P.gate_all(((c.sV, c.sV.n), (c.dY, x_ld)), engines=("vector", "scalar", "tensor"))
            prev_res = emit_ln(P, c, [yc[pt][:, :] for pt in range(NPT)], TC,
                               resid_aps=[xc[pt][:, :] for pt in range(NPT)])
            for pt in range(NPT):
                P.y(lambda e, pt=pt, sl=sl: e.dma_start(out=xO[pt * 128:(pt + 1) * 128, sl], in_=xc[pt][:, :]),
                    wait=((c.sV, prev_res),), inc=c.dY)
            x_st = c.dY.n
        P.wait_only("sync", ((c.dY, c.dY.n),))


def build_full():
    nc = bass.Bass("TRN2", target_bir_lowering=False)
    io = {}

    def inp(name, shape):
        io[name] = nc.dram_tensor(name, shape, F32, kind="ExternalInput").ap()
    inp("xT", [D, T])
    inp("qkvw4", [16, 8, 128, 128])
    inp("wvT", [D, D])
    inp("qk_b", [128, 16])
    inp("bvt", [128, 1])
    inp("owT", [D, D])
    inp("out_b", [128, 8])
    inp("CCS", [128, T])
    inp("SSG", [128, T])
    inp("w1", [D, FF])
    inp("b1", [128, 32])
    inp("w2", [FF, D])
    inp("b2", [128, 8])
    inp("wq", [D, D])
    inp("wk", [D, D])
    inp("wv", [D, D])
    inp("wo", [D, D])
    inp("hsel", [128, 128])
    inp("hbc", [16, 1024])
    inp("wr", [D, D])
    inp("wkk", [D, D])
    inp("wvv", [D, D])
    inp("mur", [128, 8])
    inp("muk", [128, 8])
    inp("bmask", [128, 128])
    inp("ones", [128, 128])
    io["xb"] = nc.dram_tensor("xb", [D, T], F32).ap()
    io["aab_in"] = nc.dram_tensor("aab_in", [NCORES, 128 * 16], F32).ap()
    io["aab_out"] = nc.dram_tensor("aab_out", [NCORES, 128 * 16], F32).ap()
    io["xO"] = nc.dram_tensor("xO", [D, T], F32, kind="ExternalOutput").ap()

    with contextlib.ExitStack() as stack:
        P = Prog(nc, stack)
        c = Ctx(P, 512)
        c.load_const(P, io["ones"])
        emit_l0attn(P, c, nc, io)
        barrier(P, c)
        emit_l0ffn(P, c, nc, io)
        barrier(P, c)
        emit_l1attn(P, c, nc, io)
        barrier(P, c)
        emit_l1ffn(P, c, nc, io)
        P.run_all()
    return nc


def kernel(**inputs):
    h = _host_prep(inputs)
    nc = build_full()
    in_maps = []
    for cid in range(NCORES):
        bmask = np.zeros((128, 128), np.float32)
        if cid > 0:
            bmask[:, (cid - 1) * 16:cid * 16] = 1.0
        in_maps.append({
            "xT": h["xT"][cid], "qkvw4": h["qkv_wt4"], "wvT": h["wvT"], "qk_b": h["qk_b"],
            "bvt": h["bvt"][cid], "owT": h["out_wT"], "out_b": h["out_b"],
            "CCS": h["CCS"][cid], "SSG": h["SSG"][cid], "ones": h["ones"],
            "w1": h["w1"], "b1": h["b1"], "w2": h["w2"], "b2": h["b2"],
            "wq": h["WqT"], "wk": h["WkT"], "wv": h["WvT"], "wo": h["WoT"],
            "hsel": h["hsel"], "hbc": h["hbc"],
            "wr": h["W_rT"], "wkk": h["W_kT"], "wvv": h["W_vT"],
            "mur": h["mu_r"], "muk": h["mu_k"], "bmask": bmask,
        })
    res = run_bass_kernel_spmd(nc, in_maps, core_ids=list(range(NCORES)))
    out = np.empty((B, S, D), np.float32)
    for cid in range(NCORES):
        xo = np.asarray(res.results[cid]["xO"])
        out[:, cid * SC:(cid + 1) * SC, :] = xo.T.reshape(B, SC, D)
    return out


# version 15
# speedup vs baseline: 1.0829x; 1.0598x over previous
import contextlib
import numpy as np
import ml_dtypes
import concourse.bass as bass
import concourse.mybir as mybir
from concourse.bass_utils import run_bass_kernel_spmd
from concourse.aot_env import allow_ici_all_to_all

F32 = mybir.dt.float32
BF16 = mybir.dt.bfloat16
AF = mybir.ActivationFunctionType
ALU = mybir.AluOpType

NCORES = 8
B = 2
S = 8192
D = 1024
NPT = D // 128
SC = S // NCORES
T = B * SC
FF = 4096
EPS_LN = 1e-5
EPS_L1 = 1e-6
C01 = float(np.sqrt(0.5))


class Sem:
    def __init__(self, h):
        self.h = h
        self.n = 0


class Prog:
    ENGINES = ("tensor", "vector", "scalar", "gpsimd", "sync")

    def __init__(self, nc, stack):
        self.nc = nc
        self.stack = stack
        self.q = {e: [] for e in self.ENGINES}
        self.pending = {e: [] for e in self.ENGINES}

    def sem(self, name):
        return Sem(self.stack.enter_context(self.nc.semaphore(name)))

    def gate(self, eng, wait):
        for w in wait:
            self.pending[eng].append(w if not isinstance(w, Sem) else (w, w.n))

    def gate_all(self, wait, engines=None):
        for e in (engines or self.ENGINES):
            self.gate(e, wait)

    def wait_only(self, eng, wait):
        waits = [(s.h, v) for s, v in ((w, w.n) if isinstance(w, Sem) else w for w in wait) if v > 0]
        self.q[eng].append(lambda e, waits=tuple(waits): [e.wait_ge(h, v) for h, v in waits])

    def emit(self, eng, fn, wait=(), inc=None, inck=1):
        waits = []
        wait = tuple(self.pending[eng]) + tuple(wait)
        self.pending[eng] = []
        for w in wait:
            if isinstance(w, Sem):
                if w.n > 0:
                    waits.append((w.h, w.n))
            else:
                s, v = w
                if v > 0:
                    waits.append((s.h, v))
        if inc is not None:
            inc.n += inck

        def run(e, waits=tuple(waits), fn=fn, inc=inc, inck=inck):
            for h, v in waits:
                e.wait_ge(h, v)
            r = fn(e)
            if inc is not None:
                r.then_inc(inc.h, inck)

        self.q[eng].append(run)

    def t(self, fn, wait=(), inc=None):
        self.emit("tensor", fn, wait, inc)

    def v(self, fn, wait=(), inc=None):
        self.emit("vector", fn, wait, inc)

    def s(self, fn, wait=(), inc=None):
        self.emit("scalar", fn, wait, inc)

    def g(self, fn, wait=(), inc=None, inck=16):
        self.emit("gpsimd", fn, wait, inc, inck)

    def y(self, fn, wait=(), inc=None, inck=16):
        self.emit("sync", fn, wait, inc, inck)

    def run_all(self):
        block = self.stack.enter_context(self.nc.Block())
        nc = self.nc

        def runner(queue):
            def go(e):
                with nc.allow_low_precision(reason="bf16 matmuls/elementwise blessed by problem tolerance"):
                    for f in queue:
                        f(e)
            return go

        block.tensor(runner(self.q["tensor"]))
        block.vector(runner(self.q["vector"]))
        block.scalar(runner(self.q["scalar"]))
        block.gpsimd(runner(self.q["gpsimd"]))
        block.sync(runner(self.q["sync"]))


def _rope_perm():
    perm = np.zeros(D, dtype=np.int64)
    for p in range(8):
        h0, h1 = 2 * p, 2 * p + 1
        base = p * 128
        perm[base + 0:base + 32] = h0 * 64 + np.arange(0, 64, 2)
        perm[base + 32:base + 64] = h0 * 64 + np.arange(1, 64, 2)
        perm[base + 64:base + 96] = h1 * 64 + np.arange(0, 64, 2)
        perm[base + 96:base + 128] = h1 * 64 + np.arange(1, 64, 2)
    return perm


def _host_prep(inputs):
    h = {}
    x = np.asarray(inputs["L0_attn_x"])
    perm = _rope_perm()

    qkv_w = np.asarray(inputs["L0_attn_qkv_w"])
    qkv_b = np.asarray(inputs["L0_attn_qkv_b"])
    qkv_perm = np.concatenate([perm, D + perm, 2 * D + np.arange(D)])
    h["qkv_wT"] = np.ascontiguousarray(qkv_w[qkv_perm].T)
    h["qkv_b"] = np.ascontiguousarray(qkv_b[qkv_perm].reshape(24, 128).T)
    h["out_wT"] = np.ascontiguousarray(np.asarray(inputs["L0_attn_out_w"]).T)
    h["out_b"] = np.ascontiguousarray(np.asarray(inputs["L0_attn_out_b"]).reshape(8, 128).T)

    h["w1"] = np.ascontiguousarray(np.asarray(inputs["L0_ffn_w1"]))
    h["b1"] = np.ascontiguousarray(np.asarray(inputs["L0_ffn_b1"]).reshape(32, 128).T)
    h["w2"] = np.ascontiguousarray(np.asarray(inputs["L0_ffn_w2"]))
    h["b2"] = np.ascontiguousarray(np.asarray(inputs["L0_ffn_b2"]).reshape(8, 128).T)

    for nm in ("Wq", "Wk", "Wv", "Wo"):
        h[nm + "T"] = np.ascontiguousarray(np.asarray(inputs["L1_attn_" + nm]).T)
    for nm in ("W_r", "W_k", "W_v"):
        h[nm + "T"] = np.ascontiguousarray(np.asarray(inputs["L1_ffn_" + nm]).T)
    h["mu_r"] = np.ascontiguousarray(np.asarray(inputs["L1_ffn_mu_r"]).reshape(8, 128).T)
    h["mu_k"] = np.ascontiguousarray(np.asarray(inputs["L1_ffn_mu_k"]).reshape(8, 128).T)

    h["ones"] = np.ones((128, 128), np.float32)
    hsel = np.zeros((128, 8 * 16), np.float32)
    hbc = np.zeros((16, 8 * 128), np.float32)
    for pt in range(8):
        for r in range(128):
            head = 2 * pt + (1 if r >= 64 else 0)
            hsel[r, pt * 16 + head] = 1.0
            hbc[head, pt * 128 + r] = 1.0
    h["hsel"] = hsel
    h["hbc"] = hbc

    h["qkv_wt4"] = np.ascontiguousarray(
        h["qkv_wT"][:, 0:2048].reshape(8, 128, 16, 128).transpose(2, 0, 1, 3))
    h["wvT"] = np.ascontiguousarray(h["qkv_wT"][:, 2048:3072])
    h["qk_b"] = np.ascontiguousarray(h["qkv_b"][:, 0:16])
    h["bvt"] = []
    vb = qkv_b[2 * D:3 * D]
    for c in range(NCORES):
        h["bvt"].append(np.ascontiguousarray(vb[c * 128:(c + 1) * 128].reshape(128, 1)))

    h["xT"] = []
    h["CCS"] = []
    h["SSG"] = []
    inv = (10000.0 ** (-(np.arange(0, 64, 2, dtype=np.float64)) / 64.0))
    for c in range(NCORES):
        xs = x[:, c * SC:(c + 1) * SC, :].reshape(T, D)
        h["xT"].append(np.ascontiguousarray(xs.T))
        pos = np.arange(c * SC, (c + 1) * SC, dtype=np.float64)
        th = np.outer(inv, pos)
        cos = np.cos(th).astype(np.float32)
        sin = np.sin(th).astype(np.float32)
        cosw = np.tile(cos, (1, 2))
        sinw = np.tile(sin, (1, 2))
        h["CCS"].append(np.ascontiguousarray(np.vstack([cosw] * 4)))
        h["SSG"].append(np.ascontiguousarray(np.vstack([-sinw, sinw, -sinw, sinw])))
    return h


class Ctx:

    def __init__(self, P, nt):
        self.nt = nt
        nc, ec = P.nc, P.stack.enter_context
        self.ones_bf = ec(nc.sbuf_tensor("ones_bf", [128, 128], BF16))
        self.ones_f = ec(nc.sbuf_tensor("ones_f", [128, 128], F32))
        self.eps_t = ec(nc.sbuf_tensor("eps_t", [128, 1], F32))
        self.invD = ec(nc.sbuf_tensor("invD", [1, 128], F32))
        self.row = ec(nc.sbuf_tensor("lnrow", [1, 2 * nt], F32))
        self.abc = ec(nc.sbuf_tensor("lnabc", [128, nt], F32))
        self.bbc = ec(nc.sbuf_tensor("lnbbc", [128, nt], F32))
        self.tmp = [ec(nc.sbuf_tensor(f"lntmp{i}", [128, nt], F32)) for i in range(2)]
        self.tmpg = ec(nc.sbuf_tensor("lntmpg", [128, nt], F32))
        self.sq = [ec(nc.sbuf_tensor(f"lnsq{i}", [128, nt], BF16)) for i in range(2)]
        self.hs = [ec(nc.sbuf_tensor(f"hs{pt}", [128, T], BF16)) for pt in range(NPT)]
        self.psBig = ec(nc.psum_tensor("psBig", [128, 2048], F32))
        self.ps = [ec(nc.psum_tensor(f"ps{i}", [128, 512], F32)) for i in range(4)]
        self.sT = P.sem("sT")
        self.sV = P.sem("sV")
        self.sS = P.sem("sS")
        self.sG = P.sem("sG")
        self.dG = P.sem("dG")
        self.dY = P.sem("dY")
        self.sC = P.sem("sC")

    def reg(self, i):
        if i < 4:
            return self.psBig[:, i * 512:(i + 1) * 512]
        return self.ps[i - 4][:, :]

    def load_const(self, P, ones_dram, eps_val=EPS_LN):
        P.y(lambda e: e.dma_start(out=self.ones_f[:, :], in_=ones_dram[:, :]), inc=self.dY)
        P.g(lambda e: e.dma_start(out=self.ones_bf[:, :], in_=ones_dram[:, :]), inc=self.dG)
        P.v(lambda e: e.memset(self.eps_t[:, :], eps_val), inc=self.sV)
        P.v(lambda e: e.memset(self.invD[:, :], 1.0 / D), inc=self.sV)


def emit_ln(P, c, x_aps, nt, out_aps=None, resid_aps=None, xb_stage=None):
    src_bf16 = resid_aps is not None
    v0 = c.sV.n
    if src_bf16:
        xb = x_aps
        xb_ready = [() for _ in range(NPT)]
    else:
        xb = xb_stage if xb_stage is not None else out_aps
        xb_ready = []
        for pt in range(NPT):
            P.s(lambda e, pt=pt: e.copy(xb[pt], x_aps[pt]),
                wait=((c.sS, c.sS.n), (c.sV, v0)), inc=c.sS)
            xb_ready.append(((c.sS, c.sS.n),))
    assert nt <= 512
    P.gate("scalar", ((c.sT, c.sT.n),))
    sq_free_at = [0, 0]
    for pt in range(NPT):
        par = pt % 2
        P.s(lambda e, pt=pt, par=par: e.activation(out=c.sq[par][:, 0:nt], in_=xb[pt], func=AF.Square),
            wait=xb_ready[pt] + ((c.sT, sq_free_at[par]),), inc=c.sS)
        sq_done = c.sS.n
        st, sp = (pt == 0), (pt == NPT - 1)
        P.t(lambda e, pt=pt, st=st, sp=sp: (
            e.ldweights(c.ones_bf[:, 0:1], tile_position=(0, 0)),
            e.matmul(c.ps[2][0:1, 0:nt], c.ones_bf[:, 0:1], xb[pt],
                     start=st, stop=sp, tile_position=(0, 0), skip_group_check=True))[-1],
            wait=xb_ready[pt] + (((c.sV, v0),) if pt == 0 else ()), inc=c.sT)
        P.t(lambda e, par=par, st=st, sp=sp: (
            e.ldweights(c.ones_bf[:, 0:1], tile_position=(0, 0)),
            e.matmul(c.ps[3][0:1, 0:nt], c.ones_bf[:, 0:1], c.sq[par][:, 0:nt],
                     start=st, stop=sp, tile_position=(0, 0), skip_group_check=True))[-1],
            wait=((c.sS, sq_done),), inc=c.sT)
        sq_free_at[par] = c.sT.n
    stats_done = c.sT.n
    r = c.row
    P.s(lambda e: e.copy(r[0:1, 0:nt], c.ps[2][0:1, 0:nt]), wait=((c.sT, stats_done),), inc=c.sS)
    P.s(lambda e: e.copy(r[0:1, nt:2 * nt], c.ps[3][0:1, 0:nt]), inc=c.sS)
    rows_ev = c.sS.n
    P.t(lambda e: e.matmul(c.ps[2][:, 0:nt], c.invD[0:1, :], r[0:1, 0:nt],
                           start=True, stop=True, skip_group_check=True),
        wait=((c.sS, rows_ev),), inc=c.sT)
    P.t(lambda e: e.matmul(c.ps[3][:, 0:nt], c.invD[0:1, :], r[0:1, nt:2 * nt],
                           start=True, stop=True, skip_group_check=True),
        inc=c.sT)
    bc_done = c.sT.n
    P.v(lambda e: e.tensor_copy(c.abc[:, 0:nt], c.ps[2][:, 0:nt]),
        wait=((c.sT, bc_done), (c.sV, c.sV.n)), inc=c.sV)
    P.v(lambda e: e.tensor_copy(c.bbc[:, 0:nt], c.ps[3][:, 0:nt]), inc=c.sV)
    P.v(lambda e: e.tensor_tensor(c.tmp[0][:, 0:nt], c.abc[:, 0:nt], c.abc[:, 0:nt], ALU.mult),
        wait=((c.sV, c.sV.n),), inc=c.sV)
    P.v(lambda e: e.tensor_tensor(c.bbc[:, 0:nt], c.bbc[:, 0:nt], c.tmp[0][:, 0:nt], ALU.subtract),
        wait=((c.sV, c.sV.n),), inc=c.sV)
    var_done = c.sV.n
    P.s(lambda e: e.activation(out=c.bbc[:, 0:nt], in_=c.bbc[:, 0:nt], func=AF.Ln,
                               bias=c.eps_t[:, :]), wait=((c.sV, var_done),), inc=c.sS)
    P.s(lambda e: e.activation(out=c.bbc[:, 0:nt], in_=c.bbc[:, 0:nt], func=AF.Exp,
                               scale=-0.5), wait=((c.sS, c.sS.n),), inc=c.sS)
    rstd_done = c.sS.n
    P.gate("vector", ((c.sS, rstd_done),))
    for pt in range(NPT):
        par = pt % 2
        if resid_aps is None:
            P.v(lambda e, pt=pt, par=par: e.tensor_tensor(c.tmp[par][:, 0:nt], x_aps[pt], c.abc[:, 0:nt],
                                                          ALU.subtract),
                wait=((c.sV, c.sV.n),), inc=c.sV)
            P.v(lambda e, pt=pt, par=par: e.tensor_tensor(out_aps[pt], c.tmp[par][:, 0:nt], c.bbc[:, 0:nt],
                                                          ALU.mult),
                wait=((c.sV, c.sV.n),), inc=c.sV)
        else:
            P.v(lambda e, pt=pt, par=par: e.tensor_tensor(c.tmp[par][:, 0:nt], x_aps[pt], c.abc[:, 0:nt],
                                                          ALU.subtract),
                wait=((c.sV, c.sV.n),), inc=c.sV)
            P.v(lambda e, pt=pt, par=par: e.tensor_tensor(c.tmp[par][:, 0:nt], c.tmp[par][:, 0:nt],
                                                          c.bbc[:, 0:nt], ALU.mult),
                wait=((c.sV, c.sV.n),), inc=c.sV)
            P.v(lambda e, pt=pt, par=par: e.tensor_tensor(resid_aps[pt], resid_aps[pt], c.tmp[par][:, 0:nt],
                                                          ALU.add),
                wait=((c.sV, c.sV.n),), inc=c.sV)
    return c.sV.n


def blk2d(t, d, off, p, n):
    return t[d:d + 1, off:off + p * n].rearrange("a (p t) -> (a p) t", p=p, t=n)


def barrier(P, c):
    P.gate_all(tuple((s, s.n) for s in (c.sT, c.sV, c.sS, c.dY, c.dG, c.sC)))


def emit_l0attn(P, c, nc, io, debug=False):
    xT = io["xT"]
    qkvw4 = io["qkvw4"]
    wvT = io["wvT"]
    qk_b = io["qk_b"]
    bvt = io["bvt"]
    owT = io["owT"]
    out_b = io["out_b"]
    CCSd = io["CCS"]
    SSGd = io["SSG"]
    xO = io["xb"]
    BLK = 3 * 128 * T
    QOFF, KOFF, VOFF = 0, 128 * T, 2 * 128 * T
    aain = nc.dram_tensor("aain", [NCORES, BLK], BF16).ap()
    aaout = nc.dram_tensor("aaout", [NCORES, BLK], BF16).ap()
    abin = [nc.dram_tensor(f"abin{b}", [NCORES, 128 * SC], BF16).ap() for b in range(B)]
    about = [nc.dram_tensor(f"about{b}", [NCORES, 128 * SC], BF16).ap() for b in range(B)]

    TC = 512
    NQC = S // TC
    NKT = S // 128
    sc0 = c.sC.n

    with contextlib.ExitStack() as stack:
        ec = stack.enter_context
        xc = [ec(nc.sbuf_tensor(f"A_xc{pt}", [128, TC], F32)) for pt in range(NPT)]
        yc = [ec(nc.sbuf_tensor(f"A_yc{pt}", [128, TC], BF16)) for pt in range(NPT)]
        wv_sb = [ec(nc.sbuf_tensor(f"A_wv{kt}", [128, 1024], BF16)) for kt in range(NPT)]
        outb_sb = ec(nc.sbuf_tensor("A_outb_sb", [128, 8], F32))
        hn = c.hs
        Kb = ec(nc.sbuf_tensor("A_Kb", [128, B * S], BF16))
        Vb = ec(nc.sbuf_tensor("A_Vb", [128, B * S], BF16))
        qsb = [ec(nc.sbuf_tensor(f"A_qsb{i}", [128, TC], BF16)) for i in range(2)]
        expb = [ec(nc.sbuf_tensor(f"A_expb{i}", [128, 1024], BF16)) for i in range(2)]
        exps = ec(nc.sbuf_tensor("A_exps", [128, 1024], BF16))
        astg = [ec(nc.sbuf_tensor(f"A_astg{i}", [128, TC], BF16)) for i in range(2)]
        denr = ec(nc.sbuf_tensor("A_denr", [1, 1024], BF16))
        denb_sb = ec(nc.sbuf_tensor("A_denb_sb", [128, TC], BF16))
        mt = [ec(nc.sbuf_tensor(f"A_mt{i}", [128, 1], F32)) for i in range(2)]
        nacc = ec(nc.sbuf_tensor("A_nacc", [1, 8], F32))
        qkbt = ec(nc.sbuf_tensor("A_qkbt", [128, 16], F32))
        bvt_sb = ec(nc.sbuf_tensor("A_bvt_sb", [128, 1], F32))
        attn_scope = contextlib.ExitStack()
        ea = attn_scope.enter_context
        wqb = [ea(nc.sbuf_tensor(f"A_wqb{i}", [128, 1024], BF16)) for i in range(2)]
        stg = [ea(nc.sbuf_tensor(f"A_stg{i}", [128, T], BF16)) for i in range(2)]
        swp = [ea(nc.sbuf_tensor(f"A_swp{i}", [128, T], BF16)) for i in range(2)]
        vstg = [ea(nc.sbuf_tensor(f"A_vstg{i}", [128, 1024], BF16)) for i in range(2)]
        ccs = ea(nc.sbuf_tensor("A_ccs", [128, T], BF16))
        ssg = ea(nc.sbuf_tensor("A_ssg", [128, T], BF16))

        P.g(lambda e: e.dma_start(out=ccs[:, :], in_=CCSd[:, :]), inc=c.dG)
        P.g(lambda e: e.dma_start(out=ssg[:, :], in_=SSGd[:, :]), inc=c.dG)
        for kt in range(NPT):
            P.g(lambda e, kt=kt: e.dma_start(out=wv_sb[kt][:, :], in_=wvT[kt * 128:(kt + 1) * 128, :]), inc=c.dG)
        P.y(lambda e: e.dma_start(out=qkbt[:, :], in_=qk_b[:, :]), inc=c.dY)
        P.y(lambda e: e.dma_start(out=bvt_sb[:, :], in_=bvt[:, :]), inc=c.dY)
        P.y(lambda e: e.dma_start(out=outb_sb[:, :], in_=out_b[:, :]), inc=c.dY)
        consts_g, consts_y = c.dG.n, c.dY.n

        prev_ln = 0
        ln_at = []
        for tcc in range(T // TC):
            sl = slice(tcc * TC, (tcc + 1) * TC)
            for pt in range(NPT):
                P.y(lambda e, pt=pt, sl=sl: e.dma_start(out=xc[pt][:, :], in_=xT[pt * 128:(pt + 1) * 128, sl]),
                    wait=((c.sV, prev_ln),), inc=c.dY)
            P.gate_all(((c.dY, c.dY.n), (c.dG, consts_g)), engines=("vector", "scalar", "tensor"))
            prev_ln = emit_ln(P, c, [xc[pt][:, :] for pt in range(NPT)], TC,
                              out_aps=[hn[pt][:, sl] for pt in range(NPT)])
            ln_at.append(prev_ln)
        hn_done = prev_ln

        wq_cons = [0, 0]
        stg_free = [0, 0]
        ev_done = {}
        for ot in range(16):
            par = ot % 2
            P.g(lambda e, ot=ot: e.dma_start(
                out=wqb[ot % 2][:, :].rearrange("p (k q) -> p k q", k=8, q=128),
                in_=qkvw4[ot:ot + 1, :, :, :].rearrange("a k p q -> p (a k) q")),
                wait=((c.sT, wq_cons[par]),), inc=c.dG)
            wq_ld = c.dG.n
            for ch in range(4):
                rg = (ot * 4 + ch) % 8
                for kt in range(NPT):
                    P.t(lambda e, ot=ot, ch=ch, kt=kt, rg=rg, par=par: (
                        e.ldweights(wqb[par][:, kt * 128:(kt + 1) * 128]),
                        e.matmul(c.reg(rg), wqb[par][:, kt * 128:(kt + 1) * 128],
                                 hn[kt][:, ch * 512:(ch + 1) * 512],
                                 start=(kt == 0), stop=(kt == NPT - 1)))[-1],
                        wait=(((c.dG, wq_ld), (c.sV, hn_done if rg >= 6 else ln_at[ch]),
                               (c.sV, ev_done.get(rg, 0)),
                               (c.dY, stg_free[par])) if kt == 0 else ()),
                        inc=c.sT)
                P.v(lambda e, ch=ch, rg=rg, par=par, ot=ot: e.tensor_scalar_add(
                    stg[par][:, ch * 512:(ch + 1) * 512], c.reg(rg), qkbt[:, ot:ot + 1]),
                    wait=((c.sT, c.sT.n),), inc=c.sV)
                ev_done[rg] = c.sV.n
            wq_cons[par] = c.sT.n
            evd = c.sV.n
            P.y(lambda e, par=par: e.dma_start(out=swp[par][0:32, :], in_=stg[par][32:64, :]),
                wait=((c.sV, evd),), inc=c.dY)
            P.y(lambda e, par=par: e.dma_start(out=swp[par][32:64, :], in_=stg[par][0:32, :]),
                inc=c.dY)
            P.y(lambda e, par=par: e.dma_start(out=swp[par][64:96, :], in_=stg[par][96:128, :]),
                inc=c.dY)
            P.y(lambda e, par=par: e.dma_start(out=swp[par][96:128, :], in_=stg[par][64:96, :]),
                inc=c.dY)
            swp_ld = c.dY.n
            P.v(lambda e, par=par: e.tensor_tensor(stg[par][:, :], stg[par][:, :], ccs[:, :], ALU.mult),
                wait=((c.dY, swp_ld),), inc=c.sV)
            P.v(lambda e, par=par: e.tensor_tensor(swp[par][:, :], swp[par][:, :], ssg[:, :], ALU.mult),
                inc=c.sV)
            P.v(lambda e, par=par: e.tensor_tensor(stg[par][:, :], stg[par][:, :], swp[par][:, :], ALU.add),
                wait=((c.sV, c.sV.n),), inc=c.sV)
            rope_done = c.sV.n
            dst, off = (ot, QOFF) if ot < 8 else (ot - 8, KOFF)
            P.y(lambda e, dst=dst, off=off, par=par: e.dma_start(
                out=blk2d(aain, dst, off, 128, T), in_=stg[par][:, :]),
                wait=((c.sV, rope_done),), inc=c.dY)
            stg_free[par] = c.dY.n

        vst_free = [0, 0]
        for tt in range(16):
            par = tt % 2
            for dvc in range(2):
                rg = (tt * 2 + dvc) % 8
                for kt in range(NPT):
                    P.t(lambda e, tt=tt, dvc=dvc, kt=kt, rg=rg: (
                        e.ldweights(hn[kt][:, tt * 128:(tt + 1) * 128]),
                        e.matmul(c.reg(rg), hn[kt][:, tt * 128:(tt + 1) * 128],
                                 wv_sb[kt][:, dvc * 512:(dvc + 1) * 512],
                                 start=(kt == 0), stop=(kt == NPT - 1)))[-1],
                        wait=(((c.sV, ev_done.get(rg, 0)), (c.dY, vst_free[par])) if kt == 0 else ()),
                        inc=c.sT)
                P.v(lambda e, dvc=dvc, rg=rg, par=par: e.tensor_copy(
                    vstg[par][:, dvc * 512:(dvc + 1) * 512], c.reg(rg)),
                    wait=((c.sT, c.sT.n),), inc=c.sV)
                ev_done[rg] = c.sV.n
            vev = c.sV.n
            for d in range(NCORES):
                P.y(lambda e, tt=tt, d=d, par=par: e.dma_start(
                    out=blk2d(aain, d, VOFF + tt * 128 * 128, 128, 128),
                    in_=vstg[par][:, d * 128:(d + 1) * 128]),
                    wait=((c.sV, vev),) if d == 0 else (), inc=c.dY)
            vst_free[par] = c.dY.n
        vproj_T = c.sT.n
        stage_done = c.dY.n

        for kt in range(NPT):
            P.g(lambda e, kt=kt: e.dma_start(out=wv_sb[kt][:, :], in_=owT[kt * 128:(kt + 1) * 128, :]),
                wait=((c.sT, vproj_T),) if kt == 0 else (), inc=c.dG)
        wo_ld = c.dG.n
        attn_scope.close()
        af_a = [ec(nc.sbuf_tensor(f"A_afa{pt}", [128, SC], BF16)) for pt in range(NPT)]
        af_b = [ec(nc.sbuf_tensor(f"A_afb{pt}", [128, SC], BF16)) for pt in range(NPT)]

        def cc1(e):
            with allow_ici_all_to_all():
                return e.collective_compute(
                    "AllToAll", ALU.bypass, replica_groups=[list(range(NCORES))],
                    ins=[aain[:, :]], outs=[aaout[:, :]])
        P.g(cc1, wait=((c.dY, stage_done),), inc=c.sC, inck=1)

        for src in range(NCORES):
            for b in range(B):
                P.y(lambda e, src=src, b=b: e.dma_start(
                    out=Kb[:, b * S + src * SC: b * S + (src + 1) * SC],
                    in_=blk2d(aaout, src, KOFF, 128, T)[:, b * SC:(b + 1) * SC]),
                    wait=((c.sC, sc0 + 1),), inc=c.dY)
                P.y(lambda e, src=src, b=b: e.dma_start(
                    out=Vb[:, (b * S + src * SC):(b * S + (src + 1) * SC)]
                        .rearrange("p (t c) -> p t c", t=SC // 128, c=128),
                    in_=aaout[src:src + 1, VOFF + b * SC * 128: VOFF + (b + 1) * SC * 128]
                        .rearrange("a (t p c) -> p (a t) c", t=SC // 128, p=128, c=128)),
                    wait=((c.sC, sc0 + 1),), inc=c.dY)
        kv_ld = c.dY.n

        P.v(lambda e: e.memset(nacc[:, :], 0.0), inc=c.sV)
        nacc0 = c.sV.n
        for b in range(B):
            for (acol, is_q) in ((1, False), (0, True)):
                for ch in range(16):
                    par = ch % 2
                    if is_q:
                        srcc = (ch * 512) // SC
                        o_in = (ch * 512) % SC
                        P.y(lambda e, b=b, srcc=srcc, o_in=o_in, par=par: e.dma_start(
                            out=qsb[par][:, :],
                            in_=blk2d(aaout, srcc, QOFF, 128, T)[:, b * SC + o_in: b * SC + o_in + 512]),
                            wait=((c.sC, sc0 + 1), (c.sS, c.sS.n)), inc=c.dY)
                        qld = c.dY.n
                        P.s(lambda e, par=par: e.activation(out=c.sq[par][:, :], in_=qsb[par][:, :],
                                                            func=AF.Square),
                            wait=((c.dY, qld), (c.sT, c.sT.n)), inc=c.sS)
                    else:
                        P.s(lambda e, b=b, ch=ch, par=par: e.activation(
                            out=c.sq[par][:, :], in_=Kb[:, b * S + ch * 512: b * S + (ch + 1) * 512],
                            func=AF.Square),
                            wait=((c.dY, kv_ld), (c.sT, c.sT.n)), inc=c.sS)
                    sqd = c.sS.n
                    P.t(lambda e, par=par: (
                        e.ldweights(c.ones_bf[:, 0:1]),
                        e.matmul(c.ps[2][0:1, :], c.ones_bf[:, 0:1],
                                 c.sq[par][:, :], start=True, stop=True,
                                 skip_group_check=True))[-1],
                        wait=((c.sS, sqd), (c.sV, c.sV.n)), inc=c.sT)
                    P.v(lambda e: e.tensor_reduce(
                        nacc[0:1, 6:7], c.ps[2][0:1, :], axis=mybir.AxisListType.X,
                        op=ALU.max), wait=((c.sT, c.sT.n),), inc=c.sV)
                    P.v(lambda e, acol=acol: e.tensor_tensor(
                        nacc[0:1, acol:acol + 1], nacc[0:1, acol:acol + 1], nacc[0:1, 6:7],
                        ALU.max), wait=((c.sV, c.sV.n), (c.sV, nacc0)), inc=c.sV)
            P.v(lambda e: e.tensor_copy(nacc[0:1, 4:5], nacc[0:1, 0:1]),
                wait=((c.sV, c.sV.n),), inc=c.sV)
            P.v(lambda e: e.tensor_copy(nacc[0:1, 5:6], nacc[0:1, 1:2]),
                wait=((c.sV, c.sV.n),), inc=c.sV)
            mx_done = c.sV.n
            P.s(lambda e: e.activation(out=nacc[0:1, 4:5], in_=nacc[0:1, 4:5], func=AF.Ln),
                wait=((c.sV, mx_done),), inc=c.sS)
            P.s(lambda e: e.activation(out=nacc[0:1, 5:6], in_=nacc[0:1, 5:6], func=AF.Ln),
                wait=((c.sS, c.sS.n),), inc=c.sS)
            P.v(lambda e: e.tensor_tensor(nacc[0:1, 4:5], nacc[0:1, 4:5], nacc[0:1, 5:6], ALU.add),
                wait=((c.sS, c.sS.n),), inc=c.sV)
            P.s(lambda e: e.activation(out=nacc[0:1, 4:5], in_=nacc[0:1, 4:5], func=AF.Exp, scale=0.5),
                wait=((c.sV, c.sV.n),), inc=c.sS)
            P.v(lambda e: e.tensor_scalar_mul(nacc[0:1, 4:5], nacc[0:1, 4:5], -0.125),
                wait=((c.sS, c.sS.n),), inc=c.sV)
            mrow = c.sV.n
            P.t(lambda e: e.matmul(c.ps[2][:, 0:1], c.ones_f[0:1, :], nacc[0:1, 4:5],
                                   start=True, stop=True),
                wait=((c.sV, mrow),), inc=c.sT)
            P.v(lambda e, b=b: e.tensor_copy(mt[b][:, :], c.ps[2][:, 0:1]),
                wait=((c.sT, c.sT.n),), inc=c.sV)
            P.v(lambda e: e.memset(nacc[0:1, 0:2], 0.0), wait=((c.sV, c.sV.n),), inc=c.sV)
        mt_done = c.sV.n

        phG_sS = c.sS.n
        abst = {}
        scores_end = {}
        av_evac_at = {}
        den_evac_at = {}
        recip_at = {}
        prev_exp_tail = [0, 0]
        prev_av_tail = [0, 0]
        prev_add_tail = [0, 0]
        prev_denq = 0
        DQ = 4
        pending_tail = [None]
        af_ld_part = [0, 0]

        def make_tail(gi, qpar, gpos, b, denq_last):
            def tail():
                P.v(lambda e: e.tensor_copy(denr[0:1, 0:512], c.ps[2][0:1, :]),
                    wait=((c.sT, denq_last), (c.sV, c.sV.n)), inc=c.sV)
                P.v(lambda e: e.tensor_copy(denr[0:1, 512:1024], c.ps[3][0:1, :]), inc=c.sV)
                recip_at[gi] = c.sV.n
                P.t(lambda e: (
                    e.ldweights(c.ones_bf[0:1, 0:64], tile_position=(0, 64)),
                    e.matmul(c.ps[2][64:128, :], c.ones_bf[0:1, 0:64], denr[0:1, 0:512],
                             start=True, stop=True, tile_position=(0, 64), skip_group_check=True))[-1],
                    wait=((c.sV, recip_at[gi]), (c.sV, den_evac_at.get(gi - 1, 0))), inc=c.sT)
                P.t(lambda e: (
                    e.ldweights(c.ones_bf[0:1, 0:64], tile_position=(0, 64)),
                    e.matmul(c.ps[3][64:128, :], c.ones_bf[0:1, 0:64], denr[0:1, 512:1024],
                             start=True, stop=True, tile_position=(0, 64), skip_group_check=True))[-1],
                    inc=c.sT)
                denb_T = c.sT.n
                P.v(lambda e: e.reciprocal(denb_sb[0:64, :], c.ps[2][64:128, :]),
                    wait=((c.sT, denb_T), (c.sV, c.sV.n)), inc=c.sV)
                P.v(lambda e: e.reciprocal(denb_sb[64:128, :], c.ps[3][64:128, :]), inc=c.sV)
                den_evac_at[gi] = c.sV.n
                P.v(lambda e: e.tensor_tensor(astg[qpar][:, :], c.ps[qpar][:, :],
                                              denb_sb[:, :], ALU.mult),
                    wait=((c.sV, c.sV.n), (c.dY, abst.get(gi - 2, 0))), inc=c.sV)
                av_evac_at[gi] = c.sV.n
                P.v(lambda e: e.tensor_scalar_add(astg[qpar][:, :], astg[qpar][:, :],
                                                  bvt_sb[:, :]),
                    wait=((c.sV, c.sV.n), (c.dY, consts_y)), inc=c.sV)
                a_done = c.sV.n
                dstc = gpos // SC
                o_in = gpos % SC
                P.y(lambda e, dstc=dstc, o_in=o_in, b=b: e.dma_start(
                    out=blk2d(abin[b], dstc, 0, 128, SC)[:, o_in:o_in + 512],
                    in_=astg[qpar][:, :]),
                    wait=((c.sV, a_done),), inc=c.dY)
                abst[gi] = c.dY.n
            return tail

        for b in range(B):
            for qc in range(NQC):
                gi = b * NQC + qc
                qpar = gi % 2
                gpos = qc * TC
                src = gpos // SC
                o_in = gpos % SC
                P.y(lambda e, src=src, o_in=o_in, b=b, qpar=qpar: e.dma_start(
                    out=qsb[qpar][:, :],
                    in_=blk2d(aaout, src, QOFF, 128, T)[:, b * SC + o_in: b * SC + o_in + 512]),
                    wait=((c.sT, scores_end.get(gi - 2, 0)), (c.sS, phG_sS if gi < 2 else 0)),
                    inc=c.dY)
                q_ld = c.dY.n
                exp_at = {-2: prev_exp_tail[0], -1: prev_exp_tail[1]}
                av_at = {-2: prev_av_tail[0], -1: prev_av_tail[1]}
                add_at = {-2: prev_add_tail[0], -1: prev_add_tail[1]}
                scores_at = {}
                exp0_at = {}
                denq_at = {-1: prev_denq}
                for step in range(NKT + 2):
                    if step < NKT:
                        kt = step
                        p = kt % 2
                        kcol = b * S + kt * 128
                        P.t(lambda e, kcol=kcol, p=p, qpar=qpar: (
                            e.ldweights(Kb[0:64, kcol:kcol + 128], tile_position=(0, 0)),
                            e.matmul(c.psBig[:, p * 1024:p * 1024 + 512], Kb[0:64, kcol:kcol + 128],
                                     qsb[qpar][0:64, :], start=True, stop=True, tile_position=(0, 0)))[-1],
                            wait=(((c.dY, q_ld), (c.sV, mt_done)) if kt == 0 else
                                  ((c.sS, exp_at.get(kt - 2, 0)),)),
                            inc=c.sT)
                        P.t(lambda e, kcol=kcol, p=p, qpar=qpar: (
                            e.ldweights(Kb[64:128, kcol:kcol + 128], tile_position=(64, 0)),
                            e.matmul(c.psBig[:, p * 1024 + 512:p * 1024 + 1024], Kb[64:128, kcol:kcol + 128],
                                     qsb[qpar][64:128, :], start=True, stop=True, tile_position=(64, 0),
                                     skip_group_check=True))[-1],
                            inc=c.sT)
                        scores_at[kt] = c.sT.n
                    if step == 2 and pending_tail[0] is not None:
                        pending_tail[0]()
                        pending_tail[0] = None
                        if gi == NQC:
                            ab0_done = c.dY.n

                            def cc2a(e):
                                with allow_ici_all_to_all():
                                    return e.collective_compute(
                                        "AllToAll", ALU.bypass, replica_groups=[list(range(NCORES))],
                                        ins=[abin[0][:, :]], outs=[about[0][:, :]])
                            P.g(cc2a, wait=((c.dY, ab0_done),), inc=c.sC, inck=1)
                    if step >= 2:
                        kt = step - 2
                        p = kt % 2
                        first, last = (kt == 0), (kt == NKT - 1)
                        vcol = (b * 64 + kt) * 128
                        P.t(lambda e, vcol=vcol, p=p, qpar=qpar, first=first, last=last: (
                            e.ldweights(Vb[:, vcol:vcol + 64], tile_position=(0, 0)),
                            e.matmul(c.ps[qpar][0:64, :], Vb[:, vcol:vcol + 64], expb[p][:, 0:512],
                                     start=first, stop=last, tile_position=(0, 0), skip_group_check=True))[-1],
                            wait=(((c.sS, exp0_at[kt]),) + (((c.sV, av_evac_at.get(gi - 2, 0)),)
                                                            if first else ())),
                            inc=c.sT)
                        P.t(lambda e, vcol=vcol, p=p, qpar=qpar, first=first, last=last: (
                            e.ldweights(Vb[:, vcol + 64:vcol + 128], tile_position=(0, 64)),
                            e.matmul(c.ps[qpar][64:128, :], Vb[:, vcol + 64:vcol + 128], expb[p][:, 512:1024],
                                     start=first, stop=last, tile_position=(0, 64), skip_group_check=True))[-1],
                            wait=((c.sS, exp_at[kt]),),
                            inc=c.sT)
                        av_at[kt] = c.sT.n
                        if kt % DQ == 0:
                            P.v(lambda e, p=p: e.tensor_copy(exps[:, :], expb[p][:, :]),
                                wait=((c.sS, exp_at[kt]), (c.sT, denq_at[kt // DQ - 1]),
                                      (c.sV, c.sV.n)), inc=c.sV)
                        else:
                            P.v(lambda e, p=p: e.tensor_tensor(exps[:, :], exps[:, :], expb[p][:, :],
                                                               ALU.add),
                                wait=((c.sS, exp_at[kt]), (c.sV, c.sV.n)), inc=c.sV)
                        add_at[kt] = c.sV.n
                        if kt % DQ == DQ - 1:
                            st, sp = (kt == DQ - 1), (kt == NKT - 1)
                            P.t(lambda e, st=st, sp=sp: (
                                e.ldweights(c.ones_bf[:, 0:1], tile_position=(0, 0)),
                                e.matmul(c.ps[2][0:1, :], c.ones_bf[:, 0:1], exps[:, 0:512],
                                         start=st, stop=sp, tile_position=(0, 0), skip_group_check=True),
                                e.matmul(c.ps[3][0:1, :], c.ones_bf[:, 0:1], exps[:, 512:1024],
                                         start=st, stop=sp, tile_position=(0, 0), skip_group_check=True))[-1],
                                wait=(((c.sV, add_at[kt]),) + (((c.sV, recip_at.get(gi - 1, 0)),)
                                                               if st else ())),
                                inc=c.sT)
                            denq_at[kt // DQ] = c.sT.n
                    if step < NKT:
                        kt = step
                        p = kt % 2
                        P.s(lambda e, p=p, b=b: e.activation(
                            out=expb[p][:, :], in_=c.psBig[:, p * 1024:(p + 1) * 1024], func=AF.Exp,
                            scale=0.125, bias=mt[b][:, :]),
                            wait=((c.sT, scores_at[kt]), (c.sT, av_at.get(kt - 2, 0)),
                                  (c.sV, add_at.get(kt - 2, 0))), inc=c.sS)
                        exp0_at[kt] = c.sS.n
                        exp_at[kt] = c.sS.n
                prev_exp_tail = [exp_at[NKT - 2], exp_at[NKT - 1]]
                prev_av_tail = [av_at[NKT - 2], av_at[NKT - 1]]
                prev_add_tail = [add_at[NKT - 2], add_at[NKT - 1]]
                prev_denq = denq_at[NKT // DQ - 1]
                scores_end[gi] = scores_at[NKT - 1]
                pending_tail[0] = make_tail(gi, qpar, gpos, b, prev_denq)
        pending_tail[0]()
        pending_tail[0] = None
        ab_stage_done = c.dY.n

        def cc2b(e):
            with allow_ici_all_to_all():
                return e.collective_compute(
                    "AllToAll", ALU.bypass, replica_groups=[list(range(NCORES))],
                    ins=[abin[1][:, :]], outs=[about[1][:, :]])
        P.g(cc2b, wait=((c.dY, ab_stage_done),), inc=c.sC, inck=1)
        for srcc in range(NCORES):
            P.g(lambda e, srcc=srcc: e.dma_start(
                out=af_a[srcc][:, :], in_=blk2d(about[0], srcc, 0, 128, SC)),
                wait=((c.sC, sc0 + 2),) if srcc == 0 else (), inc=c.dG)
        af_ld_part[0] = c.dG.n
        for srcc in range(NCORES):
            P.g(lambda e, srcc=srcc: e.dma_start(
                out=af_b[srcc][:, :], in_=blk2d(about[1], srcc, 0, 128, SC)),
                wait=((c.sC, sc0 + 3),) if srcc == 0 else (), inc=c.dG)
        af_ld_part[1] = c.dG.n

        P.gate("tensor", ((c.sV, c.sV.n), (c.sS, c.sS.n)))
        prev_res = hn_done
        x_st = 0
        for tcc in range(T // TC):
            sl = slice(tcc * TC, (tcc + 1) * TC)
            for pt in range(NPT):
                P.y(lambda e, pt=pt, sl=sl: e.dma_start(out=xc[pt][:, :], in_=xT[pt * 128:(pt + 1) * 128, sl]),
                    wait=((c.sV, prev_res), (c.dY, x_st)), inc=c.dY)
            x_ld = c.dY.n
            afh = af_a if tcc < 2 else af_b
            asl = slice(sl.start - (0 if tcc < 2 else SC), sl.stop - (0 if tcc < 2 else SC))
            af_ok = af_ld_part[0 if tcc < 2 else 1]
            for ot in range(NPT):
                rg = ot % 6
                for kt in range(NPT):
                    P.t(lambda e, ot=ot, kt=kt, rg=rg, afh=afh, asl=asl: (
                        e.ldweights(wv_sb[kt][:, ot * 128:(ot + 1) * 128]),
                        e.matmul(c.reg(rg), wv_sb[kt][:, ot * 128:(ot + 1) * 128], afh[kt][:, asl],
                                 start=(kt == 0), stop=(kt == NPT - 1)))[-1],
                        wait=(((c.dG, wo_ld), (c.dG, af_ok), (c.sV, ev_done.get(rg, 0))) if kt == 0 else ()),
                        inc=c.sT)
                P.v(lambda e, ot=ot, rg=rg: e.tensor_scalar_add(yc[ot][:, :], c.reg(rg), outb_sb[:, ot:ot + 1]),
                    wait=((c.sT, c.sT.n),), inc=c.sV)
                ev_done[rg] = c.sV.n
            P.gate_all(((c.sV, c.sV.n), (c.dY, x_ld)), engines=("vector", "scalar", "tensor"))
            resid_done = emit_ln(P, c, [yc[pt][:, :] for pt in range(NPT)], TC,
                                 resid_aps=[xc[pt][:, :] for pt in range(NPT)])
            for pt in range(NPT):
                P.y(lambda e, pt=pt, sl=sl: e.dma_start(out=xO[pt * 128:(pt + 1) * 128, sl], in_=xc[pt][:, :]),
                    wait=((c.sV, resid_done),), inc=c.dY)
            x_st = c.dY.n
            prev_res = emit_ln(P, c, [xc[pt][:, :] for pt in range(NPT)], TC,
                               out_aps=[hn[pt][:, sl] for pt in range(NPT)])
        P.wait_only("sync", ((c.dY, c.dY.n),))


def emit_l0ffn(P, c, nc, io):
    xT = io["xb"]
    w1 = io["w1"]
    b1 = io["b1"]
    w2 = io["w2"]
    b2 = io["b2"]
    xO = io["xb"]
    TC = 512
    NH = FF // 128

    with contextlib.ExitStack() as stack:
        ec = stack.enter_context
        xc = [ec(nc.sbuf_tensor(f"F_xc{pt}", [128, TC], F32)) for pt in range(NPT)]
        yc = [ec(nc.sbuf_tensor(f"F_yc{pt}", [128, TC], BF16)) for pt in range(NPT)]
        g_sb = [ec(nc.sbuf_tensor(f"F_g{kt}", [128, TC], BF16)) for kt in range(NH)]
        w1b = [ec(nc.sbuf_tensor(f"F_w1b{pt}", [128, FF], BF16)) for pt in range(NPT)]
        w2b = [ec(nc.sbuf_tensor(f"F_w2b{i}", [128, D], BF16)) for i in range(16)]
        b1t = ec(nc.sbuf_tensor("F_b1t", [128, 32], F32))
        b2t = ec(nc.sbuf_tensor("F_b2t", [128, 8], F32))

        for pt in range(NPT):
            P.g(lambda e, pt=pt: e.dma_start(out=w1b[pt][:, :], in_=w1[pt * 128:(pt + 1) * 128, :]), inc=c.dG)
        P.y(lambda e: e.dma_start(out=b1t[:, :], in_=b1[:, :]), inc=c.dY)
        P.y(lambda e: e.dma_start(out=b2t[:, :], in_=b2[:, :]), inc=c.dY)
        w1_loaded = c.dG.n
        const_loaded = c.dY.n
        P.gate("scalar", ((c.dY, const_loaded),))
        P.gate("vector", ((c.dY, const_loaded),))

        x_stored = [0] * (T // TC)
        prev_resid_done = 0
        prev_lnps = 0
        w2_cons = 0
        for tc in range(T // TC):
            sl = slice(tc * TC, (tc + 1) * TC)
            prev_store = x_stored[tc - 1] if tc > 0 else 0
            for pt in range(NPT):
                P.y(lambda e, pt=pt, sl=sl: e.dma_start(out=xc[pt][:, :], in_=xT[pt * 128:(pt + 1) * 128, sl]),
                    wait=((c.sV, prev_resid_done), (c.dY, prev_store)), inc=c.dY)
            x_ld = c.dY.n
            evac_done_at = [0] * 8
            for ot in range(NH):
                rg = ot % 6
                for kt in range(NPT):
                    P.t(lambda e, ot=ot, kt=kt, rg=rg, sl=sl: (
                        e.ldweights(w1b[kt][:, ot * 128:(ot + 1) * 128]),
                        e.matmul(c.reg(rg), w1b[kt][:, ot * 128:(ot + 1) * 128], c.hs[kt][:, sl],
                                 start=(kt == 0), stop=(kt == NPT - 1)))[-1],
                        wait=(((c.dG, w1_loaded), (c.sS, evac_done_at[rg]))
                              if kt == 0 else ()),
                        inc=c.sT)
                mm_done = c.sT.n
                P.s(lambda e, ot=ot, rg=rg: e.activation(out=g_sb[ot][:, :], in_=c.reg(rg), func=AF.Gelu,
                                                         bias=b1t[:, ot:ot + 1]),
                    wait=((c.sT, mm_done),), inc=c.sS)
                evac_done_at[rg] = c.sS.n
            g_done = c.sS.n
            yev = 0
            for half in range(2):
                for i in range(16):
                    P.g(lambda e, half=half, i=i: e.dma_start(
                        out=w2b[i][:, :], in_=w2[(half * 16 + i) * 128:(half * 16 + i + 1) * 128, :]),
                        wait=((c.sT, w2_cons),) if i == 0 else (), inc=c.dG)
                w2_ld = c.dG.n
                for ot in range(NPT):
                    for i in range(16):
                        kt = half * 16 + i
                        P.t(lambda e, ot=ot, i=i, kt=kt: (
                            e.ldweights(w2b[i][:, ot * 128:(ot + 1) * 128]),
                            e.matmul(c.reg(ot), w2b[i][:, ot * 128:(ot + 1) * 128], g_sb[kt][:, :],
                                     start=(kt == 0), stop=(kt == NH - 1)))[-1],
                            wait=(((c.dG, w2_ld), (c.sS, g_done)) +
                                  (((c.sV, prev_lnps),) if (kt == 0 and ot >= 6) else ())
                                  if i == 0 else ()),
                            inc=c.sT)
                w2_cons = c.sT.n
            mm2_done = c.sT.n
            for ot in range(NPT):
                P.v(lambda e, ot=ot: e.tensor_scalar_add(yc[ot][:, :], c.reg(ot), b2t[:, ot:ot + 1]),
                    wait=((c.sT, mm2_done),), inc=c.sV)
            y_done_v = c.sV.n
            P.gate_all(((c.sV, y_done_v), (c.dY, x_ld)), engines=("vector", "tensor", "scalar"))
            resid_done = emit_ln(P, c, [yc[pt][:, :] for pt in range(NPT)], TC,
                                 resid_aps=[xc[pt][:, :] for pt in range(NPT)])
            for pt in range(NPT):
                P.y(lambda e, pt=pt, sl=sl: e.dma_start(out=xO[pt * 128:(pt + 1) * 128, sl], in_=xc[pt][:, :]),
                    wait=((c.sV, resid_done),), inc=c.dY)
            x_stored[tc] = c.dY.n
            prev_resid_done = emit_ln(P, c, [xc[pt][:, :] for pt in range(NPT)], TC,
                                      out_aps=[c.hs[pt][:, sl] for pt in range(NPT)])
            prev_lnps = prev_resid_done
        P.wait_only("sync", ((c.dY, c.dY.n),))


def wslice3(wT, ot):
    return wT[:, ot * 128:(ot + 1) * 128].rearrange("(k p) q -> p k q", k=8, p=128)


def emit_l1attn(P, c, nc, io):
    xT = io["xb"]
    wq = io["wq"]
    wk = io["wk"]
    wv = io["wv"]
    wo = io["wo"]
    hseld = io["hsel"]
    hbcd = io["hbc"]
    xO = io["xb"]
    TC = 512
    HT = T // 2

    with contextlib.ExitStack() as stack:
        ec = stack.enter_context
        xc = [ec(nc.sbuf_tensor(f"C_xc{pt}", [128, TC], F32)) for pt in range(NPT)]
        hn = c.hs
        bcol = ec(nc.sbuf_tensor("C_bcol", [128, 16], F32))
        v_sb = [ec(nc.sbuf_tensor(f"C_v{pt}", [128, T], BF16)) for pt in range(NPT)]
        yc = [ec(nc.sbuf_tensor(f"C_yc{pt}", [128, TC], BF16)) for pt in range(NPT)]
        wb = [ec(nc.sbuf_tensor(f"C_wb{i}", [128, 1024], BF16)) for i in range(2)]
        hsel_sb = ec(nc.sbuf_tensor("C_hsel_sb", [128, 128], BF16))
        hbc_sb = ec(nc.sbuf_tensor("C_hbc_sb", [16, 1024], BF16))
        prod = [ec(nc.sbuf_tensor(f"C_prod{i}", [128, HT], BF16)) for i in range(2)]
        s_sb = [ec(nc.sbuf_tensor(f"C_s{i}", [16, HT], BF16)) for i in range(4)]
        z_sb = [ec(nc.sbuf_tensor(f"C_z{i}", [16, HT], F32)) for i in range(2)]
        w_sb = [ec(nc.sbuf_tensor(f"C_wl{i}", [16, HT], BF16)) for i in range(4)]
        l1t = [ec(nc.sbuf_tensor(f"C_l1t{i}", [128, HT], BF16)) for i in range(2)]
        qk_scope = contextlib.ExitStack()
        eq = qk_scope.enter_context
        q_sb = [eq(nc.sbuf_tensor(f"C_q{pt}", [128, T], BF16)) for pt in range(NPT)]
        k_sb = [eq(nc.sbuf_tensor(f"C_k{pt}", [128, T], BF16)) for pt in range(NPT)]

        P.g(lambda e: e.dma_start(out=hsel_sb[:, :], in_=hseld[:, :]), inc=c.dG)
        P.g(lambda e: e.dma_start(out=hbc_sb[:, :], in_=hbcd[:, :]), inc=c.dG)
        consts_g = c.dG.n
        P.gate("tensor", ((c.dG, consts_g),))

        evS = {}
        evV = {}
        w_cons = [0, 0]
        for (wd, dst, actf) in ((wq, q_sb, AF.Relu), (wk, k_sb, AF.Relu), (wv, v_sb, None)):
            for ot in range(NPT):
                par = ot % 2
                P.g(lambda e, wd=wd, ot=ot, par=par: e.dma_start(
                    out=wb[par][:, :].rearrange("p (k q) -> p k q", k=8, q=128),
                    in_=wslice3(wd, ot)),
                    wait=((c.sT, w_cons[par]),), inc=c.dG)
                w_ld = c.dG.n
                for ch in range(4):
                    rg = (ot * 4 + ch) % 8
                    for kt in range(NPT):
                        P.t(lambda e, ot=ot, ch=ch, kt=kt, rg=rg, par=par: (
                            e.ldweights(wb[par][:, kt * 128:(kt + 1) * 128]),
                            e.matmul(c.reg(rg), wb[par][:, kt * 128:(kt + 1) * 128],
                                     hn[kt][:, ch * 512:(ch + 1) * 512],
                                     start=(kt == 0), stop=(kt == NPT - 1)))[-1],
                            wait=(((c.dG, w_ld),
                                   (c.sS, evS.get(rg, 0)), (c.sV, evV.get(rg, 0))) if kt == 0 else ()),
                            inc=c.sT)
                    if actf is not None:
                        P.s(lambda e, dst=dst, ot=ot, ch=ch, rg=rg, actf=actf: e.activation(
                            out=dst[ot][:, ch * 512:(ch + 1) * 512], in_=c.reg(rg), func=actf),
                            wait=((c.sT, c.sT.n),), inc=c.sS)
                        evS[rg] = c.sS.n
                    else:
                        P.v(lambda e, dst=dst, ot=ot, ch=ch, rg=rg: e.tensor_copy(
                            dst[ot][:, ch * 512:(ch + 1) * 512], c.reg(rg)),
                            wait=((c.sT, c.sT.n),), inc=c.sV)
                        evV[rg] = c.sV.n
                w_cons[par] = c.sT.n
        proj_done_v = c.sV.n
        proj_done_s = c.sS.n

        P.gate_all(((c.sV, proj_done_v), (c.sS, proj_done_s)), engines=("vector", "tensor"))
        combos = [(0, 0), (0, 1), (1, 0), (1, 1)]
        for ci, (l, lp) in enumerate(combos):
            for pt in range(NPT):
                par = pt % 2
                P.v(lambda e, pt=pt, par=par, l=l, lp=lp: e.tensor_tensor(
                    prod[par][:, :], q_sb[pt][:, l * HT:(l + 1) * HT],
                    k_sb[pt][:, lp * HT:(lp + 1) * HT], ALU.mult),
                    wait=((c.sT, evS.get(("p", par), 0)), (c.sV, c.sV.n)), inc=c.sV)
                pd = c.sV.n
                for ch in range(2):
                    P.t(lambda e, pt=pt, par=par, ch=ch: (
                        e.ldweights(hsel_sb[:, pt * 16:pt * 16 + 16]),
                        e.matmul(c.psBig[0:16, ch * 512:(ch + 1) * 512],
                                 hsel_sb[:, pt * 16:pt * 16 + 16],
                                 prod[par][:, ch * 512:(ch + 1) * 512],
                                 start=(pt == 0), stop=(pt == NPT - 1), skip_group_check=True))[-1],
                        wait=((c.sV, pd),), inc=c.sT)
                evS[("p", par)] = c.sT.n
            P.v(lambda e, ci=ci: e.tensor_copy(s_sb[ci][:, :], c.psBig[0:16, 0:HT]),
                wait=((c.sT, c.sT.n),), inc=c.sV)
        s_done = c.sV.n
        qk_scope.close()
        wo_sb = [ec(nc.sbuf_tensor(f"C_wo{kt}", [128, 1024], BF16)) for kt in range(NPT)]
        for kt in range(NPT):
            P.g(lambda e, kt=kt: e.dma_start(out=wo_sb[kt][:, :], in_=wo[kt * 128:(kt + 1) * 128, :]),
                wait=((c.sV, s_done),) if kt == 0 else (), inc=c.dG)
        wo_res = c.dG.n

        P.v(lambda e: e.scalar_tensor_tensor(z_sb[0][:, :], s_sb[1][:, :], C01, s_sb[0][:, :],
                                             ALU.mult, ALU.add),
            wait=((c.sV, s_done),), inc=c.sV)
        P.v(lambda e: e.scalar_tensor_tensor(z_sb[1][:, :], s_sb[2][:, :], C01, s_sb[3][:, :],
                                             ALU.mult, ALU.add),
            wait=((c.sV, c.sV.n),), inc=c.sV)
        P.v(lambda e: e.tensor_scalar_max(z_sb[0][:, :], z_sb[0][:, :], EPS_L1),
            wait=((c.sV, c.sV.n),), inc=c.sV)
        P.v(lambda e: e.tensor_scalar_max(z_sb[1][:, :], z_sb[1][:, :], EPS_L1),
            wait=((c.sV, c.sV.n),), inc=c.sV)
        P.v(lambda e: e.reciprocal(z_sb[0][:, :], z_sb[0][:, :]), wait=((c.sV, c.sV.n),), inc=c.sV)
        P.v(lambda e: e.reciprocal(z_sb[1][:, :], z_sb[1][:, :]), wait=((c.sV, c.sV.n),), inc=c.sV)
        z_done = c.sV.n
        P.v(lambda e: e.tensor_tensor(w_sb[0][:, :], s_sb[0][:, :], z_sb[0][:, :], ALU.mult),
            wait=((c.sV, z_done),), inc=c.sV)
        P.v(lambda e: e.tensor_tensor(w_sb[1][:, :], s_sb[1][:, :], z_sb[0][:, :], ALU.mult),
            inc=c.sV)
        P.v(lambda e: e.tensor_tensor(w_sb[2][:, :], s_sb[2][:, :], z_sb[1][:, :], ALU.mult),
            inc=c.sV)
        P.v(lambda e: e.tensor_tensor(w_sb[3][:, :], s_sb[3][:, :], z_sb[1][:, :], ALU.mult),
            inc=c.sV)
        P.v(lambda e: e.tensor_scalar_mul(w_sb[1][:, :], w_sb[1][:, :], C01),
            wait=((c.sV, c.sV.n),), inc=c.sV)
        P.v(lambda e: e.tensor_scalar_mul(w_sb[2][:, :], w_sb[2][:, :], C01),
            wait=((c.sV, c.sV.n),), inc=c.sV)
        w_done = c.sV.n

        P.gate("tensor", ((c.sV, w_done),))
        wbmm = 0
        for pt in range(NPT):
            for l in range(2):
                for lp in range(2):
                    wi = l * 2 + lp
                    for ch in range(2):
                        P.t(lambda e, pt=pt, lp=lp, wi=wi, ch=ch: (
                            e.ldweights(hbc_sb[:, pt * 128:(pt + 1) * 128]),
                            e.matmul(c.psBig[:, lp * 1024 + ch * 512: lp * 1024 + (ch + 1) * 512],
                                     hbc_sb[:, pt * 128:(pt + 1) * 128],
                                     w_sb[wi][:, ch * 512:(ch + 1) * 512],
                                     start=True, stop=True, skip_group_check=True))[-1],
                            wait=((c.sV, wbmm),), inc=c.sT)
                wb_T = c.sT.n
                P.v(lambda e, pt=pt, l=l: e.tensor_tensor(
                    l1t[0][:, :], c.psBig[:, 0:1024], v_sb[pt][:, 0:HT], ALU.mult),
                    wait=((c.sT, wb_T), (c.sV, c.sV.n)), inc=c.sV)
                P.v(lambda e, pt=pt, l=l: e.tensor_tensor(
                    l1t[1][:, :], c.psBig[:, 1024:2048], v_sb[pt][:, HT:T], ALU.mult),
                    wait=((c.sV, c.sV.n),), inc=c.sV)
                P.v(lambda e, pt=pt, l=l: e.tensor_tensor(
                    hn[pt][:, l * HT:(l + 1) * HT], l1t[0][:, :], l1t[1][:, :], ALU.add),
                    wait=((c.sV, c.sV.n),), inc=c.sV)
                wbmm = c.sV.n
        attn_done = c.sV.n

        aab_in = io["aab_in"]
        aab_out = io["aab_out"]
        P.gate("tensor", ((c.sV, attn_done),))
        prev_res = 0
        x_st = 0
        for ci, tcc in enumerate((1, 3, 0, 2)):
            sl = slice(tcc * TC, (tcc + 1) * TC)
            for pt in range(NPT):
                P.y(lambda e, pt=pt, sl=sl: e.dma_start(out=xc[pt][:, :], in_=xT[pt * 128:(pt + 1) * 128, sl]),
                    wait=((c.sV, prev_res), (c.dY, x_st)), inc=c.dY)
            x_ld = c.dY.n
            for ot in range(NPT):
                rg = ot % 6
                for kt in range(NPT):
                    P.t(lambda e, ot=ot, kt=kt, rg=rg, sl=sl: (
                        e.ldweights(wo_sb[kt][:, ot * 128:(ot + 1) * 128]),
                        e.matmul(c.reg(rg), wo_sb[kt][:, ot * 128:(ot + 1) * 128], hn[kt][:, sl],
                                 start=(kt == 0), stop=(kt == NPT - 1)))[-1],
                        wait=(((c.dG, wo_res), (c.sV, evV.get(("o", rg), 0))) if kt == 0 else ()),
                        inc=c.sT)
                P.v(lambda e, ot=ot, rg=rg: e.tensor_copy(yc[ot][:, :], c.reg(rg)),
                    wait=((c.sT, c.sT.n),), inc=c.sV)
                evV[("o", rg)] = c.sV.n
            P.gate_all(((c.sV, c.sV.n), (c.dY, x_ld)), engines=("vector", "scalar", "tensor"))
            resid_done = emit_ln(P, c, [yc[pt][:, :] for pt in range(NPT)], TC,
                                 resid_aps=[xc[pt][:, :] for pt in range(NPT)])
            for pt in range(NPT):
                P.y(lambda e, pt=pt, sl=sl: e.dma_start(out=xO[pt * 128:(pt + 1) * 128, sl], in_=xc[pt][:, :]),
                    wait=((c.sV, resid_done),), inc=c.dY)
            x_st = c.dY.n
            prev_res = emit_ln(P, c, [xc[pt][:, :] for pt in range(NPT)], TC,
                               out_aps=[hn[pt][:, sl] for pt in range(NPT)])
            if ci == 1:
                for pt in range(NPT):
                    for b in range(B):
                        col = pt * B + b
                        P.v(lambda e, pt=pt, b=b, col=col: e.tensor_copy(
                            bcol[:, col:col + 1], hn[pt][:, b * SC + SC - 1:b * SC + SC]),
                            wait=((c.sV, prev_res),) if col == 0 else (), inc=c.sV)
                bcol_done = c.sV.n
                for d in range(NCORES):
                    P.y(lambda e, d=d: e.dma_start(out=blk2d(aab_in, d, 0, 128, 16), in_=bcol[:, :]),
                        wait=((c.sV, bcol_done),) if d == 0 else (), inc=c.dY)
                ab_staged = c.dY.n

                def cc3(e):
                    with allow_ici_all_to_all():
                        return e.collective_compute(
                            "AllToAll", ALU.bypass, replica_groups=[list(range(NCORES))],
                            ins=[aab_in[:, :]], outs=[aab_out[:, :]])
                P.g(cc3, wait=((c.dY, ab_staged),), inc=c.sC, inck=1)
        P.wait_only("sync", ((c.dY, c.dY.n),))


def emit_l1ffn(P, c, nc, io):
    xT = io["xb"]
    wr = io["wr"]
    wkk = io["wkk"]
    wvv = io["wvv"]
    mur = io["mur"]
    muk = io["muk"]
    bmaskd = io["bmask"]
    xO = io["xO"]
    aab_out = io["aab_out"]
    TC = 512
    cc3_at = c.sC.n

    with contextlib.ExitStack() as stack:
        ec = stack.enter_context
        xc = [ec(nc.sbuf_tensor(f"R_xc{pt}", [128, TC], F32)) for pt in range(NPT)]
        hn = c.hs
        hp = [ec(nc.sbuf_tensor(f"R_hp{pt}", [128, T], BF16)) for pt in range(NPT)]
        xr = [ec(nc.sbuf_tensor(f"R_xr{pt}", [128, T], BF16)) for pt in range(NPT)]
        r_sb = [ec(nc.sbuf_tensor(f"R_r{pt}", [128, T], BF16)) for pt in range(NPT)]
        yc = [ec(nc.sbuf_tensor(f"R_yc{pt}", [128, TC], BF16)) for pt in range(NPT)]
        wb = [ec(nc.sbuf_tensor(f"R_wb{i}", [128, 1024], BF16)) for i in range(2)]
        murt = ec(nc.sbuf_tensor("R_murt", [128, 8], F32))
        mukt = ec(nc.sbuf_tensor("R_mukt", [128, 8], F32))
        dtb = [ec(nc.sbuf_tensor(f"R_dtb{i}", [128, T], BF16)) for i in range(1)]
        wvv_sb = [ec(nc.sbuf_tensor(f"R_wvv{kt}", [128, 1024], BF16)) for kt in range(NPT)]
        bndall = ec(nc.sbuf_tensor("R_bndall", [128, 128], F32))
        hpb = ec(nc.sbuf_tensor("R_hpb", [128, 16], F32))
        bmask_sb = ec(nc.sbuf_tensor("R_bmask_sb", [128, 128], F32))

        P.y(lambda e: e.dma_start(out=murt[:, :], in_=mur[:, :]), inc=c.dY)
        P.y(lambda e: e.dma_start(out=mukt[:, :], in_=muk[:, :]), inc=c.dY)
        P.y(lambda e: e.dma_start(out=bmask_sb[:, :], in_=bmaskd[:, :]), inc=c.dY)
        consts_y = c.dY.n

        for pt in range(NPT):
            for b in range(B):
                P.y(lambda e, pt=pt, b=b: e.dma_start(
                    out=hp[pt][:, b * SC + 1:(b + 1) * SC], in_=hn[pt][:, b * SC:(b + 1) * SC - 1]),
                    inc=c.dY)
        hp_shift = c.dY.n
        for s in range(NCORES):
            P.y(lambda e, s=s: e.dma_start(out=bndall[:, s * 16:(s + 1) * 16],
                                           in_=blk2d(aab_out, s, 0, 128, 16)),
                wait=((c.sC, cc3_at),) if s == 0 else (), inc=c.dY)
        bnd_ld = c.dY.n
        P.v(lambda e: e.tensor_tensor(bndall[:, :], bndall[:, :], bmask_sb[:, :], ALU.mult),
            wait=((c.dY, bnd_ld), (c.dY, consts_y), (c.sV, c.sV.n)), inc=c.sV)
        P.v(lambda e: e.tensor_copy(hpb[:, :], bndall[:, 0:16]), wait=((c.sV, c.sV.n),), inc=c.sV)
        for s in range(1, NCORES):
            P.v(lambda e, s=s: e.tensor_tensor(hpb[:, :], hpb[:, :], bndall[:, s * 16:(s + 1) * 16],
                                               ALU.add),
                wait=((c.sV, c.sV.n),), inc=c.sV)
        hpb_done = c.sV.n
        for pt in range(NPT):
            for b in range(B):
                col = pt * B + b
                P.v(lambda e, pt=pt, b=b, col=col: e.tensor_copy(
                    hp[pt][:, b * SC:b * SC + 1], hpb[:, col:col + 1]),
                    wait=((c.sV, hpb_done), (c.dY, hp_shift)) if col == 0 else (), inc=c.sV)
        hp_done = c.sV.n

        for pt in range(NPT):
            P.v(lambda e, pt=pt: e.tensor_sub(dtb[0][:, :], hn[pt][:, :], hp[pt][:, :]),
                wait=((c.sV, c.sV.n), (c.sV, hp_done), (c.dY, hp_shift)), inc=c.sV)
            P.v(lambda e, pt=pt: e.scalar_tensor_tensor(
                xr[pt][:, :], dtb[0][:, :], murt[:, pt:pt + 1], hp[pt][:, :], ALU.mult, ALU.add),
                wait=((c.sV, c.sV.n), (c.dY, consts_y)), inc=c.sV)
        xr_done = c.sV.n

        ev_done = {}
        w_cons = [0, 0]

        def proj(P, wd, rhs_tiles, rhs_ready, evac, name):
            for ot in range(NPT):
                par = ot % 2
                P.g(lambda e, wd=wd, ot=ot, par=par: e.dma_start(
                    out=wb[par][:, :].rearrange("p (k q) -> p k q", k=8, q=128),
                    in_=wslice3(wd, ot)),
                    wait=((c.sT, w_cons[par]),), inc=c.dG)
                w_ld = c.dG.n
                for ch in range(4):
                    rg = (ot * 4 + ch) % 8
                    for kt in range(NPT):
                        P.t(lambda e, ot=ot, ch=ch, kt=kt, rg=rg, par=par: (
                            e.ldweights(wb[par][:, kt * 128:(kt + 1) * 128]),
                            e.matmul(c.reg(rg), wb[par][:, kt * 128:(kt + 1) * 128],
                                     rhs_tiles[kt][:, ch * 512:(ch + 1) * 512],
                                     start=(kt == 0), stop=(kt == NPT - 1)))[-1],
                            wait=(((c.dG, w_ld), rhs_ready, (ev_done.get(rg, (c.sV, 0)))) if kt == 0 else ()),
                            inc=c.sT)
                    evac(ot, ch, rg)
                w_cons[par] = c.sT.n

        def evac_sig(ot, ch, rg):
            P.s(lambda e, ot=ot, ch=ch, rg=rg: e.activation(
                out=r_sb[ot][:, ch * 512:(ch + 1) * 512], in_=c.reg(rg), func=AF.Sigmoid),
                wait=((c.sT, c.sT.n),), inc=c.sS)
            ev_done[rg] = (c.sS, c.sS.n)
        proj(P, wr, xr, (c.sV, xr_done), evac_sig, "r")
        r_done_s = c.sS.n

        P.gate("vector", ((c.sT, c.sT.n),))
        for pt in range(NPT):
            P.v(lambda e, pt=pt: e.tensor_sub(dtb[0][:, :], hn[pt][:, :], hp[pt][:, :]),
                wait=((c.sV, c.sV.n),), inc=c.sV)
            P.v(lambda e, pt=pt: e.scalar_tensor_tensor(
                xr[pt][:, :], dtb[0][:, :], mukt[:, pt:pt + 1], hp[pt][:, :], ALU.mult, ALU.add),
                wait=((c.sV, c.sV.n),), inc=c.sV)
        xk_done = c.sV.n

        def evac_relu2(ot, ch, rg):
            P.s(lambda e, ot=ot, ch=ch, rg=rg: e.activation(
                out=hp[ot][:, ch * 512:(ch + 1) * 512], in_=c.reg(rg), func=AF.Relu),
                wait=((c.sT, c.sT.n), (c.sV, xk_done)), inc=c.sS)
            P.s(lambda e, ot=ot, ch=ch: e.activation(
                out=hp[ot][:, ch * 512:(ch + 1) * 512], in_=hp[ot][:, ch * 512:(ch + 1) * 512],
                func=AF.Square),
                wait=((c.sS, c.sS.n),), inc=c.sS)
            ev_done[rg] = (c.sS, c.sS.n)
        P.gate("tensor", ((c.sV, xk_done),))
        proj(P, wkk, xr, (c.sV, xk_done), evac_relu2, "ka")

        for kt in range(NPT):
            P.g(lambda e, kt=kt: e.dma_start(out=wvv_sb[kt][:, :], in_=wvv[kt * 128:(kt + 1) * 128, :]),
                inc=c.dG)
        wvv_res = c.dG.n
        P.gate("tensor", ((c.sS, c.sS.n),))
        prev_res = 0
        x_st = 0
        for tcc in range(T // TC):
            sl = slice(tcc * TC, (tcc + 1) * TC)
            for pt in range(NPT):
                P.y(lambda e, pt=pt, sl=sl: e.dma_start(out=xc[pt][:, :], in_=xT[pt * 128:(pt + 1) * 128, sl]),
                    wait=((c.sV, prev_res), (c.dY, x_st)), inc=c.dY)
            x_ld = c.dY.n
            for ot in range(NPT):
                rg = ot % 6
                for kt in range(NPT):
                    P.t(lambda e, ot=ot, kt=kt, rg=rg, sl=sl: (
                        e.ldweights(wvv_sb[kt][:, ot * 128:(ot + 1) * 128]),
                        e.matmul(c.reg(rg), wvv_sb[kt][:, ot * 128:(ot + 1) * 128], hp[kt][:, sl],
                                 start=(kt == 0), stop=(kt == NPT - 1)))[-1],
                        wait=(((c.dG, wvv_res), (ev_done.get(("y", rg), (c.sV, 0)))) if kt == 0 else ()),
                        inc=c.sT)
                P.v(lambda e, ot=ot, rg=rg, sl=sl: e.tensor_tensor(
                    yc[ot][:, :], c.reg(rg), r_sb[ot][:, sl], ALU.mult),
                    wait=((c.sT, c.sT.n), (c.sS, r_done_s)), inc=c.sV)
                ev_done[("y", rg)] = (c.sV, c.sV.n)
            P.gate_all(((c.sV, c.sV.n), (c.dY, x_ld)), engines=("vector", "scalar", "tensor"))
            prev_res = emit_ln(P, c, [yc[pt][:, :] for pt in range(NPT)], TC,
                               resid_aps=[xc[pt][:, :] for pt in range(NPT)])
            for pt in range(NPT):
                P.y(lambda e, pt=pt, sl=sl: e.dma_start(out=xO[pt * 128:(pt + 1) * 128, sl], in_=xc[pt][:, :]),
                    wait=((c.sV, prev_res),), inc=c.dY)
            x_st = c.dY.n
        P.wait_only("sync", ((c.dY, c.dY.n),))


def build_full():
    nc = bass.Bass("TRN2", target_bir_lowering=False)
    io = {}

    def inp(name, shape):
        io[name] = nc.dram_tensor(name, shape, F32, kind="ExternalInput").ap()
    inp("xT", [D, T])
    inp("qkvw4", [16, 8, 128, 128])
    inp("wvT", [D, D])
    inp("qk_b", [128, 16])
    inp("bvt", [128, 1])
    inp("owT", [D, D])
    inp("out_b", [128, 8])
    inp("CCS", [128, T])
    inp("SSG", [128, T])
    inp("w1", [D, FF])
    inp("b1", [128, 32])
    inp("w2", [FF, D])
    inp("b2", [128, 8])
    inp("wq", [D, D])
    inp("wk", [D, D])
    inp("wv", [D, D])
    inp("wo", [D, D])
    inp("hsel", [128, 128])
    inp("hbc", [16, 1024])
    inp("wr", [D, D])
    inp("wkk", [D, D])
    inp("wvv", [D, D])
    inp("mur", [128, 8])
    inp("muk", [128, 8])
    inp("bmask", [128, 128])
    inp("ones", [128, 128])
    io["xb"] = nc.dram_tensor("xb", [D, T], F32).ap()
    io["aab_in"] = nc.dram_tensor("aab_in", [NCORES, 128 * 16], F32).ap()
    io["aab_out"] = nc.dram_tensor("aab_out", [NCORES, 128 * 16], F32).ap()
    io["xO"] = nc.dram_tensor("xO", [D, T], F32, kind="ExternalOutput").ap()

    with contextlib.ExitStack() as stack:
        P = Prog(nc, stack)
        c = Ctx(P, 512)
        c.load_const(P, io["ones"])
        emit_l0attn(P, c, nc, io)
        barrier(P, c)
        emit_l0ffn(P, c, nc, io)
        barrier(P, c)
        emit_l1attn(P, c, nc, io)
        barrier(P, c)
        emit_l1ffn(P, c, nc, io)
        P.run_all()
    return nc


def kernel(**inputs):
    h = _host_prep(inputs)
    nc = build_full()
    in_maps = []
    for cid in range(NCORES):
        bmask = np.zeros((128, 128), np.float32)
        if cid > 0:
            bmask[:, (cid - 1) * 16:cid * 16] = 1.0
        in_maps.append({
            "xT": h["xT"][cid], "qkvw4": h["qkv_wt4"], "wvT": h["wvT"], "qk_b": h["qk_b"],
            "bvt": h["bvt"][cid], "owT": h["out_wT"], "out_b": h["out_b"],
            "CCS": h["CCS"][cid], "SSG": h["SSG"][cid], "ones": h["ones"],
            "w1": h["w1"], "b1": h["b1"], "w2": h["w2"], "b2": h["b2"],
            "wq": h["WqT"], "wk": h["WkT"], "wv": h["WvT"], "wo": h["WoT"],
            "hsel": h["hsel"], "hbc": h["hbc"],
            "wr": h["W_rT"], "wkk": h["W_kT"], "wvv": h["W_vT"],
            "mur": h["mu_r"], "muk": h["mu_k"], "bmask": bmask,
        })
    res = run_bass_kernel_spmd(nc, in_maps, core_ids=list(range(NCORES)))
    out = np.empty((B, S, D), np.float32)
    for cid in range(NCORES):
        xo = np.asarray(res.results[cid]["xO"])
        out[:, cid * SC:(cid + 1) * SC, :] = xo.T.reshape(B, SC, D)
    return out
